# version 13
# baseline (speedup 1.0000x reference)
import math
import numpy as np
import concourse.bass as bass
import concourse.mybir as mybir
from concourse.bass_utils import run_bass_kernel_spmd

F32 = mybir.dt.float32
BF16 = mybir.dt.bfloat16
FP8 = mybir.dt.float8e4
AF = mybir.ActivationFunctionType
ALU = mybir.AluOpType

B, S, D = 16, 2048, 1024
NCORES = 8
SEQ_PER_CORE = B // NCORES
T = SEQ_PER_CORE * S
NC_F = D // 128
H, HD = 16, 64
NPAIR = H // 2
TILE = 512
NTILE = T // TILE
CH = 128

ENGS = ("tensor", "scalar", "vector", "gpsimd", "sync")
DEV_TAPS = {}


class Sched:

    def __init__(self):
        self.ops = {e: [] for e in ENGS}
        self.cnt = {}
        self.last_wait = {e: {} for e in ENGS}

    def sem(self, name):
        self.cnt.setdefault(name, 0)

    def emit(self, eng, fn=None, wait=(), inc=None):
        w = []
        for ev in wait:
            if ev is None:
                continue
            s, v = ev
            if v is None:
                v = self.cnt.get(s, 0)
            if v <= 0:
                continue
            assert v <= self.cnt.get(s, 0), f"wait on future count {s}={v} (cur {self.cnt.get(s,0)})"
            if self.last_wait[eng].get(s, -1) >= v:
                continue
            self.last_wait[eng][s] = v
            w.append((s, v))
        ev_out = None
        if inc is not None:
            s, a = inc
            self.cnt[s] = self.cnt.get(s, 0) + a
            ev_out = (s, self.cnt[s])
        self.ops[eng].append((fn, w, inc))
        return ev_out

    def t(self, fn, wait=(), inc=True):
        return self.emit("tensor", fn, wait, ("sT", 1) if inc else None)

    def s(self, fn, wait=(), inc=True):
        w = list(wait) + [("sS", self.cnt.get("sS", 0))]
        return self.emit("scalar", fn, w, ("sS", 1) if inc else None)

    def v(self, fn, wait=(), inc=True):
        w = list(wait) + [("sV", self.cnt.get("sV", 0))]
        return self.emit("vector", fn, w, ("sV", 1) if inc else None)

    def g(self, fn, wait=(), inc=True):
        w = list(wait) + [("sG", self.cnt.get("sG", 0))]
        return self.emit("gpsimd", fn, w, ("sG", 1) if inc else None)

    def dma(self, eng, fn, sem, wait=()):
        return self.emit(eng, fn, wait, (sem, 16))


class Bank:
    __slots__ = ("ap", "name", "ready", "free")

    def __init__(self, ap, name):
        self.ap = ap
        self.name = name
        self.ready = None
        self.free = None


class Alloc:

    def __init__(self, total_elems, name):
        self.total = total_elems
        self.name = name
        self.cursor = 0
        self.marks = {}

    def take(self, n, align=8):
        c = (self.cursor + align - 1) // align * align
        assert c + n <= self.total, f"{self.name} arena overflow: {c}+{n} > {self.total}"
        self.cursor = c + n
        return c

    def mark(self, key):
        self.marks[key] = self.cursor

    def reset(self, key):
        self.cursor = self.marks[key]


def build_nc(dev_taps=(), upto="out"):
    nc = bass.Bass("TRN2", target_bir_lowering=False)
    sc = Sched()
    for s_ in ("sT", "sS", "sV", "sG", "sD0", "sD1", "sD2", "sD3", "sD4", "sX0", "sX1"):
        sc.sem(s_)

    x_in = nc.dram_tensor("x_in", [T, D], F32, kind="ExternalInput").ap()
    consts = nc.dram_tensor("consts", [128, CONST_W], F32, kind="ExternalInput").ap()
    w_in = {}
    for name, shp in WEIGHT_SHAPES.items():
        w_in[name] = nc.dram_tensor(name, shp, F32, kind="ExternalInput").ap()
    out = nc.dram_tensor("out", [T, D], F32, kind="ExternalOutput").ap()
    taps = {}
    for tname, tshape in DEV_TAPS.items():
        taps[tname] = nc.dram_tensor("tap_" + tname, list(tshape), F32, kind="ExternalOutput").ap()

    import contextlib
    with contextlib.ExitStack() as st_:
        e = st_.enter_context
        x_fm = e(nc.sbuf_tensor("x_fm", [128, NC_F * T], BF16))
        ctab = e(nc.sbuf_tensor("ctab", [128, 4096], BF16))
        cbf = e(nc.sbuf_tensor("cbf", [128, 1024], BF16))
        rows = e(nc.sbuf_tensor("rows", [128, 2304], F32))
        cols = e(nc.sbuf_tensor("cols", [128, 64], F32))
        stg = e(nc.sbuf_tensor("stg", [128, 2 * 1024], F32))
        bfa = e(nc.sbuf_tensor("bfa", [128, BFA_N], BF16))
        f8a = e(nc.sbuf_tensor("f8a", [128, F8_N], FP8))
        f32a = e(nc.sbuf_tensor("f32a", [128, F32A_N], F32))
        banks = [Bank(e(nc.psum_tensor(f"pb{i}", [128, 512], F32)), f"pb{i}") for i in range(7)]
        tbank = Bank(e(nc.psum_tensor("tb", [128, 1024], BF16)), "tb")
        sems = {}
        for s_ in ("sT", "sS", "sV", "sG", "sD0", "sD1", "sD2", "sD3", "sD4", "sX0", "sX1"):
            sems[s_] = e(nc.semaphore(s_))
        block = e(nc.Block())

        build_program(nc, sc, locals(), upto=upto)
        build_nc.last_sc = sc

        for engname in ENGS:
            section = getattr(block, engname)

            def _body(eng, _ops=sc.ops[engname]):
                for fn, waits, inc in _ops:
                    for s_, v in waits:
                        eng.wait_ge(sems[s_], v)
                    instr = fn(eng) if fn is not None else None
                    if inc is not None:
                        if instr is not None:
                            instr.then_inc(sems[inc[0]], inc[1])
                        else:
                            eng.nop().then_inc(sems[inc[0]], inc[1])

            section(_body)
    return nc


CONST_W = 4800

WEIGHT_SHAPES = {
    "wq": [D, D], "wk": [D, D], "wv": [D, D], "wo": [D, D],
    "wg1": [D, HD], "wg2": [HD, D],
    "lnw2d": [128, NC_F], "lnb2d": [128, NC_F],
    "f0wT": [D, D], "f0b2d": [128, NC_F],
    "qkvT": [D, 3 * D], "oT": [D, D],
    "f1w": [D, D], "f1a1": [D, D], "f1a2": [D, D], "f1b2d": [128, NC_F],
}

BFA_N = 49408
F8_N = 12 * 1024
F32A_N = 2 * 1024


def build_program(nc, sc, env, upto="out"):
    x_in, out = env["x_in"], env["out"]
    x_fm, ctab, cbf = env["x_fm"], env["ctab"], env["cbf"]
    stg, banks, rows = env["stg"], env["banks"], env["rows"]
    f32a, consts = env["f32a"], env["consts"]
    bfa, f8a, cols, w_in = env["bfa"], env["f8a"], env["cols"], env["w_in"]
    tbank = env["tbank"]

    S_ps = banks[6]
    avail = list(banks[0:6])
    held = []

    def take():
        b = avail.pop(0)
        held.append(b)
        return b

    def rel(b):
        held.remove(b)
        avail.append(b)

    OFF_WQ, OFF_WK, OFF_WV, OFF_WOW = 0, 8192, 16384, 24576
    OFF_U2, OFF_F0B2, OFF_WG1, OFF_WG2 = 32768, 33792, 34816, 35328
    OFF_QT, OFF_KT, OFF_KTM, OFF_G1 = 36352, 36864, 37376, 37888
    OFF_ATTD, OFF_SSB, OFF_VT, OFF_AO = 38400, 38912, 39424, 43520
    OFF_RTMP, OFF_SQ, OFF_ST2 = 47616, 48128, 48640
    SEG_A, SEG_B, SEG_C, SEG_D = 0, 512, 1024, 1536

    def bsl(off, n, p0=0, p1=128):
        return bfa[p0:p1, off:off + n]

    taps = env["taps"]
    tap_stage = env["f32a"]

    def tap(tag, ap, ev):
        if tag not in taps:
            return
        dst = taps[tag]
        n = dst.shape[1]
        p = dst.shape[0]
        cv = sc.v(lambda e, ap=ap, p=p, n=n: e.tensor_copy(tap_stage[0:p, 0:n], ap),
                  wait=[ev, tap.last])
        tap.last = sc.dma("gpsimd", lambda e, dst=dst, p=p, n=n: e.dma_start(
            out=dst, in_=tap_stage[0:p, 0:n]), "sD2", wait=[cv])
    tap.last = None

    ev_tab = None
    for piece in range(4):
        sl = stg[:, (piece % 2) * 1024:(piece % 2) * 1024 + 1024]
        ld = sc.dma("sync", lambda e, p=piece, sl=sl: e.dma_start(out=sl, in_=consts[:, p * 1024:(p + 1) * 1024]),
                    "sD0" if piece % 2 == 0 else "sD1", wait=[ev_tab] if piece >= 2 else [])
        ev_tab = sc.v(lambda e, p=piece, sl=sl: e.tensor_copy(ctab[:, p * 1024:(p + 1) * 1024], sl), wait=[ld])
    sl = stg[:, 0:704]
    ld_m = sc.dma("sync", lambda e, sl=sl: e.dma_start(out=sl, in_=consts[:, 4096:4800]), "sD0", wait=[ev_tab])
    ev_cbf = sc.v(lambda e, sl=sl: e.tensor_copy(cbf[:, 0:704], sl), wait=[ld_m])
    ev_idf = sc.v(lambda e, sl=sl: e.tensor_copy(rows[:, 2048:2176], sl[:, 0:128]), wait=[ld_m])
    ld_consts = ev_idf
    idt_bf = cbf[:, 0:128]
    ident_f32 = rows[:, 2048:2176]
    triu2 = cbf[:, 128:384]
    p2m = cbf[:, 384:512]
    onescol = cbf[:, 512:513]
    bc2 = cbf[0:2, 520:648]
    bc2_32 = cbf[32:34, 520:648]
    mz1 = sc.v(lambda e: e.memset(rows[0:64, 0:2048], 0.0))
    mz2 = sc.v(lambda e: e.memset(rows[0:64, 2176:2304], 0.0), wait=[mz1])
    mz3 = sc.v(lambda e: e.memset(rows[0:1, 2176:2304], 1.0), wait=[mz2])
    mz4 = sc.v(lambda e: e.memset(bfa[0:64, OFF_ST2:OFF_ST2 + 512], 0.0), wait=[mz3])
    ev_ones = sc.v(lambda e: e.memset(bfa[0:64, OFF_U2:OFF_U2 + 1024], 0.0), wait=[mz4])
    bc2f = rows[0:2, 2176:2304]
    st2bf = bfa[0:2, OFF_ST2:OFF_ST2 + 512]

    n_tc = T // 128
    stg_ev = [ld_consts, ld_consts]
    x_ready = []
    for c in range(n_tc):
        slot = c % 2
        stg_ap = stg[:, slot * 1024:(slot + 1) * 1024]
        ld = sc.dma("gpsimd", lambda e, c=c, a=stg_ap: e.dma_start(out=a, in_=x_in[c * 128:(c + 1) * 128, :]),
                    "sX0" if slot == 0 else "sX1", wait=[stg_ev[slot]])
        ev = None
        for half in range(2):
            bk = take()
            tp_ev = None
            for j in range(4):
                fc = half * 4 + j
                tp_ev = sc.t(lambda e, a=stg_ap, fc=fc, bk=bk, j=j:
                             e.transpose(bk.ap[:, j * 128:(j + 1) * 128],
                                         a[:, fc * 128:(fc + 1) * 128], ident_f32),
                             wait=[ld, ld_consts, bk.free] if j == 0 else [])
            bk.ready = tp_ev

            def ev_fn(e, bk=bk, c=c, half=half):
                dst = x_fm[:, :].rearrange("p (f t) -> p f t", f=NC_F)
                src = bk.ap[:, 0:512].rearrange("p (f t) -> p f t", f=4)
                return e.copy(dst[:, half * 4:(half + 1) * 4, c * 128:(c + 1) * 128], src)
            ev = sc.s(ev_fn, wait=[bk.ready])
            bk.free = ev
            rel(bk)
        stg_ev[slot] = ev
        x_ready.append(ev)

    wev = {}

    def staged(dram_sl, width, cast_fn, prows=128):
        slot = staged.rot[0] % 2
        staged.rot[0] += 1
        sl = stg[0:prows, slot * 1024: slot * 1024 + width]
        ld = sc.dma("sync", lambda e, sl=sl, d=dram_sl: e.dma_start(out=sl, in_=d),
                    "sD0" if slot == 0 else "sD1", wait=[stg_ev[slot]])
        ev = cast_fn(sl, ld)
        stg_ev[slot] = ev
        return ev
    staged.rot = [0]

    def load_plain_bf(name, off, width=1024):
        for ck in range(8):
            wev[name] = staged(
                w_in[name][ck * 128:(ck + 1) * 128, 0:width], width,
                lambda sl, ld, ck=ck: sc.v(
                    lambda e, sl=sl: e.tensor_copy(bsl(off + ck * width, width), sl), wait=[ld]))

    def load_weights_l0():
        sc.dma("sync", lambda e: e.dma_start(out=cols[:, 0:8], in_=w_in["lnw2d"]), "sD4")
        sc.dma("sync", lambda e: e.dma_start(out=cols[:, 8:16], in_=w_in["lnb2d"]), "sD4")
        sc.dma("sync", lambda e: e.dma_start(out=cols[:, 16:24], in_=w_in["f1b2d"]), "sD4")
        load_plain_bf("wq", OFF_WQ)
        load_plain_bf("wk", OFF_WK)
        load_plain_bf("wv", OFF_WV)
        for ck in range(8):
            wev["wg1"] = staged(
                w_in["wg1"][ck * 128:(ck + 1) * 128, :], 64,
                lambda sl, ld, ck=ck: sc.v(
                    lambda e, sl=sl: e.tensor_copy(bsl(OFF_WG1 + ck * 64, 64), sl), wait=[ld]))
        wev["wg2"] = staged(
            w_in["wg2"][:, :], 1024,
            lambda sl, ld: sc.v(lambda e, sl=sl: e.tensor_copy(bfa[0:64, OFF_WG2:OFF_WG2 + 1024], sl),
                                wait=[ld]), prows=64)
        for ck in range(8):
            wev["w8"] = staged(
                w_in["f0wT"][ck * 128:(ck + 1) * 128, :], 1024,
                lambda sl, ld, ck=ck: sc.v(
                    lambda e, sl=sl: e.tensor_copy(f8a[:, ck * 1024:(ck + 1) * 1024], sl), wait=[ld]))
        sc.dma("sync", lambda e: e.dma_start(out=cols[:, 40:48], in_=w_in["f0b2d"]), "sD4")
        colev = ("sD4", sc.cnt["sD4"])
        wev["lnw"] = colev
        wev["lnb"] = colev
        wev["f1b"] = colev
        wev["f0b2"] = colev
        bank_u = take()
        bank_c = take()
        for ck in range(8):
            def cast_wo(sl, ld, ck=ck):
                for half in range(2):
                    sc.t(lambda e, sl=sl, h=half, ck=ck: e.matmul(
                        bank_u.ap[(0 if h == 0 else 32):(1 if h == 0 else 33), 0:512],
                        cols[:, ck:ck + 1], sl[:, h * 512:(h + 1) * 512],
                        start=(ck == 0), stop=(ck == 7)),
                        wait=(([wev["lnw"], wev["lnb"], bank_u.free, bank_c.free] if ck == 0 else [])
                              + ([ld] if half == 0 else [])))
                for dc in range(8):
                    sc.t(lambda e, sl=sl, dc=dc, ck=ck: e.matmul(
                        bank_c.ap[:, dc:dc + 1], sl[:, dc * 128:(dc + 1) * 128],
                        cols[:, 8 + ck:9 + ck], start=(ck == 0 and dc == 0),
                        stop=(ck == 7 and dc == 7)))
                return sc.v(lambda e, sl=sl, ck=ck: e.tensor_scalar(
                    bsl(OFF_WOW + ck * 1024, 1024), sl, cols[:, ck:ck + 1], None, op0=ALU.mult),
                    wait=[wev["lnw"], (("sT", sc.cnt["sT"]))])
            wev["wo"] = staged(w_in["wo"][ck * 128:(ck + 1) * 128, :], 1024, cast_wo)
        bank_u.ready = (("sT", sc.cnt["sT"]))
        sc.v(lambda e: e.tensor_copy(bsl(OFF_U2, 512, 0, 1), bank_u.ap[0:1, 0:512]),
             wait=[bank_u.ready, ev_ones])
        sc.v(lambda e: e.tensor_copy(bsl(OFF_U2 + 512, 512, 0, 1), bank_u.ap[32:33, 0:512]))
        e4 = sc.v(lambda e: e.tensor_copy(cols[:, 32:40], bank_c.ap[:, 0:8]))
        bank_u.free = e4
        bank_c.free = e4
        rel(bank_u)
        rel(bank_c)
        wev["u2"] = e4


    sq_cell = [None]
    mur_bf_cell = [None]
    rows_guard = [None]

    def ln_stats(bank_s):
        sc.v(lambda e, bk=bank_s: e.tensor_scalar(
            rows[0:1, SEG_B:SEG_B + 512], bk.ap[0:1, 0:512], 1.0 / 1024.0, None,
            op0=ALU.mult), wait=[bank_s.ready, ev_ones, rows_guard[0]])
        sc.v(lambda e, bk=bank_s: e.tensor_scalar(
            rows[0:1, SEG_C:SEG_C + 512], bk.ap[32:33, 0:512], 1.0 / 1024.0, None,
            op0=ALU.mult))
        sc.v(lambda e: e.tensor_tensor(rows[0:1, SEG_D:SEG_D + 512],
                                       rows[0:1, SEG_B:SEG_B + 512],
                                       rows[0:1, SEG_B:SEG_B + 512], op=ALU.mult))
        vv = sc.v(lambda e: e.scalar_tensor_tensor(rows[0:1, SEG_A:SEG_A + 512],
                                                   rows[0:1, SEG_C:SEG_C + 512], 1e-5,
                                                   rows[0:1, SEG_D:SEG_D + 512],
                                                   op0=ALU.add, op1=ALU.subtract))
        ev5 = sc.s(lambda e: e.activation(rows[0:1, SEG_D:SEG_D + 512],
                                          rows[0:1, SEG_A:SEG_A + 512], AF.Sqrt),
                   wait=[vv])
        r_ev = sc.v(lambda e: e.reciprocal(rows[0:1, SEG_A:SEG_A + 512],
                                           rows[0:1, SEG_D:SEG_D + 512]), wait=[ev5])
        mur_ev = sc.v(lambda e: e.scalar_tensor_tensor(
            rows[0:1, SEG_C:SEG_C + 512], rows[0:1, SEG_B:SEG_B + 512], -1.0,
            rows[0:1, SEG_A:SEG_A + 512], op0=ALU.mult, op1=ALU.mult))
        mur_bf_cell[0] = sc.v(lambda e: e.tensor_copy(
            bfa[0:1, OFF_ST2:OFF_ST2 + 512], rows[0:1, SEG_C:SEG_C + 512]), wait=[ev_ones])
        r_bf = sc.v(lambda e: e.tensor_copy(
            bfa[32:33, OFF_ST2:OFF_ST2 + 512], rows[0:1, SEG_A:SEG_A + 512]))
        bank_s.free = r_bf
        return r_bf, mur_bf_cell[0]

    def ln_sums(src_fn, rd_ev):
        bank_s = take()
        for fc in range(NC_F):
            sc.t(lambda e, bk=bank_s, fc=fc, ap=src_fn(fc): e.matmul(
                bk.ap[0:1, 0:512], onescol, ap,
                start=(fc == 0), stop=(fc == 7)),
                wait=([bank_s.free, rd_ev(fc)] if fc == 0 else [rd_ev(fc)]))
        smm = None
        for fc in range(NC_F):
            sqw = [rd_ev(fc)]
            if smm is not None:
                sqw.append(smm)
            elif sq_cell[0] is not None:
                sqw.append(sq_cell[0])
            sqe = sc.v(lambda e, fc=fc, ap=src_fn(fc): e.tensor_tensor(
                bsl(OFF_SQ, 512), ap, ap, op=ALU.mult), wait=sqw)
            smm = sc.t(lambda e, bk=bank_s, fc=fc: e.matmul(
                bk.ap[32:33, 0:512], onescol, bsl(OFF_SQ, 512),
                start=(fc == 0), stop=(fc == 7)), wait=[sqe])
        bank_s.ready = smm
        sq_cell[0] = smm
        return bank_s

    def ln_norm(src_fn, dst_fn, s_evs, filler=None):
        bank_s = ln_sums(src_fn, lambda fc: s_evs[fc])
        r_ev, mur_ev = ln_stats(bank_s)
        rel(bank_s)
        if filler is not None:
            filler()
        bb = take()
        rb = sc.t(lambda e, bk=bb: e.matmul(
            bk.ap[:, 0:512], bc2_32, bfa[32:34, OFF_ST2:OFF_ST2 + 512], start=True, stop=True),
            wait=[bb.free, r_ev, ld_consts])
        bb.ready = rb
        bcx = take()
        cbx = sc.t(lambda e, bk=bcx: e.matmul(
            bk.ap[:, 0:512], bc2, bfa[0:2, OFF_ST2:OFF_ST2 + 512], start=True, stop=True),
            wait=[bcx.free, mur_ev])
        bcx.ready = cbx
        outev = [None] * NC_F
        for fc in range(NC_F):
            sc.v(lambda e, fc=fc, bk=bb, ap=src_fn(fc): e.tensor_tensor(
                ap, ap, bk.ap[:, 0:512], op=ALU.mult), wait=[bb.ready, sq_cell[0]])
            outev[fc] = sc.v(lambda e, fc=fc, bk=bcx, ap=src_fn(fc), d=dst_fn(fc): e.tensor_tensor(
                d, ap, bk.ap[:, 0:512], op=ALU.add), wait=[bcx.ready])
        bb.free = outev[-1]
        bcx.free = outev[-1]
        rel(bb)
        rel(bcx)
        return outev

    def l0_layer():
        import os as _os
        NTILE_RUN = int(_os.environ.get("DEV_NTILE", NTILE))
        LVL = int(_os.environ.get("DEV_LVL", "99"))
        l0_layer.last_sevac = None
        l0_layer.s_free_dc = None
        l0_layer.pre = {}
        l0_layer.qk_free = [None, None]
        l0_layer.ktm_free = None
        s_evac_ev = [None] * NPAIR
        xq_free = None
        vt_free = None
        ao_free = None
        attd_free = [None, None]
        last_x_ev = None
        for t in range(NTILE_RUN):
            ts = t % 4
            first_of_seq = (ts == 0)

            def xsl(fc, t=t):
                return x_fm[:, fc * T + t * TILE: fc * T + (t + 1) * TILE]

            ev_xt = x_ready[4 * t + 3]

            v_ev = None
            for c4 in range(4):
                for j in range(2):
                    bk = take()
                    vtw = [l0_layer.s_free_dc[2 * c4], l0_layer.s_free_dc[2 * c4 + 1]] \
                        if l0_layer.s_free_dc is not None else []
                    last = None
                    for ck in range(8):
                        last = sc.t(lambda e, bk=bk, ck=ck, c4=c4, j=j, t=t: e.matmul(
                            bk.ap[:, 0:512],
                            x_fm[:, ck * T + (4 * t + c4) * 128: ck * T + (4 * t + c4) * 128 + 128],
                            bsl(OFF_WV + ck * 1024 + j * 512, 512),
                            start=(ck == 0), stop=(ck == 7)),
                            wait=([bk.free, x_ready[4 * t + c4], wev["wv"]] + vtw
                                  if ck == 0 else []))
                    bk.ready = last
                    v_ev = sc.s(lambda e, bk=bk, c4=c4, j=j: e.copy(
                        bsl(OFF_VT + c4 * 1024 + j * 512, 512), bk.ap[:, 0:512]), wait=[bk.ready])
                    bk.free = v_ev
                    rel(bk)

            if LVL < 2:
                continue
            def emit_g1(xsl_, ev_xt_):
                bk = take()
                last = None
                for ck in range(8):
                    last = sc.t(lambda e, bk=bk, ck=ck, xs=xsl_(ck): e.matmul(
                        bk.ap[0:64, 0:512], bsl(OFF_WG1 + ck * 64, 64), xs,
                        start=(ck == 0), stop=(ck == 7)),
                        wait=([bk.free, ev_xt_, wev["wg1"]] if ck == 0 else []))
                bk.ready = last
                ev = sc.s(lambda e, bk=bk: e.copy(bsl(OFF_G1, 512, 0, 64), bk.ap[0:64, 0:512]),
                          wait=[bk.ready])
                bk.free = ev
                rel(bk)
                return ev

            g1_ev = l0_layer.pre.pop("g1_ev", None)
            if g1_ev is None:
                g1_ev = emit_g1(xsl, ev_xt)

            def emit_g2(g1e, aof):
                gev = [None] * NC_F
                for fc in range(NC_F):
                    bk = take()
                    mv = sc.t(lambda e, bk=bk, fc=fc: e.matmul(
                        bk.ap[:, 0:512], bfa[0:64, OFF_WG2 + fc * 128: OFF_WG2 + (fc + 1) * 128],
                        bsl(OFF_G1, 512, 0, 64), start=True, stop=True),
                        wait=[bk.free, g1e, wev["wg2"], aof])
                    bk.ready = mv
                    gev[fc] = sc.s(lambda e, bk=bk, fc=fc: e.activation(
                        bsl(OFF_AO + fc * 512, 512), bk.ap[:, 0:512], AF.Sigmoid),
                        wait=[bk.ready])
                    bk.free = gev[fc]
                    rel(bk)
                return gev

            gate_ev = l0_layer.pre.pop("gate_ev", None)
            if gate_ev is None:
                gate_ev = emit_g2(g1_ev, ao_free)
                if t == 0:
                    tap("gate0", bsl(OFF_AO, 512), gate_ev[0])

            if LVL < 3:
                continue
            def QB(p, off, n, p0=0, p1=128):
                base = OFF_QT if p % 2 == 0 else OFF_F0B2
                return bfa[p0:p1, base + off: base + off + n]

            def KB(p, off, n, p0=0, p1=128):
                base = OFF_KT if p % 2 == 0 else OFF_F0B2 + 512
                return bfa[p0:p1, base + off: base + off + n]

            po_ev = [None] * NPAIR
            silu_ev = l0_layer.pre.pop("silu_ev", {})
            rope_ev = l0_layer.pre.pop("rope_ev", {})
            ktm_ev_d = l0_layer.pre.pop("ktm_ev_d", {})

            def l0_proj(p, which, xsl_=None, ev_xt_=None, ts_=None):
                if xsl_ is None:
                    xsl_, ev_xt_, ts_ = xsl, ev_xt, ts
                woff = OFF_WQ if which == "q" else OFF_WK
                wname = "wq" if which == "q" else "wk"
                BUFF = QB if which == "q" else KB
                bk = take()
                last = None
                for ck in range(8):
                    last = sc.t(lambda e, bk=bk, ck=ck, woff=woff, p=p, xs=xsl_(ck): e.matmul(
                        bk.ap[:, 0:512], bsl(woff + ck * 1024 + p * 128, 128), xs,
                        start=(ck == 0), stop=(ck == 7)),
                        wait=([bk.free, ev_xt_, wev[wname], l0_layer.qk_free[p % 2]]
                              if ck == 0 else []))
                bk.ready = last
                ev = sc.s(lambda e, bk=bk, d=BUFF(p, 0, 512): e.activation(
                    d, bk.ap[:, 0:512], AF.Silu), wait=[bk.ready])
                bk.free = ev
                rel(bk)
                silu_ev[(p, which)] = ev
                bk2 = take()
                rot = sc.t(lambda e, bk=bk2, srcb=BUFF(p, 0, 512): e.matmul(
                    bk.ap[:, 0:512], p2m, srcb, start=True, stop=True),
                    wait=[bk2.free, ev, ld_consts])
                bk2.ready = rot
                sc.v(lambda e, srcb=BUFF(p, 0, 512), ts=ts_: e.tensor_tensor(
                    bsl(OFF_RTMP, 512), srcb, ctab[:, ts * 512:(ts + 1) * 512],
                    op=ALU.mult), wait=[ev])
                sc.v(lambda e, bk=bk2, d=BUFF(p, 0, 512), ts=ts_: e.tensor_tensor(
                    d, bk.ap[:, 0:512], ctab[:, 2048 + ts * 512: 2048 + (ts + 1) * 512],
                    op=ALU.mult), wait=[bk2.ready])
                dn = sc.v(lambda e, d=BUFF(p, 0, 512): e.tensor_tensor(
                    d, d, bsl(OFF_RTMP, 512), op=ALU.add))
                bk2.free = dn
                rel(bk2)
                rope_ev[(p, which)] = dn
                if t == 0 and p == 0 and which == "q":
                    tap("q_silu", QB(0, 0, 512), ev)
                    tap("q_rope", QB(0, 0, 512), dn)

            def l0_ktm(p):
                tp = None
                for c4 in range(4):
                    tp = sc.t(lambda e, c4=c4, srcb=KB(p, c4 * 128, 128): e.transpose(
                        tbank.ap[:, c4 * 128:(c4 + 1) * 128], srcb, idt_bf),
                        wait=([rope_ev[(p, "k")], tbank.free, l0_layer.ktm_free]
                              if c4 == 0 else []))
                tbank.ready = tp
                ev = sc.s(lambda e: e.copy(bsl(OFF_KTM, 512), tbank.ap[:, 0:512]),
                          wait=[tbank.ready])
                tbank.free = ev
                ktm_ev_d[p] = ev

            def l0_attn_chunk(p, c4, po, po2, filler=None):
                cs = ts * 4 + c4
                bdA = take()
                bdB = take()
                mA = sc.t(lambda e, bd=bdA, kq=KB(p, c4 * 128, 128, 0, 64),
                          qq=QB(p, c4 * 128, 128, 0, 64): e.matmul(
                    bd.ap[:, 0:128], kq, qq, start=True, stop=True),
                    wait=[bdA.free, bdB.free, rope_ev[(p, "q")], rope_ev[(p, "k")]])
                mB = sc.t(lambda e, bd=bdB, kq=KB(p, c4 * 128, 128, 64, 128),
                          qq=QB(p, c4 * 128, 128, 64, 128): e.matmul(
                    bd.ap[:, 0:128], kq, qq, start=True, stop=True,
                    tile_position=(64, 0), skip_group_check=True))
                bdA.ready = mA
                bdB.ready = mB
                par = c4 % 2
                sc.v(lambda e, bd=bdA, par=par: e.tensor_tensor(
                    bsl(OFF_ATTD + par * 256, 128), bd.ap[:, 0:128], triu2[:, 0:128],
                    op=ALU.mult), wait=[bdA.ready, bdB.ready, attd_free[par]])
                m_ev = sc.v(lambda e, bd=bdB, par=par: e.tensor_tensor(
                    bsl(OFF_ATTD + par * 256 + 128, 128), bd.ap[:, 0:128], triu2[:, 0:128],
                    op=ALU.mult))
                bdA.free = m_ev
                bdB.free = m_ev
                rel(bdA)
                rel(bdB)
                if t == 0 and p == 0 and c4 == 0:
                    tap("attd", bsl(OFF_ATTD, 256), m_ev)
                if filler is not None:
                    filler()
                last_mm = (c4 == 3)
                sc.t(lambda e, po=po, c4=c4, p=p, st=(last_mm and cs == 0): e.matmul(
                    po.ap[0:64, c4 * 128:(c4 + 1) * 128],
                    bfa[:, OFF_VT + c4 * 1024 + 2 * p * 64: OFF_VT + c4 * 1024 + 2 * p * 64 + 64],
                    bsl(OFF_ATTD + (c4 % 2) * 256, 128), start=(c4 == 0), stop=st),
                    wait=([po.free, po2.free, m_ev, v_ev] if c4 == 0 else [m_ev]))
                last_po = sc.t(lambda e, po=po2, c4=c4, p=p, st=(last_mm and cs == 0): e.matmul(
                    po.ap[64:128, c4 * 128:(c4 + 1) * 128],
                    bfa[:, OFF_VT + c4 * 1024 + (2 * p + 1) * 64: OFF_VT + c4 * 1024 + (2 * p + 1) * 64 + 64],
                    bsl(OFF_ATTD + (c4 % 2) * 256 + 128, 128), start=(c4 == 0), stop=st,
                    tile_position=(0, 64), skip_group_check=True))
                if cs > 0:
                    sc.t(lambda e, po=po, c4=c4, p=p, st=last_mm,
                         qq=QB(p, c4 * 128, 128, 0, 64): e.matmul(
                        po.ap[0:64, c4 * 128:(c4 + 1) * 128], bsl(OFF_SSB + p * 64, 64, 0, 64),
                        qq, start=False, stop=st),
                        wait=[s_evac_ev[p]] if s_evac_ev[p] else [])
                    last_po = sc.t(lambda e, po=po2, c4=c4, p=p, st=last_mm,
                                   qq=QB(p, c4 * 128, 128, 64, 128): e.matmul(
                        po.ap[64:128, c4 * 128:(c4 + 1) * 128], bsl(OFF_SSB + p * 64, 64, 64, 128),
                        qq, start=False, stop=st,
                        tile_position=(64, 64), skip_group_check=True))
                su_w = [ktm_ev_d[p], v_ev]
                if l0_layer.last_sevac is not None:
                    su_w.append(l0_layer.last_sevac)
                s_first = (first_of_seq and c4 == 0 and p == 0)
                sc.t(lambda e, c4=c4, p=p, st=s_first: e.matmul(
                    S_ps.ap[0:64, p * 64:(p + 1) * 64], bsl(OFF_KTM + c4 * 128, 64),
                    bfa[:, OFF_VT + c4 * 1024 + 2 * p * 64: OFF_VT + c4 * 1024 + 2 * p * 64 + 64],
                    start=st, stop=True, skip_group_check=True), wait=su_w)
                sB = sc.t(lambda e, c4=c4, p=p, st=s_first: e.matmul(
                    S_ps.ap[64:128, p * 64:(p + 1) * 64], bsl(OFF_KTM + c4 * 128 + 64, 64),
                    bfa[:, OFF_VT + c4 * 1024 + (2 * p + 1) * 64: OFF_VT + c4 * 1024 + (2 * p + 1) * 64 + 64],
                    start=st, stop=True, tile_position=(0, 64), skip_group_check=True))
                s_evac_ev[p] = sc.s(lambda e, p=p: e.copy(
                    bsl(OFF_SSB + p * 64, 64), S_ps.ap[:, p * 64:(p + 1) * 64]), wait=[sB])
                l0_layer.last_sevac = s_evac_ev[p]
                attd_free[par] = last_po
                if t == 0 and p == 0 and c4 == 0:
                    tap("s_sb", bsl(OFF_SSB, 64), s_evac_ev[p])

            if not l0_layer.pre.pop("proj0", False):
                l0_proj(0, "q")
                l0_proj(0, "k")
            if not l0_layer.pre.pop("ktm0", False):
                l0_ktm(0)
            for p in range(NPAIR):
                po = take()
                po2 = take()
                for c4 in range(4):
                    filler = None
                    if p + 1 < NPAIR:
                        if c4 == 0:
                            filler = (lambda pp=p + 1: l0_proj(pp, "q"))
                        elif c4 == 2:
                            filler = (lambda pp=p + 1: l0_proj(pp, "k"))
                    l0_attn_chunk(p, c4, po, po2, filler)
                po.ready = (("sT", sc.cnt["sT"]))
                sc.v(lambda e, po=po, p=p: e.tensor_tensor(
                    bsl(OFF_AO + p * 512, 512, 0, 64), po.ap[0:64, 0:512],
                    bsl(OFF_AO + p * 512, 512, 0, 64),
                    op=ALU.mult), wait=[po.ready, gate_ev[p]])
                po_ev[p] = sc.v(lambda e, po=po2, p=p: e.tensor_tensor(
                    bsl(OFF_AO + p * 512, 512, 64, 128), po.ap[64:128, 0:512],
                    bsl(OFF_AO + p * 512, 512, 64, 128),
                    op=ALU.mult))
                po.free = po_ev[p]
                po2.free = po_ev[p]
                rel(po)
                rel(po2)
                l0_layer.qk_free[p % 2] = (("sT", sc.cnt["sT"]))
                l0_layer.ktm_free = (("sT", sc.cnt["sT"]))
                if p + 1 < NPAIR:
                    l0_ktm(p + 1)
                if t == 0 and p == 0:
                    tap("ao_p0", bsl(OFF_AO, 512), po_ev[0])

            if LVL < 8:
                continue
            nx = (t + 1 < NTILE_RUN)
            if nx:
                t1 = t + 1

                def xsl_next(fc, t1=t1):
                    return x_fm[:, fc * T + t1 * TILE: fc * T + (t1 + 1) * TILE]

                ev_xt_next = x_ready[4 * t1 + 3]
                ts_next = t1 % 4
            bank_s = ln_sums(lambda fc: bsl(OFF_AO + fc * 512, 512), lambda fc: po_ev[fc])
            r_ev, mur_ev = ln_stats(bank_s)
            rel(bank_s)
            if nx:
                l0_layer.pre["g1_ev"] = emit_g1(xsl_next, ev_xt_next)
            bank_b = take()
            rb = sc.t(lambda e, bk=bank_b: e.matmul(
                bk.ap[:, 0:512], bc2_32, bfa[32:34, OFF_ST2:OFF_ST2 + 512], start=True, stop=True),
                wait=[bank_b.free, r_ev, ev_ones, ld_consts])
            bank_b.ready = rb
            if nx:
                l0_proj(0, "q", xsl_next, ev_xt_next, ts_next)
            m_ev = [None] * NC_F
            for fc in range(NC_F):
                m_ev[fc] = sc.v(lambda e, fc=fc, bk=bank_b: e.tensor_tensor(
                    bsl(OFF_AO + fc * 512, 512), bsl(OFF_AO + fc * 512, 512), bk.ap[:, 0:512],
                    op=ALU.mult), wait=[bank_b.ready, sq_cell[0]])
            bank_b.free = m_ev[-1]
            rel(bank_b)

            if LVL < 9:
                continue
            s_ev = [None] * NC_F
            for dc in range(NC_F):
                bk = take()
                for ck in range(NC_F):
                    sc.t(lambda e, bk=bk, ck=ck, dc=dc: e.matmul(
                        bk.ap[:, 0:512], bsl(OFF_WOW + ck * 1024 + dc * 128, 128),
                        bsl(OFF_AO + ck * 512, 512), start=(ck == 0), stop=False),
                        wait=([bk.free, wev["wo"], wev["u2"], m_ev[-1]] if ck == 0 else []))
                last = sc.t(lambda e, bk=bk, dc=dc: e.matmul(
                    bk.ap[:, 0:512], bfa[0:2, OFF_U2 + dc * 128: OFF_U2 + (dc + 1) * 128],
                    st2bf, start=False, stop=True),
                    wait=[mur_bf_cell[0], ev_ones])
                bk.ready = last
                s_ev[dc] = sc.v(lambda e, bk=bk, dc=dc, xs=xsl(dc): e.scalar_tensor_tensor(
                    bsl(OFF_VT + dc * 512, 512), bk.ap[:, 0:512], cols[:, 32 + dc:33 + dc],
                    xs, op0=ALU.add, op1=ALU.add),
                    wait=[bk.ready])
                bk.free = s_ev[dc]
                rel(bk)
            ao_free = (("sT", sc.cnt["sT"]))

            if LVL < 10:
                continue
            if t == 0:
                tap("s0", bsl(OFF_VT, 512), s_ev[0])
                tap("stats0", rows[0:1, 0:2048], s_ev[0])
            fill1 = (lambda: l0_proj(0, "k", xsl_next, ev_xt_next, ts_next)) if nx else None
            x1_ev = ln_norm(lambda fc: bsl(OFF_VT + fc * 512, 512), lambda fc: xsl(fc), s_ev,
                            filler=fill1)
            if upto == "x1":
                vt_free = x1_ev[-1]
                l0_layer.s_free_dc = x1_ev
                if nx:
                    l0_layer.pre["proj0"] = True
                    l0_layer.pre["silu_ev"] = silu_ev
                    l0_layer.pre["rope_ev"] = rope_ev
                    l0_layer.pre["ktm_ev_d"] = ktm_ev_d
                ao_free = x1_ev[-1]
                last_x_ev = x1_ev[-1]
                continue

            q_ev = [None] * NC_F
            for fc in range(NC_F):
                q_ev[fc] = sc.s(lambda e, fc=fc, xs=xsl(fc): e.copy(
                    f8a[:, 8192 + fc * 512: 8192 + (fc + 1) * 512], xs), wait=[x1_ev[fc]])
            s_ev2 = [None] * NC_F
            for dc in range(NC_F):
                bk = take()
                for ck in range(NC_F):
                    sc.t(lambda e, bk=bk, ck=ck, dc=dc: e.matmul(
                        bk.ap[:, 0:512], f8a[:, ck * 1024 + dc * 128: ck * 1024 + dc * 128 + 128],
                        f8a[:, 8192 + ck * 512: 8192 + (ck + 1) * 512],
                        start=(ck == 0), stop=(ck == NC_F - 1)),
                        wait=([bk.free, wev["w8"], q_ev[-1]] if ck == 0 else []))
                bk.ready = (("sT", sc.cnt["sT"]))
                s_ev2[dc] = sc.v(lambda e, bk=bk, dc=dc, xs=xsl(dc): e.scalar_tensor_tensor(
                    bsl(OFF_VT + dc * 512, 512), bk.ap[:, 0:512], cols[:, 40 + dc:41 + dc],
                    xs, op0=ALU.add, op1=ALU.add),
                    wait=[bk.ready, x1_ev[dc], wev["f0b2"]])
                bk.free = s_ev2[dc]
                rel(bk)
            if nx:
                def fill2(aof=ao_free):
                    l0_ktm(0)
                    l0_layer.pre["ktm0"] = True
                    l0_layer.pre["gate_ev"] = emit_g2(l0_layer.pre["g1_ev"], aof)
            else:
                fill2 = None
            x2_ev = ln_norm(lambda fc: bsl(OFF_VT + fc * 512, 512), lambda fc: xsl(fc), s_ev2,
                            filler=fill2)
            vt_free = x2_ev[-1]
            l0_layer.s_free_dc = x2_ev
            if nx:
                l0_layer.pre["proj0"] = True
                l0_layer.pre["silu_ev"] = silu_ev
                l0_layer.pre["rope_ev"] = rope_ev
                l0_layer.pre["ktm_ev_d"] = ktm_ev_d
            ao_free = x2_ev[-1]
            last_x_ev = x2_ev[-1]
        return last_x_ev


    L1X, L1QKV, L1Q, L1K, L1V, L1AO, L1U, L1RT = 0, 8192, 10240, 12288, 14336, 30976, 47360, 47872
    L1S = 24576

    def l1_layer(x2_done):
        import os as _os
        if S_ps in held:
            pass
        if S_ps not in avail:
            S_ps.free = x2_done
            avail.append(S_ps)
        l1_layer.pending_norm = None
        l1_layer.x_free = x2_done
        l1_layer.v_free = x2_done
        l1_layer.u_free = x2_done
        l1_layer.ao_free = x2_done
        l1_layer.s_free = x2_done
        neg3_ev = sc.v(lambda e: e.memset(cols[:, 50:51], -3.0))
        fz_ev = sc.v(lambda e: e.memset(bfa[0:64, 48384:49408], 0.0))
        fz2_ev = sc.v(lambda e: e.memset(env["f32a"][0:64, 0:1024], 0.0), wait=[fz_ev])
        f32a = env["f32a"]
        vv = bfa[:, L1V:L1V + 16640].rearrange("p (c h d) -> p c h d", c=16, h=16)
        x3_ev_all = []
        vproj_last_reader = None
        x_ev_o = {}
        for s_ in range(SEQ_PER_CORE):
            sbase = s_ * S
            ones_v_ev = sc.v(lambda e: e.memset(vv[:, :, :, 64:65], 1.0),
                             wait=[l1_layer.v_free])
            wv1_ev = None
            for ck in range(8):
                wv1_ev = staged(
                    w_in["qkvT"][ck * 128:(ck + 1) * 128, 2048:3072], 1024,
                    lambda sl, ld, ck=ck: sc.v(
                        lambda e, sl=sl: e.tensor_copy(bsl(L1X + ck * 1024, 1024), sl),
                        wait=[ld, l1_layer.x_free]))
            v_ev = None
            for tc in range(16):
                for j in range(2):
                    bk = take()
                    last = None
                    for ck in range(8):
                        last = sc.t(lambda e, bk=bk, ck=ck, tc=tc, j=j, sbase=sbase: e.matmul(
                            bk.ap[:, 0:512],
                            x_fm[:, ck * T + sbase + tc * 128: ck * T + sbase + tc * 128 + 128],
                            bsl(L1X + ck * 1024 + j * 512, 512),
                            start=(ck == 0), stop=(ck == 7)),
                            wait=([bk.free, wv1_ev, x2_done, l1_layer.v_free] if ck == 0 else []))
                    bk.ready = last
                    def vevac(e, bk=bk, tc=tc, j=j):
                        dst = vv[:, tc:tc + 1, 8 * j:8 * j + 8, 0:64]
                        srcv = bk.ap[:, 0:512].rearrange("p (o h d) -> p o h d", o=1, h=8)
                        return e.copy(dst, srcv)
                    v_ev = sc.s(vevac, wait=[bk.ready, ones_v_ev])
                    bk.free = v_ev
                    rel(bk)
            f8bf = f8a[:, 0:12288].bitcast(BF16)

            def BUF(p, which, off, n, p0=0, p1=128):
                if p % 2 == 0:
                    base = {"q": L1Q, "k": L1K, "w": L1QKV}[which]
                    return bfa[p0:p1, base + off: base + off + n]
                base = {"q": 0, "k": 2048, "w": 4096}[which]
                return f8bf[p0:p1, base + off: base + off + n]

            ao_ev = {}
            att_free = wv1_ev
            wfree = [x2_done, x2_done]
            qkfree = [x2_done, x2_done]
            wq_done = {}
            rope_done = {}

            def emit_load(p):
                ev = None
                for ck in range(8):
                    ev = staged(
                        w_in["qkvT"][ck * 128:(ck + 1) * 128, 128 * p:128 * (p + 1)], 128,
                        lambda sl, ld, ck=ck, p=p: sc.v(
                            lambda e, sl=sl, w=BUF(p, "w", ck * 128, 128): e.tensor_scalar(
                                w, sl, 0.125, None, op0=ALU.mult),
                            wait=[ld, wfree[p % 2]]))
                for ck in range(8):
                    ev = staged(
                        w_in["qkvT"][ck * 128:(ck + 1) * 128, 1024 + 128 * p:1024 + 128 * (p + 1)], 128,
                        lambda sl, ld, ck=ck, p=p: sc.v(
                            lambda e, sl=sl, w=BUF(p, "w", 1024 + ck * 128, 128): e.tensor_copy(w, sl),
                            wait=[ld]))
                wq_done[p] = ev

            def emit_proj(p, which, i):
                woff = 0 if which == "q" else 1024
                bk = take()
                last = None
                for ck in range(8):
                    last = sc.t(lambda e, bk=bk, w=BUF(p, "w", woff + ck * 128, 128), ck=ck,
                                i=i, sbase=sbase: e.matmul(
                        bk.ap[:, 0:512], w,
                        x_fm[:, ck * T + sbase + i * 512: ck * T + sbase + (i + 1) * 512],
                        start=(ck == 0), stop=(ck == 7)),
                        wait=([bk.free, wq_done[p], x2_done, qkfree[p % 2]] if ck == 0 else []))
                bk.ready = last
                pe = sc.v(lambda e, bk=bk, d=BUF(p, which, i * 512, 512): e.tensor_copy(
                    d, bk.ap[:, 0:512]), wait=[bk.ready])
                bk.free = pe
                rel(bk)
                bk2 = take()
                rot = sc.t(lambda e, bk=bk2, srcb=BUF(p, which, i * 512, 512): e.matmul(
                    bk.ap[:, 0:512], p2m, srcb, start=True, stop=True),
                    wait=[bk2.free, pe, ld_consts])
                bk2.ready = rot
                sc.v(lambda e, srcb=BUF(p, which, i * 512, 512), i=i: e.tensor_tensor(
                    bsl(L1RT, 512), srcb, ctab[:, i * 512:(i + 1) * 512], op=ALU.mult), wait=[pe])
                sc.v(lambda e, bk=bk2, d=BUF(p, which, i * 512, 512), i=i: e.tensor_tensor(
                    d, bk.ap[:, 0:512], ctab[:, 2048 + i * 512: 2048 + (i + 1) * 512],
                    op=ALU.mult), wait=[bk2.ready])
                dn = sc.v(lambda e, d=BUF(p, which, i * 512, 512): e.tensor_tensor(
                    d, d, bsl(L1RT, 512), op=ALU.add))
                bk2.free = dn
                rel(bk2)
                rope_done[(p, which, i)] = dn

            emit_load(0)
            for i in range(4):
                emit_proj(0, "q", i)
                emit_proj(0, "k", i)
            for p in range(NPAIR):
                if p + 1 < NPAIR:
                    emit_load(p + 1)
                k_done = rope_done[(p, "k", 3)]
                for i in range(4):
                    q_done = rope_done[(p, "q", i)]
                    poA = take()
                    poB = take()
                    avA = avB = None
                    for half in range(2):
                        if half == 1 and l1_layer.pending_norm is not None:
                            l1_layer.pending_norm()
                            l1_layer.pending_norm = None
                        exp_half = []
                        for kc8 in range(8):
                            kc = 8 * half + kc8
                            bkA = take()
                            bkB = take()
                            mA = sc.t(lambda e, bk=bkA, kq=BUF(p, "k", kc * 128, 128, 0, 64),
                                      qq=BUF(p, "q", i * 512, 512, 0, 64): e.matmul(
                                bk.ap[:, 0:512], kq, qq, start=True, stop=True),
                                wait=[bkA.free, bkB.free, k_done, q_done])
                            mB = sc.t(lambda e, bk=bkB, kq=BUF(p, "k", kc * 128, 128, 64, 128),
                                      qq=BUF(p, "q", i * 512, 512, 64, 128): e.matmul(
                                bk.ap[:, 0:512], kq, qq, start=True, stop=True,
                                tile_position=(64, 0), skip_group_check=True))
                            eA = sc.s(lambda e, bk=bkA, kc8=kc8: e.activation(
                                bsl(L1X + kc8 * 512, 512), bk.ap[:, 0:512], AF.Exp,
                                bias=cols[:, 50:51]),
                                wait=[mA, neg3_ev] + ([att_free] if att_free else []))
                            eB = sc.s(lambda e, bk=bkB, kc8=kc8: e.activation(
                                bsl(L1X + 4096 + kc8 * 512, 512), bk.ap[:, 0:512], AF.Exp,
                                bias=cols[:, 50:51]), wait=[mB])
                            bkA.free = eA
                            bkB.free = eB
                            rel(bkA)
                            rel(bkB)
                            exp_half.append(eB)
                            if kc8 == 2 and p + 1 < NPAIR:
                                emit_proj(p + 1, "q" if half == 0 else "k", i)
                        for kc8 in range(8):
                            kc = 8 * half + kc8
                            avA = sc.t(lambda e, po=poA, kc=kc, kc8=kc8, p=p: e.matmul(
                                po.ap[0:65, 0:512],
                                bfa[:, L1V + kc * 1040 + 2 * p * 65: L1V + kc * 1040 + 2 * p * 65 + 65],
                                bsl(L1X + kc8 * 512, 512),
                                start=(kc == 0), stop=(kc == 15), skip_group_check=True),
                                wait=([poA.free, poB.free, v_ev, exp_half[kc8]] if kc == 0
                                      else [exp_half[kc8]]))
                            avB = sc.t(lambda e, po=poB, kc=kc, kc8=kc8, p=p: e.matmul(
                                po.ap[0:65, 0:512],
                                bfa[:, L1V + kc * 1040 + (2 * p + 1) * 65: L1V + kc * 1040 + (2 * p + 1) * 65 + 65],
                                bsl(L1X + 4096 + kc8 * 512, 512),
                                start=(kc == 0), stop=(kc == 15), skip_group_check=True))
                        att_free = avB
                    def norm_closure(p=p, i=i, poA=poA, poB=poB, avA=avA, avB=avB):
                        f32a_ = env["f32a"]
                        r0 = sc.v(lambda e, po=poA: e.reciprocal(
                            f32a_[0:1, 0:512], po.ap[64:65, 0:512]), wait=[avA, avB, fz2_ev])
                        rA = sc.v(lambda e: e.tensor_copy(bfa[0:1, 48384:48896], f32a_[0:1, 0:512]),
                                  wait=[fz_ev])
                        r1 = sc.v(lambda e, po=poB: e.reciprocal(
                            f32a_[0:1, 512:1024], po.ap[64:65, 0:512]))
                        rB = sc.v(lambda e: e.tensor_copy(bfa[0:1, 48896:49408], f32a_[0:1, 512:1024]))
                        uA = sc.s(lambda e, po=poA: e.copy(bsl(L1U, 512, 0, 64), po.ap[0:64, 0:512]),
                                  wait=[rA, l1_layer.u_free])
                        poA.free = uA
                        rel(poA)
                        bcA = take()
                        mmA = sc.t(lambda e, bk=bcA: e.matmul(
                            bk.ap[:, 0:512], bc2, bfa[0:2, 48384:48896], start=True, stop=True),
                            wait=[bcA.free, rA])
                        bcA.ready = mmA
                        evA = sc.v(lambda e, bk=bcA, p=p, i=i: e.tensor_tensor(
                            bfa[0:64, L1AO + p * 2048 + i * 512: L1AO + p * 2048 + (i + 1) * 512],
                            bsl(L1U, 512, 0, 64), bk.ap[0:64, 0:512], op=ALU.mult),
                            wait=[bcA.ready, uA] + ([l1_layer.ao_free] if l1_layer.ao_free else []))
                        uB = sc.s(lambda e, po=poB: e.copy(bsl(L1U, 512, 64, 128), po.ap[0:64, 0:512]),
                                  wait=[rB, evA])
                        poB.free = uB
                        rel(poB)
                        bcB = take()
                        mmB = sc.t(lambda e, bk=bcB: e.matmul(
                            bk.ap[:, 0:512], bc2, bfa[0:2, 48896:49408], start=True, stop=True),
                            wait=[bcB.free, rB])
                        bcB.ready = mmB
                        evB = sc.v(lambda e, bk=bcB, p=p, i=i: e.tensor_tensor(
                            bfa[64:128, L1AO + p * 2048 + i * 512: L1AO + p * 2048 + (i + 1) * 512],
                            bsl(L1U, 512, 64, 128), bk.ap[0:64, 0:512], op=ALU.mult),
                            wait=[bcB.ready, uB])
                        bcA.free = evA
                        bcB.free = evB
                        rel(bcA)
                        rel(bcB)
                        l1_layer.u_free = evB
                        ao_ev[(p, i)] = evB
                    l1_layer.pending_norm = norm_closure
                if p == NPAIR - 1 and l1_layer.pending_norm is not None:
                    l1_layer.pending_norm()
                    l1_layer.pending_norm = None
                qkfree[p % 2] = (("sT", sc.cnt["sT"]))
                wfree[p % 2] = qkfree[p % 2]
            l1_layer.v_free = ao_ev[(NPAIR - 1, 3)]
            ot_ev = None
            for ck in range(8):
                ot_ev = staged(
                    w_in["oT"][ck * 128:(ck + 1) * 128, :], 1024,
                    lambda sl, ld, ck=ck: sc.v(
                        lambda e, sl=sl: e.tensor_copy(bsl(L1X + ck * 1024, 1024), sl),
                        wait=[ld, att_free]))
            for tt in range(4):
                s_ev3 = [None] * NC_F
                for dc in range(NC_F):
                    bk = take()
                    last = None
                    for ck in range(NC_F):
                        last = sc.t(lambda e, bk=bk, ck=ck, dc=dc, tt=tt: e.matmul(
                            bk.ap[:, 0:512], bsl(L1X + ck * 1024 + dc * 128, 128),
                            bsl(L1AO + ck * 2048 + tt * 512, 512),
                            start=(ck == 0), stop=(ck == 7)),
                            wait=([bk.free, ot_ev] + [ao_ev[(pp, tt)] for pp in range(NPAIR)]
                                  if ck == 0 else []))
                    bk.ready = last
                    s_ev3[dc] = sc.v(lambda e, bk=bk, dc=dc, tt=tt, sbase=sbase: e.tensor_tensor(
                        bsl(L1S + dc * 512, 512),
                        x_fm[:, dc * T + sbase + tt * 512: dc * T + sbase + (tt + 1) * 512],
                        bk.ap[:, 0:512], op=ALU.add), wait=[bk.ready, l1_layer.s_free])
                    bk.free = s_ev3[dc]
                    rel(bk)
                x3_ev = ln_norm(
                    lambda fc: bsl(L1S + fc * 512, 512),
                    lambda fc, tt=tt, sbase=sbase: x_fm[:, fc * T + sbase + tt * 512: fc * T + sbase + (tt + 1) * 512],
                    s_ev3)
                x3_ev_all.append(x3_ev[-1])
                x_ev_o[(s_, tt)] = x3_ev[-1]
                l1_layer.s_free = x3_ev[-1]
            l1_layer.x_free = x3_ev_all[-1]
            l1_layer.ao_free = x3_ev_all[-1]
            l1_layer.v_free = x3_ev_all[-1]
        l1_layer.x3_done = x3_ev_all[-1]
        l1_layer.x_ev_o = x_ev_o
        return x3_ev_all[-1]
    l1_layer.x_free = None
    l1_layer.v_free = None
    l1_layer.u_free = None
    l1_layer.ao_free = None

    F1W, F1A1, F1A2 = 0, 8192, 16384
    F1H1P = (24576, 28672)
    F1H2P = (32768, 32768)
    F1SP = (36864, 40960)

    def f1_phase(x3_done, write_out):
        f32a = env["f32a"]
        wevs = {}
        for name, off in (("f1w", F1W), ("f1a1", F1A1), ("f1a2", F1A2)):
            for ck in range(8):
                wevs[name] = staged(
                    w_in[name][ck * 128:(ck + 1) * 128, :], 1024,
                    lambda sl, ld, ck=ck, off=off: sc.v(
                        lambda e, sl=sl: e.tensor_copy(bsl(off + ck * 1024, 1024), sl),
                        wait=[ld, x3_done]))
        slot_ev = [None, None]
        h_free = {0: [None, None, None], 1: [None, None, None]}
        h2_free = [None]
        cfree = [None, None]
        for t8 in range(NTILE):
            par = t8 % 2
            F1H1, F1H2, F1S = F1H1P[par], F1H2P[par], F1SP[par]
            hfp = h_free[par]
            sbase = (t8 // 4) * S
            tt = t8 % 4

            def x3sl(fc, t8=t8):
                return x_fm[:, fc * T + t8 * 512: fc * T + (t8 + 1) * 512]

            x3_ok = l1_layer.x_ev_o[(t8 // 4, tt)]
            prev_ev = [x3_ok] * NC_F
            for idx, (woff, dsto, wname) in enumerate(
                    ((F1W, F1H1, "f1w"), (F1A1, F1H2, "f1a1"))):
                nxt = [None] * NC_F
                for dc in range(NC_F):
                    bk = take()
                    last = None
                    for ck in range(NC_F):
                        src = (x3sl(ck) if idx == 0 else bsl(F1H2 + 0, 0) if False else
                               (x3sl(ck) if idx == 0 else bsl(F1H1 + ck * 512, 512)))
                        last = sc.t(lambda e, bk=bk, ck=ck, dc=dc, woff=woff, src=src: e.matmul(
                            bk.ap[:, 0:512], bsl(woff + ck * 1024 + dc * 128, 128), src,
                            start=(ck == 0), stop=(ck == 7)),
                            wait=([bk.free, wevs[wname],
                                   (hfp[idx] if idx == 0 else h2_free[0])] + prev_ev
                                  if ck == 0 else []))
                    bk.ready = last
                    nxt[dc] = sc.v(lambda e, bk=bk, dc=dc, dsto=dsto: e.tensor_copy(
                        bsl(dsto + dc * 512, 512), bk.ap[:, 0:512]), wait=[bk.ready])
                    bk.free = nxt[dc]
                    rel(bk)
                prev_ev = nxt
            s_ev4 = [None] * NC_F
            t_ev = [None] * NC_F
            for dc in range(NC_F):
                bk = take()
                last = None
                for ck in range(NC_F):
                    last = sc.t(lambda e, bk=bk, ck=ck, dc=dc, F1H2=F1H2: e.matmul(
                        bk.ap[:, 0:512], bsl(F1A2 + ck * 1024 + dc * 128, 128),
                        bsl(F1H2 + ck * 512, 512), start=(ck == 0), stop=(ck == 7)),
                        wait=([bk.free, wevs["f1a2"], hfp[2]] + prev_ev if ck == 0 else []))
                bk.ready = last
                t_ev[dc] = sc.s(lambda e, bk=bk, dc=dc, F1H1=F1H1: e.activation(
                    bsl(F1H1 + dc * 512, 512), bk.ap[:, 0:512], AF.Tanh,
                    bias=cols[:, 16 + dc:17 + dc]), wait=[bk.ready, wev["f1b"]])
                s_ev4[dc] = sc.v(lambda e, dc=dc, xs=x3sl(dc), F1S=F1S, F1H1=F1H1: e.tensor_tensor(
                    bsl(F1S + dc * 512, 512), xs, bsl(F1H1 + dc * 512, 512), op=ALU.add),
                    wait=[t_ev[dc]])
                bk.free = s_ev4[dc]
                rel(bk)
            bank_s = ln_sums(lambda fc: bsl(F1S + fc * 512, 512), lambda fc: s_ev4[fc])
            r_ev, mur_ev = ln_stats(bank_s)
            rel(bank_s)
            if not write_out:
                x4_ev = ln_norm(lambda fc, F1S=F1S: bsl(F1S + fc * 512, 512),
                                lambda fc, t8=t8: x_fm[:, fc * T + t8 * 512: fc * T + (t8 + 1) * 512],
                                s_ev4)
                h_free[par] = [x4_ev[-1]] * 3
                h2_free[0] = x4_ev[-1]
                continue
            for c4 in range(4):
                slot = (t8 * 4 + c4) % 2
                sl = f32a[:, slot * 1024:(slot + 1) * 1024]
                tp = None
                for fc in range(NC_F):
                    tp = sc.t(lambda e, fc=fc, c4=c4, F1S=F1S: e.transpose(
                        tbank.ap[:, fc * 128:(fc + 1) * 128],
                        bsl(F1S + fc * 512 + c4 * 128, 128), idt_bf),
                        wait=([tbank.free, sq_cell[0], mur_ev] if fc == 0 else []))
                tbank.ready = tp
                bkc = take()
                cmm = sc.t(lambda e, bk=bkc, c4=c4: e.matmul(
                    bk.ap[:, 0:1], rows[0:2, SEG_A + c4 * 128: SEG_A + (c4 + 1) * 128],
                    bc2f[:, 0:1], start=True, stop=False), wait=[bkc.free, r_ev])
                cmm = sc.t(lambda e, bk=bkc, c4=c4: e.matmul(
                    bk.ap[:, 1:2], rows[0:2, SEG_C + c4 * 128: SEG_C + (c4 + 1) * 128],
                    bc2f[:, 0:1], start=False, stop=True), wait=[mur_ev])
                bkc.ready = cmm
                rows_guard[0] = cmm
                cev = sc.v(lambda e, bk=bkc, c4=c4: e.tensor_copy(
                    cols[:, 48 + 2 * (c4 % 2):50 + 2 * (c4 % 2)], bk.ap[:, 0:2]),
                    wait=[bkc.ready, cfree[c4 % 2]])
                bkc.free = cev
                rel(bkc)
                oev = None
                for fc in range(NC_F):
                    oev = sc.v(lambda e, fc=fc, sl=sl, c4=c4: e.tensor_scalar(
                        sl[:, fc * 128:(fc + 1) * 128], tbank.ap[:, fc * 128:(fc + 1) * 128],
                        cols[:, 48 + 2 * (c4 % 2):49 + 2 * (c4 % 2)],
                        cols[:, 49 + 2 * (c4 % 2):50 + 2 * (c4 % 2)], op0=ALU.mult, op1=ALU.add),
                        wait=([tbank.ready, cev, slot_ev[slot]] if fc == 0 else []))
                cfree[c4 % 2] = oev
                tbank.free = oev
                st = sc.dma("gpsimd", lambda e, sl=sl, t8=t8, c4=c4: e.dma_start(
                    out=out[t8 * 512 + c4 * 128: t8 * 512 + (c4 + 1) * 128, :], in_=sl),
                    "sD2" if slot == 0 else "sD3", wait=[oev])
                slot_ev[slot] = st
            tend = (("sT", sc.cnt["sT"]))
            h_free[par] = [s_ev4[-1], tend, tend]
            h2_free[0] = tend
        return slot_ev

    ev_x_ready = (("sS", sc.cnt["sS"]))
    emit_generic_out = True
    if upto != "x0":
        load_weights_l0()
        x2_done = l0_layer()
        ev_x_ready = (("sV", sc.cnt["sV"]))
        if upto not in ("x1", "x2"):
            l1_layer(x2_done)
            ev_x_ready = (("sV", sc.cnt["sV"]))
            if upto == "x3":
                pass
            else:
                f1_phase(l1_layer.x3_done, write_out=True)
                emit_generic_out = False

    out_stage = env["f32a"][:, 0:2048]
    slot_ev = [tap.last, tap.last]
    if not emit_generic_out:
        n_tc_gen = 0
    else:
        n_tc_gen = n_tc
    for c in range(n_tc_gen):
        slot = c % 2
        sl = out_stage[:, slot * 1024:(slot + 1) * 1024]
        tb = tbank
        tp_ev = None
        for fc in range(NC_F):
            tp_ev = sc.t(lambda e, fc=fc, c=c, tb=tb:
                         e.transpose(tb.ap[:, fc * 128:(fc + 1) * 128],
                                     x_fm[:, fc * T + c * 128: fc * T + (c + 1) * 128], idt_bf),
                         wait=([ev_x_ready, ld_consts, tb.free, slot_ev[slot]] if fc == 0 else []))
        tb.ready = tp_ev
        ev = sc.v(lambda e, sl=sl, tb=tb: e.tensor_copy(sl, tb.ap[:, 0:1024]), wait=[tb.ready])
        tb.free = ev
        st = sc.dma("gpsimd", lambda e, sl=sl, c=c: e.dma_start(out=out[c * 128:(c + 1) * 128, :], in_=sl),
                    "sD2" if slot == 0 else "sD3", wait=[ev])
        slot_ev[slot] = st
    sc.emit("gpsimd", None, wait=[("sD2", sc.cnt["sD2"]), ("sD3", sc.cnt["sD3"])])


def host_constants():
    c = np.zeros((128, CONST_W), dtype=np.float32)
    d = np.float32(HD)
    inv = (1.0 / (np.float32(10000.0) ** (np.arange(0, HD, 2, dtype=np.float32) / d))).astype(np.float32)
    pos = np.arange(S, dtype=np.float32)
    ang = (pos[None, :] * np.concatenate([inv, inv])[:, None]).astype(np.float32)
    cos = np.cos(ang).astype(np.float32)
    sin = np.sin(ang).astype(np.float32)
    c[:, 0:S] = np.concatenate([cos, cos], axis=0)
    c[:, S:2 * S] = np.concatenate([sin, sin], axis=0)
    c[:, 4096:4224] = np.eye(128, dtype=np.float32)
    triu = np.triu(np.ones((128, 128), dtype=np.float32))
    c[:, 4224:4352] = triu
    c[:, 4352:4480] = triu
    P = np.zeros((64, 64), dtype=np.float32)
    for i in range(32):
        P[i + 32, i] = -1.0
        P[i, i + 32] = 1.0
    c[0:64, 4480:4544] = P
    c[64:128, 4544:4608] = P
    c[:, 4608] = 1.0
    c[0, 4616:4744] = 1.0
    c[32, 4616:4744] = 1.0
    return c


KERNEL_UPTO = "out"


def kernel(L0_attn_x, L0_attn_Wq, L0_attn_Wk, L0_attn_Wv, L0_attn_Wo, L0_attn_Wg1,
           L0_attn_Wg2, L0_attn_ln_w, L0_attn_ln_b, L0_ffn_weight, L0_ffn_bias,
           L1_attn_qkv_w, L1_attn_o_w, L1_ffn_weight, L1_ffn_aux_weight1,
           L1_ffn_aux_weight2, L1_ffn_bias):
    xf = np.ascontiguousarray(np.asarray(L0_attn_x, dtype=np.float32))
    consts = host_constants()

    def v2d(v):
        return np.ascontiguousarray(np.asarray(v, np.float32).reshape(NC_F, 128).T)


    shared = {
        "consts": consts,
        "wq": np.ascontiguousarray(L0_attn_Wq, np.float32),
        "wk": np.ascontiguousarray(L0_attn_Wk, np.float32),
        "wv": np.ascontiguousarray(L0_attn_Wv, np.float32),
        "wo": np.ascontiguousarray(L0_attn_Wo, np.float32),
        "wg1": np.ascontiguousarray(L0_attn_Wg1, np.float32),
        "wg2": np.ascontiguousarray(L0_attn_Wg2, np.float32),
        "lnw2d": v2d(L0_attn_ln_w), "lnb2d": v2d(L0_attn_ln_b),
        "f0wT": np.ascontiguousarray(np.asarray(L0_ffn_weight, np.float32).T),
        "f0b2d": v2d(L0_ffn_bias),
        "qkvT": np.ascontiguousarray(np.asarray(L1_attn_qkv_w, np.float32).T),
        "oT": np.ascontiguousarray(np.asarray(L1_attn_o_w, np.float32).T),
        "f1w": np.ascontiguousarray(L1_ffn_weight, np.float32),
        "f1a1": np.ascontiguousarray(L1_ffn_aux_weight1, np.float32),
        "f1a2": np.ascontiguousarray(L1_ffn_aux_weight2, np.float32),
        "f1b2d": v2d(L1_ffn_bias),
    }
    nc = build_nc(upto=KERNEL_UPTO)
    in_maps = []
    for c in range(NCORES):
        m = dict(shared)
        m["x_in"] = np.ascontiguousarray(
            xf[c * SEQ_PER_CORE:(c + 1) * SEQ_PER_CORE].reshape(T, D))
        in_maps.append(m)
    res = run_bass_kernel_spmd(nc, in_maps, core_ids=list(range(NCORES)))
    outs = [res.results[c]["out"].reshape(SEQ_PER_CORE, S, D) for c in range(NCORES)]
    return np.concatenate(outs, axis=0).astype(np.float32)


if __name__ == "__main__":
    rng = np.random.default_rng(0)
    args = {}
    import inspect
    sig = inspect.signature(kernel)
    for name in sig.parameters:
        shp = {"L0_attn_x": (B, S, D), "L0_attn_Wq": (D, D), "L0_attn_Wk": (D, D),
               "L0_attn_Wv": (D, D), "L0_attn_Wo": (D, D), "L0_attn_Wg1": (D, HD),
               "L0_attn_Wg2": (HD, D), "L0_attn_ln_w": (D,), "L0_attn_ln_b": (D,),
               "L0_ffn_weight": (D, D), "L0_ffn_bias": (D,), "L1_attn_qkv_w": (3 * D, D),
               "L1_attn_o_w": (D, D), "L1_ffn_weight": (D, D), "L1_ffn_aux_weight1": (D, D),
               "L1_ffn_aux_weight2": (D, D), "L1_ffn_bias": (D,)}[name]
        args[name] = rng.standard_normal(shp).astype(np.float32) * 0.1
    o = kernel(**args)
    print("out", o.shape, o.dtype, np.abs(o - args["L0_attn_x"]).max())


# version 14 (final)
# speedup vs baseline: 1.0436x; 1.0436x over previous
import math
import numpy as np
import concourse.bass as bass
import concourse.mybir as mybir
from concourse.bass_utils import run_bass_kernel_spmd

F32 = mybir.dt.float32
BF16 = mybir.dt.bfloat16
FP8 = mybir.dt.float8e4
AF = mybir.ActivationFunctionType
ALU = mybir.AluOpType

B, S, D = 16, 2048, 1024
NCORES = 8
SEQ_PER_CORE = B // NCORES
T = SEQ_PER_CORE * S
NC_F = D // 128
H, HD = 16, 64
NPAIR = H // 2
TILE = 512
NTILE = T // TILE
CH = 128

ENGS = ("tensor", "scalar", "vector", "gpsimd", "sync")
DEV_TAPS = {}


class Sched:

    def __init__(self):
        self.ops = {e: [] for e in ENGS}
        self.cnt = {}
        self.last_wait = {e: {} for e in ENGS}

    def sem(self, name):
        self.cnt.setdefault(name, 0)

    def emit(self, eng, fn=None, wait=(), inc=None):
        w = []
        for ev in wait:
            if ev is None:
                continue
            s, v = ev
            if v is None:
                v = self.cnt.get(s, 0)
            if v <= 0:
                continue
            assert v <= self.cnt.get(s, 0), f"wait on future count {s}={v} (cur {self.cnt.get(s,0)})"
            if self.last_wait[eng].get(s, -1) >= v:
                continue
            self.last_wait[eng][s] = v
            w.append((s, v))
        ev_out = None
        if inc is not None:
            s, a = inc
            self.cnt[s] = self.cnt.get(s, 0) + a
            ev_out = (s, self.cnt[s])
        self.ops[eng].append((fn, w, inc))
        return ev_out

    def t(self, fn, wait=(), inc=True):
        return self.emit("tensor", fn, wait, ("sT", 1) if inc else None)

    def s(self, fn, wait=(), inc=True):
        w = list(wait) + [("sS", self.cnt.get("sS", 0))]
        return self.emit("scalar", fn, w, ("sS", 1) if inc else None)

    def v(self, fn, wait=(), inc=True):
        w = list(wait) + [("sV", self.cnt.get("sV", 0))]
        return self.emit("vector", fn, w, ("sV", 1) if inc else None)

    def g(self, fn, wait=(), inc=True):
        w = list(wait) + [("sG", self.cnt.get("sG", 0))]
        return self.emit("gpsimd", fn, w, ("sG", 1) if inc else None)

    def dma(self, eng, fn, sem, wait=()):
        return self.emit(eng, fn, wait, (sem, 16))


class Bank:
    __slots__ = ("ap", "name", "ready", "free")

    def __init__(self, ap, name):
        self.ap = ap
        self.name = name
        self.ready = None
        self.free = None


class Alloc:

    def __init__(self, total_elems, name):
        self.total = total_elems
        self.name = name
        self.cursor = 0
        self.marks = {}

    def take(self, n, align=8):
        c = (self.cursor + align - 1) // align * align
        assert c + n <= self.total, f"{self.name} arena overflow: {c}+{n} > {self.total}"
        self.cursor = c + n
        return c

    def mark(self, key):
        self.marks[key] = self.cursor

    def reset(self, key):
        self.cursor = self.marks[key]


def build_nc(dev_taps=(), upto="out"):
    nc = bass.Bass("TRN2", target_bir_lowering=False)
    sc = Sched()
    for s_ in ("sT", "sS", "sV", "sG", "sD0", "sD1", "sD2", "sD3", "sD4", "sX0", "sX1"):
        sc.sem(s_)

    x_in = nc.dram_tensor("x_in", [T, D], F32, kind="ExternalInput").ap()
    consts = nc.dram_tensor("consts", [128, CONST_W], F32, kind="ExternalInput").ap()
    w_in = {}
    for name, shp in WEIGHT_SHAPES.items():
        w_in[name] = nc.dram_tensor(name, shp, F32, kind="ExternalInput").ap()
    out = nc.dram_tensor("out", [T, D], F32, kind="ExternalOutput").ap()
    taps = {}
    for tname, tshape in DEV_TAPS.items():
        taps[tname] = nc.dram_tensor("tap_" + tname, list(tshape), F32, kind="ExternalOutput").ap()

    import contextlib
    with contextlib.ExitStack() as st_:
        e = st_.enter_context
        x_fm = e(nc.sbuf_tensor("x_fm", [128, NC_F * T], BF16))
        ctab = e(nc.sbuf_tensor("ctab", [128, 4096], BF16))
        cbf = e(nc.sbuf_tensor("cbf", [128, 1024], BF16))
        rows = e(nc.sbuf_tensor("rows", [128, 2304], F32))
        cols = e(nc.sbuf_tensor("cols", [128, 64], F32))
        stg = e(nc.sbuf_tensor("stg", [128, 2 * 1024], F32))
        bfa = e(nc.sbuf_tensor("bfa", [128, BFA_N], BF16))
        f8a = e(nc.sbuf_tensor("f8a", [128, F8_N], FP8))
        f32a = e(nc.sbuf_tensor("f32a", [128, F32A_N], F32))
        banks = [Bank(e(nc.psum_tensor(f"pb{i}", [128, 512], F32)), f"pb{i}") for i in range(7)]
        tbank = Bank(e(nc.psum_tensor("tb", [128, 1024], BF16)), "tb")
        sems = {}
        for s_ in ("sT", "sS", "sV", "sG", "sD0", "sD1", "sD2", "sD3", "sD4", "sX0", "sX1"):
            sems[s_] = e(nc.semaphore(s_))
        block = e(nc.Block())

        build_program(nc, sc, locals(), upto=upto)
        build_nc.last_sc = sc

        for engname in ENGS:
            section = getattr(block, engname)

            def _body(eng, _ops=sc.ops[engname]):
                for fn, waits, inc in _ops:
                    for s_, v in waits:
                        eng.wait_ge(sems[s_], v)
                    instr = fn(eng) if fn is not None else None
                    if inc is not None:
                        if instr is not None:
                            instr.then_inc(sems[inc[0]], inc[1])
                        else:
                            eng.nop().then_inc(sems[inc[0]], inc[1])

            section(_body)
    return nc


CONST_W = 4800

WEIGHT_SHAPES = {
    "wq": [D, D], "wk": [D, D], "wv": [D, D], "wo": [D, D],
    "wg1": [D, HD], "wg2": [HD, D],
    "lnw2d": [128, NC_F], "lnb2d": [128, NC_F],
    "f0wT": [D, D], "f0b2d": [128, NC_F],
    "qkvT": [D, 3 * D], "oT": [D, D],
    "f1w": [D, D], "f1a1": [D, D], "f1a2": [D, D], "f1b2d": [128, NC_F],
}

BFA_N = 49408
F8_N = 12 * 1024
F32A_N = 2 * 1024


def build_program(nc, sc, env, upto="out"):
    x_in, out = env["x_in"], env["out"]
    x_fm, ctab, cbf = env["x_fm"], env["ctab"], env["cbf"]
    stg, banks, rows = env["stg"], env["banks"], env["rows"]
    f32a, consts = env["f32a"], env["consts"]
    bfa, f8a, cols, w_in = env["bfa"], env["f8a"], env["cols"], env["w_in"]
    tbank = env["tbank"]

    S_ps = banks[6]
    avail = list(banks[0:6])
    held = []

    def take():
        b = avail.pop(0)
        held.append(b)
        return b

    def rel(b):
        held.remove(b)
        avail.append(b)

    OFF_WQ, OFF_WK, OFF_WV, OFF_WOW = 0, 8192, 16384, 24576
    OFF_U2, OFF_F0B2, OFF_WG1, OFF_WG2 = 32768, 33792, 34816, 35328
    OFF_QT, OFF_KT, OFF_KTM, OFF_G1 = 36352, 36864, 37376, 37888
    OFF_ATTD, OFF_SSB, OFF_VT, OFF_AO = 38400, 38912, 39424, 43520
    OFF_RTMP, OFF_SQ, OFF_ST2 = 47616, 48128, 48640
    SEG_A, SEG_B, SEG_C, SEG_D = 0, 512, 1024, 1536

    def bsl(off, n, p0=0, p1=128):
        return bfa[p0:p1, off:off + n]

    taps = env["taps"]
    tap_stage = env["f32a"]

    def tap(tag, ap, ev):
        if tag not in taps:
            return
        dst = taps[tag]
        n = dst.shape[1]
        p = dst.shape[0]
        cv = sc.v(lambda e, ap=ap, p=p, n=n: e.tensor_copy(tap_stage[0:p, 0:n], ap),
                  wait=[ev, tap.last])
        tap.last = sc.dma("gpsimd", lambda e, dst=dst, p=p, n=n: e.dma_start(
            out=dst, in_=tap_stage[0:p, 0:n]), "sD2", wait=[cv])
    tap.last = None

    ev_tab = None
    for piece in range(4):
        sl = stg[:, (piece % 2) * 1024:(piece % 2) * 1024 + 1024]
        ld = sc.dma("sync", lambda e, p=piece, sl=sl: e.dma_start(out=sl, in_=consts[:, p * 1024:(p + 1) * 1024]),
                    "sD0" if piece % 2 == 0 else "sD1", wait=[ev_tab] if piece >= 2 else [])
        ev_tab = sc.v(lambda e, p=piece, sl=sl: e.tensor_copy(ctab[:, p * 1024:(p + 1) * 1024], sl), wait=[ld])
    sl = stg[:, 0:704]
    ld_m = sc.dma("sync", lambda e, sl=sl: e.dma_start(out=sl, in_=consts[:, 4096:4800]), "sD0", wait=[ev_tab])
    ev_cbf = sc.v(lambda e, sl=sl: e.tensor_copy(cbf[:, 0:704], sl), wait=[ld_m])
    ev_idf = sc.v(lambda e, sl=sl: e.tensor_copy(rows[:, 2048:2176], sl[:, 0:128]), wait=[ld_m])
    ld_consts = ev_idf
    idt_bf = cbf[:, 0:128]
    ident_f32 = rows[:, 2048:2176]
    triu2 = cbf[:, 128:384]
    p2m = cbf[:, 384:512]
    onescol = cbf[:, 512:513]
    bc2 = cbf[0:2, 520:648]
    bc2_32 = cbf[32:34, 520:648]
    mz1 = sc.v(lambda e: e.memset(rows[0:64, 0:2048], 0.0))
    mz2 = sc.v(lambda e: e.memset(rows[0:64, 2176:2304], 0.0), wait=[mz1])
    mz3 = sc.v(lambda e: e.memset(rows[0:1, 2176:2304], 1.0), wait=[mz2])
    mz4 = sc.v(lambda e: e.memset(bfa[0:64, OFF_ST2:OFF_ST2 + 512], 0.0), wait=[mz3])
    ev_ones = sc.v(lambda e: e.memset(bfa[0:64, OFF_U2:OFF_U2 + 1024], 0.0), wait=[mz4])
    bc2f = rows[0:2, 2176:2304]
    st2bf = bfa[0:2, OFF_ST2:OFF_ST2 + 512]

    n_tc = T // 128
    stg_ev = [ld_consts, ld_consts]
    x_ready = []
    for c in range(n_tc):
        slot = c % 2
        stg_ap = stg[:, slot * 1024:(slot + 1) * 1024]
        ld = sc.dma("gpsimd", lambda e, c=c, a=stg_ap: e.dma_start(out=a, in_=x_in[c * 128:(c + 1) * 128, :]),
                    "sX0" if slot == 0 else "sX1", wait=[stg_ev[slot]])
        ev = None
        for half in range(2):
            bk = take()
            tp_ev = None
            for j in range(4):
                fc = half * 4 + j
                tp_ev = sc.t(lambda e, a=stg_ap, fc=fc, bk=bk, j=j:
                             e.transpose(bk.ap[:, j * 128:(j + 1) * 128],
                                         a[:, fc * 128:(fc + 1) * 128], ident_f32),
                             wait=[ld, ld_consts, bk.free] if j == 0 else [])
            bk.ready = tp_ev

            def ev_fn(e, bk=bk, c=c, half=half):
                dst = x_fm[:, :].rearrange("p (f t) -> p f t", f=NC_F)
                src = bk.ap[:, 0:512].rearrange("p (f t) -> p f t", f=4)
                return e.copy(dst[:, half * 4:(half + 1) * 4, c * 128:(c + 1) * 128], src)
            ev = sc.s(ev_fn, wait=[bk.ready])
            bk.free = ev
            rel(bk)
        stg_ev[slot] = ev
        x_ready.append(ev)

    wev = {}

    def staged(dram_sl, width, cast_fn, prows=128):
        slot = staged.rot[0] % 2
        staged.rot[0] += 1
        sl = stg[0:prows, slot * 1024: slot * 1024 + width]
        ld = sc.dma("sync", lambda e, sl=sl, d=dram_sl: e.dma_start(out=sl, in_=d),
                    "sD0" if slot == 0 else "sD1", wait=[stg_ev[slot]])
        ev = cast_fn(sl, ld)
        stg_ev[slot] = ev
        return ev
    staged.rot = [0]

    def load_plain_bf(name, off, width=1024):
        for ck in range(8):
            wev[name] = staged(
                w_in[name][ck * 128:(ck + 1) * 128, 0:width], width,
                lambda sl, ld, ck=ck: sc.v(
                    lambda e, sl=sl: e.tensor_copy(bsl(off + ck * width, width), sl), wait=[ld]))

    def load_weights_l0():
        sc.dma("sync", lambda e: e.dma_start(out=cols[:, 0:8], in_=w_in["lnw2d"]), "sD4")
        sc.dma("sync", lambda e: e.dma_start(out=cols[:, 8:16], in_=w_in["lnb2d"]), "sD4")
        sc.dma("sync", lambda e: e.dma_start(out=cols[:, 16:24], in_=w_in["f1b2d"]), "sD4")
        load_plain_bf("wq", OFF_WQ)
        load_plain_bf("wk", OFF_WK)
        load_plain_bf("wv", OFF_WV)
        for ck in range(8):
            wev["wg1"] = staged(
                w_in["wg1"][ck * 128:(ck + 1) * 128, :], 64,
                lambda sl, ld, ck=ck: sc.v(
                    lambda e, sl=sl: e.tensor_copy(bsl(OFF_WG1 + ck * 64, 64), sl), wait=[ld]))
        wev["wg2"] = staged(
            w_in["wg2"][:, :], 1024,
            lambda sl, ld: sc.v(lambda e, sl=sl: e.tensor_copy(bfa[0:64, OFF_WG2:OFF_WG2 + 1024], sl),
                                wait=[ld]), prows=64)
        for ck in range(8):
            wev["w8"] = staged(
                w_in["f0wT"][ck * 128:(ck + 1) * 128, :], 1024,
                lambda sl, ld, ck=ck: sc.v(
                    lambda e, sl=sl: e.tensor_copy(f8a[:, ck * 1024:(ck + 1) * 1024], sl), wait=[ld]))
        sc.dma("sync", lambda e: e.dma_start(out=cols[:, 40:48], in_=w_in["f0b2d"]), "sD4")
        colev = ("sD4", sc.cnt["sD4"])
        wev["lnw"] = colev
        wev["lnb"] = colev
        wev["f1b"] = colev
        wev["f0b2"] = colev
        bank_u = take()
        bank_c = take()
        for ck in range(8):
            def cast_wo(sl, ld, ck=ck):
                for half in range(2):
                    sc.t(lambda e, sl=sl, h=half, ck=ck: e.matmul(
                        bank_u.ap[(0 if h == 0 else 32):(1 if h == 0 else 33), 0:512],
                        cols[:, ck:ck + 1], sl[:, h * 512:(h + 1) * 512],
                        start=(ck == 0), stop=(ck == 7)),
                        wait=(([wev["lnw"], wev["lnb"], bank_u.free, bank_c.free] if ck == 0 else [])
                              + ([ld] if half == 0 else [])))
                for dc in range(8):
                    sc.t(lambda e, sl=sl, dc=dc, ck=ck: e.matmul(
                        bank_c.ap[:, dc:dc + 1], sl[:, dc * 128:(dc + 1) * 128],
                        cols[:, 8 + ck:9 + ck], start=(ck == 0 and dc == 0),
                        stop=(ck == 7 and dc == 7)))
                return sc.v(lambda e, sl=sl, ck=ck: e.tensor_scalar(
                    bsl(OFF_WOW + ck * 1024, 1024), sl, cols[:, ck:ck + 1], None, op0=ALU.mult),
                    wait=[wev["lnw"], (("sT", sc.cnt["sT"]))])
            wev["wo"] = staged(w_in["wo"][ck * 128:(ck + 1) * 128, :], 1024, cast_wo)
        bank_u.ready = (("sT", sc.cnt["sT"]))
        sc.v(lambda e: e.tensor_copy(bsl(OFF_U2, 512, 0, 1), bank_u.ap[0:1, 0:512]),
             wait=[bank_u.ready, ev_ones])
        sc.v(lambda e: e.tensor_copy(bsl(OFF_U2 + 512, 512, 0, 1), bank_u.ap[32:33, 0:512]))
        e4 = sc.v(lambda e: e.tensor_copy(cols[:, 32:40], bank_c.ap[:, 0:8]))
        bank_u.free = e4
        bank_c.free = e4
        rel(bank_u)
        rel(bank_c)
        wev["u2"] = e4


    sq_cell = [None]
    mur_bf_cell = [None]
    rows_guard = [None]

    def ln_stats(bank_s):
        sc.v(lambda e, bk=bank_s: e.tensor_scalar(
            rows[0:1, SEG_B:SEG_B + 512], bk.ap[0:1, 0:512], 1.0 / 1024.0, None,
            op0=ALU.mult), wait=[bank_s.ready, ev_ones, rows_guard[0]])
        sc.v(lambda e, bk=bank_s: e.tensor_scalar(
            rows[0:1, SEG_C:SEG_C + 512], bk.ap[32:33, 0:512], 1.0 / 1024.0, None,
            op0=ALU.mult))
        sc.v(lambda e: e.tensor_tensor(rows[0:1, SEG_D:SEG_D + 512],
                                       rows[0:1, SEG_B:SEG_B + 512],
                                       rows[0:1, SEG_B:SEG_B + 512], op=ALU.mult))
        vv = sc.v(lambda e: e.scalar_tensor_tensor(rows[0:1, SEG_A:SEG_A + 512],
                                                   rows[0:1, SEG_C:SEG_C + 512], 1e-5,
                                                   rows[0:1, SEG_D:SEG_D + 512],
                                                   op0=ALU.add, op1=ALU.subtract))
        ev5 = sc.s(lambda e: e.activation(rows[0:1, SEG_D:SEG_D + 512],
                                          rows[0:1, SEG_A:SEG_A + 512], AF.Sqrt),
                   wait=[vv])
        r_ev = sc.v(lambda e: e.reciprocal(rows[0:1, SEG_A:SEG_A + 512],
                                           rows[0:1, SEG_D:SEG_D + 512]), wait=[ev5])
        mur_ev = sc.v(lambda e: e.scalar_tensor_tensor(
            rows[0:1, SEG_C:SEG_C + 512], rows[0:1, SEG_B:SEG_B + 512], -1.0,
            rows[0:1, SEG_A:SEG_A + 512], op0=ALU.mult, op1=ALU.mult))
        mur_bf_cell[0] = sc.v(lambda e: e.tensor_copy(
            bfa[0:1, OFF_ST2:OFF_ST2 + 512], rows[0:1, SEG_C:SEG_C + 512]), wait=[ev_ones])
        r_bf = sc.v(lambda e: e.tensor_copy(
            bfa[32:33, OFF_ST2:OFF_ST2 + 512], rows[0:1, SEG_A:SEG_A + 512]))
        bank_s.free = r_bf
        return r_bf, mur_bf_cell[0]

    def ln_sums(src_fn, rd_ev):
        bank_s = take()
        for fc in range(NC_F):
            sc.t(lambda e, bk=bank_s, fc=fc, ap=src_fn(fc): e.matmul(
                bk.ap[0:1, 0:512], onescol, ap,
                start=(fc == 0), stop=(fc == 7)),
                wait=([bank_s.free, rd_ev(fc)] if fc == 0 else [rd_ev(fc)]))
        smm = None
        for fc in range(NC_F):
            sqw = [rd_ev(fc)]
            if smm is not None:
                sqw.append(smm)
            elif sq_cell[0] is not None:
                sqw.append(sq_cell[0])
            sqe = sc.v(lambda e, fc=fc, ap=src_fn(fc): e.tensor_tensor(
                bsl(OFF_SQ, 512), ap, ap, op=ALU.mult), wait=sqw)
            smm = sc.t(lambda e, bk=bank_s, fc=fc: e.matmul(
                bk.ap[32:33, 0:512], onescol, bsl(OFF_SQ, 512),
                start=(fc == 0), stop=(fc == 7)), wait=[sqe])
        bank_s.ready = smm
        sq_cell[0] = smm
        return bank_s

    def ln_norm(src_fn, dst_fn, s_evs, filler=None):
        bank_s = ln_sums(src_fn, lambda fc: s_evs[fc])
        r_ev, mur_ev = ln_stats(bank_s)
        rel(bank_s)
        if filler is not None:
            filler()
        bb = take()
        rb = sc.t(lambda e, bk=bb: e.matmul(
            bk.ap[:, 0:512], bc2_32, bfa[32:34, OFF_ST2:OFF_ST2 + 512], start=True, stop=True),
            wait=[bb.free, r_ev, ld_consts])
        bb.ready = rb
        bcx = take()
        cbx = sc.t(lambda e, bk=bcx: e.matmul(
            bk.ap[:, 0:512], bc2, bfa[0:2, OFF_ST2:OFF_ST2 + 512], start=True, stop=True),
            wait=[bcx.free, mur_ev])
        bcx.ready = cbx
        outev = [None] * NC_F
        for fc in range(NC_F):
            sc.v(lambda e, fc=fc, bk=bb, ap=src_fn(fc): e.tensor_tensor(
                ap, ap, bk.ap[:, 0:512], op=ALU.mult), wait=[bb.ready, sq_cell[0]])
            outev[fc] = sc.v(lambda e, fc=fc, bk=bcx, ap=src_fn(fc), d=dst_fn(fc): e.tensor_tensor(
                d, ap, bk.ap[:, 0:512], op=ALU.add), wait=[bcx.ready])
        bb.free = outev[-1]
        bcx.free = outev[-1]
        rel(bb)
        rel(bcx)
        return outev

    def l0_layer():
        import os as _os
        NTILE_RUN = int(_os.environ.get("DEV_NTILE", NTILE))
        LVL = int(_os.environ.get("DEV_LVL", "99"))
        l0_layer.last_sevac = None
        l0_layer.s_free_dc = None
        l0_layer.pre = {}
        l0_layer.qk_free = [None, None]
        l0_layer.ktm_free = None
        s_evac_ev = [None] * NPAIR
        xq_free = None
        vt_free = None
        ao_free = None
        attd_free = [None, None]
        last_x_ev = None
        for t in range(NTILE_RUN):
            ts = t % 4
            first_of_seq = (ts == 0)

            def xsl(fc, t=t):
                return x_fm[:, fc * T + t * TILE: fc * T + (t + 1) * TILE]

            ev_xt = x_ready[4 * t + 3]

            v_ev = None
            for c4 in range(4):
                for j in range(2):
                    bk = take()
                    vtw = [l0_layer.s_free_dc[2 * c4], l0_layer.s_free_dc[2 * c4 + 1]] \
                        if l0_layer.s_free_dc is not None else []
                    last = None
                    for ck in range(8):
                        last = sc.t(lambda e, bk=bk, ck=ck, c4=c4, j=j, t=t: e.matmul(
                            bk.ap[:, 0:512],
                            x_fm[:, ck * T + (4 * t + c4) * 128: ck * T + (4 * t + c4) * 128 + 128],
                            bsl(OFF_WV + ck * 1024 + j * 512, 512),
                            start=(ck == 0), stop=(ck == 7)),
                            wait=([bk.free, x_ready[4 * t + c4], wev["wv"]] + vtw
                                  if ck == 0 else []))
                    bk.ready = last
                    v_ev = sc.s(lambda e, bk=bk, c4=c4, j=j: e.copy(
                        bsl(OFF_VT + c4 * 1024 + j * 512, 512), bk.ap[:, 0:512]), wait=[bk.ready])
                    bk.free = v_ev
                    rel(bk)

            if LVL < 2:
                continue
            def emit_g1(xsl_, ev_xt_):
                bk = take()
                last = None
                for ck in range(8):
                    last = sc.t(lambda e, bk=bk, ck=ck, xs=xsl_(ck): e.matmul(
                        bk.ap[0:64, 0:512], bsl(OFF_WG1 + ck * 64, 64), xs,
                        start=(ck == 0), stop=(ck == 7)),
                        wait=([bk.free, ev_xt_, wev["wg1"]] if ck == 0 else []))
                bk.ready = last
                ev = sc.s(lambda e, bk=bk: e.copy(bsl(OFF_G1, 512, 0, 64), bk.ap[0:64, 0:512]),
                          wait=[bk.ready])
                bk.free = ev
                rel(bk)
                return ev

            g1_ev = l0_layer.pre.pop("g1_ev", None)
            if g1_ev is None:
                g1_ev = emit_g1(xsl, ev_xt)

            def emit_g2(g1e, aof):
                gev = [None] * NC_F
                for fc in range(NC_F):
                    bk = take()
                    mv = sc.t(lambda e, bk=bk, fc=fc: e.matmul(
                        bk.ap[:, 0:512], bfa[0:64, OFF_WG2 + fc * 128: OFF_WG2 + (fc + 1) * 128],
                        bsl(OFF_G1, 512, 0, 64), start=True, stop=True),
                        wait=[bk.free, g1e, wev["wg2"], aof])
                    bk.ready = mv
                    gev[fc] = sc.s(lambda e, bk=bk, fc=fc: e.activation(
                        bsl(OFF_AO + fc * 512, 512), bk.ap[:, 0:512], AF.Sigmoid),
                        wait=[bk.ready])
                    bk.free = gev[fc]
                    rel(bk)
                return gev

            gate_ev = l0_layer.pre.pop("gate_ev", None)
            if gate_ev is None:
                gate_ev = emit_g2(g1_ev, ao_free)
                if t == 0:
                    tap("gate0", bsl(OFF_AO, 512), gate_ev[0])

            if LVL < 3:
                continue
            def QB(p, off, n, p0=0, p1=128):
                base = OFF_QT if p % 2 == 0 else OFF_F0B2
                return bfa[p0:p1, base + off: base + off + n]

            def KB(p, off, n, p0=0, p1=128):
                base = OFF_KT if p % 2 == 0 else OFF_F0B2 + 512
                return bfa[p0:p1, base + off: base + off + n]

            po_ev = [None] * NPAIR
            silu_ev = l0_layer.pre.pop("silu_ev", {})
            rope_ev = l0_layer.pre.pop("rope_ev", {})
            ktm_ev_d = l0_layer.pre.pop("ktm_ev_d", {})

            def proj_mm(p, which, xsl_=None, ev_xt_=None):
                if xsl_ is None:
                    xsl_, ev_xt_ = xsl, ev_xt
                woff = OFF_WQ if which == "q" else OFF_WK
                wname = "wq" if which == "q" else "wk"
                BUFF = QB if which == "q" else KB
                bk = take()
                last = None
                for ck in range(8):
                    last = sc.t(lambda e, bk=bk, ck=ck, woff=woff, p=p, xs=xsl_(ck): e.matmul(
                        bk.ap[:, 0:512], bsl(woff + ck * 1024 + p * 128, 128), xs,
                        start=(ck == 0), stop=(ck == 7)),
                        wait=([bk.free, ev_xt_, wev[wname], l0_layer.qk_free[p % 2]]
                              if ck == 0 else []))
                bk.ready = last
                ev = sc.s(lambda e, bk=bk, d=BUFF(p, 0, 512): e.activation(
                    d, bk.ap[:, 0:512], AF.Silu), wait=[bk.ready])
                bk.free = ev
                rel(bk)
                silu_ev[(p, which)] = ev

            def proj_rope(p, which, ts_=None):
                if ts_ is None:
                    ts_ = ts
                BUFF = QB if which == "q" else KB
                ev = silu_ev[(p, which)]
                bk2 = take()
                rot = sc.t(lambda e, bk=bk2, srcb=BUFF(p, 0, 512): e.matmul(
                    bk.ap[:, 0:512], p2m, srcb, start=True, stop=True),
                    wait=[bk2.free, ev, ld_consts])
                bk2.ready = rot
                sc.v(lambda e, srcb=BUFF(p, 0, 512), ts=ts_: e.tensor_tensor(
                    bsl(OFF_RTMP, 512), srcb, ctab[:, ts * 512:(ts + 1) * 512],
                    op=ALU.mult), wait=[ev])
                sc.v(lambda e, bk=bk2, d=BUFF(p, 0, 512), ts=ts_: e.tensor_tensor(
                    d, bk.ap[:, 0:512], ctab[:, 2048 + ts * 512: 2048 + (ts + 1) * 512],
                    op=ALU.mult), wait=[bk2.ready])
                dn = sc.v(lambda e, d=BUFF(p, 0, 512): e.tensor_tensor(
                    d, d, bsl(OFF_RTMP, 512), op=ALU.add))
                bk2.free = dn
                rel(bk2)
                rope_ev[(p, which)] = dn
                if t == 0 and p == 0 and which == "q":
                    tap("q_silu", QB(0, 0, 512), ev)
                    tap("q_rope", QB(0, 0, 512), dn)

            def l0_proj(p, which, xsl_=None, ev_xt_=None, ts_=None):
                proj_mm(p, which, xsl_, ev_xt_)
                proj_rope(p, which, ts_)

            def l0_ktm(p):
                tp = None
                for c4 in range(4):
                    tp = sc.t(lambda e, c4=c4, srcb=KB(p, c4 * 128, 128): e.transpose(
                        tbank.ap[:, c4 * 128:(c4 + 1) * 128], srcb, idt_bf),
                        wait=([rope_ev[(p, "k")], tbank.free, l0_layer.ktm_free]
                              if c4 == 0 else []))
                tbank.ready = tp
                ev = sc.s(lambda e: e.copy(bsl(OFF_KTM, 512), tbank.ap[:, 0:512]),
                          wait=[tbank.ready])
                tbank.free = ev
                ktm_ev_d[p] = ev

            def l0_attn_chunk(p, c4, po, po2, filler=None):
                cs = ts * 4 + c4
                bdA = take()
                bdB = take()
                mA = sc.t(lambda e, bd=bdA, kq=KB(p, c4 * 128, 128, 0, 64),
                          qq=QB(p, c4 * 128, 128, 0, 64): e.matmul(
                    bd.ap[:, 0:128], kq, qq, start=True, stop=True),
                    wait=[bdA.free, bdB.free, rope_ev[(p, "q")], rope_ev[(p, "k")]])
                mB = sc.t(lambda e, bd=bdB, kq=KB(p, c4 * 128, 128, 64, 128),
                          qq=QB(p, c4 * 128, 128, 64, 128): e.matmul(
                    bd.ap[:, 0:128], kq, qq, start=True, stop=True,
                    tile_position=(64, 0), skip_group_check=True))
                bdA.ready = mA
                bdB.ready = mB
                par = c4 % 2
                sc.v(lambda e, bd=bdA, par=par: e.tensor_tensor(
                    bsl(OFF_ATTD + par * 256, 128), bd.ap[:, 0:128], triu2[:, 0:128],
                    op=ALU.mult), wait=[bdA.ready, bdB.ready, attd_free[par]])
                m_ev = sc.v(lambda e, bd=bdB, par=par: e.tensor_tensor(
                    bsl(OFF_ATTD + par * 256 + 128, 128), bd.ap[:, 0:128], triu2[:, 0:128],
                    op=ALU.mult))
                bdA.free = m_ev
                bdB.free = m_ev
                rel(bdA)
                rel(bdB)
                if t == 0 and p == 0 and c4 == 0:
                    tap("attd", bsl(OFF_ATTD, 256), m_ev)
                if filler is not None:
                    filler()
                last_mm = (c4 == 3)
                sc.t(lambda e, po=po, c4=c4, p=p, st=(last_mm and cs == 0): e.matmul(
                    po.ap[0:64, c4 * 128:(c4 + 1) * 128],
                    bfa[:, OFF_VT + c4 * 1024 + 2 * p * 64: OFF_VT + c4 * 1024 + 2 * p * 64 + 64],
                    bsl(OFF_ATTD + (c4 % 2) * 256, 128), start=(c4 == 0), stop=st),
                    wait=([po.free, po2.free, m_ev, v_ev] if c4 == 0 else [m_ev]))
                last_po = sc.t(lambda e, po=po2, c4=c4, p=p, st=(last_mm and cs == 0): e.matmul(
                    po.ap[64:128, c4 * 128:(c4 + 1) * 128],
                    bfa[:, OFF_VT + c4 * 1024 + (2 * p + 1) * 64: OFF_VT + c4 * 1024 + (2 * p + 1) * 64 + 64],
                    bsl(OFF_ATTD + (c4 % 2) * 256 + 128, 128), start=(c4 == 0), stop=st,
                    tile_position=(0, 64), skip_group_check=True))
                if cs > 0:
                    sc.t(lambda e, po=po, c4=c4, p=p, st=last_mm,
                         qq=QB(p, c4 * 128, 128, 0, 64): e.matmul(
                        po.ap[0:64, c4 * 128:(c4 + 1) * 128], bsl(OFF_SSB + p * 64, 64, 0, 64),
                        qq, start=False, stop=st),
                        wait=[s_evac_ev[p]] if s_evac_ev[p] else [])
                    last_po = sc.t(lambda e, po=po2, c4=c4, p=p, st=last_mm,
                                   qq=QB(p, c4 * 128, 128, 64, 128): e.matmul(
                        po.ap[64:128, c4 * 128:(c4 + 1) * 128], bsl(OFF_SSB + p * 64, 64, 64, 128),
                        qq, start=False, stop=st,
                        tile_position=(64, 64), skip_group_check=True))
                su_w = [ktm_ev_d[p], v_ev]
                if l0_layer.last_sevac is not None:
                    su_w.append(l0_layer.last_sevac)
                s_first = (first_of_seq and c4 == 0 and p == 0)
                sc.t(lambda e, c4=c4, p=p, st=s_first: e.matmul(
                    S_ps.ap[0:64, p * 64:(p + 1) * 64], bsl(OFF_KTM + c4 * 128, 64),
                    bfa[:, OFF_VT + c4 * 1024 + 2 * p * 64: OFF_VT + c4 * 1024 + 2 * p * 64 + 64],
                    start=st, stop=True, skip_group_check=True), wait=su_w)
                sB = sc.t(lambda e, c4=c4, p=p, st=s_first: e.matmul(
                    S_ps.ap[64:128, p * 64:(p + 1) * 64], bsl(OFF_KTM + c4 * 128 + 64, 64),
                    bfa[:, OFF_VT + c4 * 1024 + (2 * p + 1) * 64: OFF_VT + c4 * 1024 + (2 * p + 1) * 64 + 64],
                    start=st, stop=True, tile_position=(0, 64), skip_group_check=True))
                s_evac_ev[p] = sc.s(lambda e, p=p: e.copy(
                    bsl(OFF_SSB + p * 64, 64), S_ps.ap[:, p * 64:(p + 1) * 64]), wait=[sB])
                l0_layer.last_sevac = s_evac_ev[p]
                attd_free[par] = last_po
                if t == 0 and p == 0 and c4 == 0:
                    tap("s_sb", bsl(OFF_SSB, 64), s_evac_ev[p])

            if not l0_layer.pre.pop("proj0", False):
                l0_proj(0, "q")
                l0_proj(0, "k")
            if not l0_layer.pre.pop("ktm0", False):
                l0_ktm(0)
            for p in range(NPAIR):
                po = take()
                po2 = take()
                for c4 in range(4):
                    filler = None
                    if p + 1 < NPAIR:
                        if c4 == 0:
                            filler = (lambda pp=p + 1: proj_mm(pp, "q"))
                        elif c4 == 1:
                            filler = (lambda pp=p + 1: proj_rope(pp, "q"))
                        elif c4 == 2:
                            filler = (lambda pp=p + 1: proj_mm(pp, "k"))
                        elif c4 == 3:
                            filler = (lambda pp=p + 1: proj_rope(pp, "k"))
                    l0_attn_chunk(p, c4, po, po2, filler)
                po.ready = (("sT", sc.cnt["sT"]))
                sc.v(lambda e, po=po, p=p: e.tensor_tensor(
                    bsl(OFF_AO + p * 512, 512, 0, 64), po.ap[0:64, 0:512],
                    bsl(OFF_AO + p * 512, 512, 0, 64),
                    op=ALU.mult), wait=[po.ready, gate_ev[p]])
                po_ev[p] = sc.v(lambda e, po=po2, p=p: e.tensor_tensor(
                    bsl(OFF_AO + p * 512, 512, 64, 128), po.ap[64:128, 0:512],
                    bsl(OFF_AO + p * 512, 512, 64, 128),
                    op=ALU.mult))
                po.free = po_ev[p]
                po2.free = po_ev[p]
                rel(po)
                rel(po2)
                l0_layer.qk_free[p % 2] = (("sT", sc.cnt["sT"]))
                l0_layer.ktm_free = (("sT", sc.cnt["sT"]))
                if p + 1 < NPAIR:
                    l0_ktm(p + 1)
                if t == 0 and p == 0:
                    tap("ao_p0", bsl(OFF_AO, 512), po_ev[0])

            if LVL < 8:
                continue
            nx = (t + 1 < NTILE_RUN)
            if nx:
                t1 = t + 1

                def xsl_next(fc, t1=t1):
                    return x_fm[:, fc * T + t1 * TILE: fc * T + (t1 + 1) * TILE]

                ev_xt_next = x_ready[4 * t1 + 3]
                ts_next = t1 % 4
            bank_s = ln_sums(lambda fc: bsl(OFF_AO + fc * 512, 512), lambda fc: po_ev[fc])
            r_ev, mur_ev = ln_stats(bank_s)
            rel(bank_s)
            if nx:
                l0_layer.pre["g1_ev"] = emit_g1(xsl_next, ev_xt_next)
            bank_b = take()
            rb = sc.t(lambda e, bk=bank_b: e.matmul(
                bk.ap[:, 0:512], bc2_32, bfa[32:34, OFF_ST2:OFF_ST2 + 512], start=True, stop=True),
                wait=[bank_b.free, r_ev, ev_ones, ld_consts])
            bank_b.ready = rb
            if nx:
                l0_proj(0, "q", xsl_next, ev_xt_next, ts_next)
            m_ev = [None] * NC_F
            for fc in range(NC_F):
                m_ev[fc] = sc.v(lambda e, fc=fc, bk=bank_b: e.tensor_tensor(
                    bsl(OFF_AO + fc * 512, 512), bsl(OFF_AO + fc * 512, 512), bk.ap[:, 0:512],
                    op=ALU.mult), wait=[bank_b.ready, sq_cell[0]])
            bank_b.free = m_ev[-1]
            rel(bank_b)

            if LVL < 9:
                continue
            s_ev = [None] * NC_F
            for dc in range(NC_F):
                bk = take()
                for ck in range(NC_F):
                    sc.t(lambda e, bk=bk, ck=ck, dc=dc: e.matmul(
                        bk.ap[:, 0:512], bsl(OFF_WOW + ck * 1024 + dc * 128, 128),
                        bsl(OFF_AO + ck * 512, 512), start=(ck == 0), stop=False),
                        wait=([bk.free, wev["wo"], wev["u2"], m_ev[-1]] if ck == 0 else []))
                last = sc.t(lambda e, bk=bk, dc=dc: e.matmul(
                    bk.ap[:, 0:512], bfa[0:2, OFF_U2 + dc * 128: OFF_U2 + (dc + 1) * 128],
                    st2bf, start=False, stop=True),
                    wait=[mur_bf_cell[0], ev_ones])
                bk.ready = last
                s_ev[dc] = sc.v(lambda e, bk=bk, dc=dc, xs=xsl(dc): e.scalar_tensor_tensor(
                    bsl(OFF_VT + dc * 512, 512), bk.ap[:, 0:512], cols[:, 32 + dc:33 + dc],
                    xs, op0=ALU.add, op1=ALU.add),
                    wait=[bk.ready])
                bk.free = s_ev[dc]
                rel(bk)
            ao_free = (("sT", sc.cnt["sT"]))

            if LVL < 10:
                continue
            if t == 0:
                tap("s0", bsl(OFF_VT, 512), s_ev[0])
                tap("stats0", rows[0:1, 0:2048], s_ev[0])
            fill1 = (lambda: l0_proj(0, "k", xsl_next, ev_xt_next, ts_next)) if nx else None
            x1_ev = ln_norm(lambda fc: bsl(OFF_VT + fc * 512, 512), lambda fc: xsl(fc), s_ev,
                            filler=fill1)
            if upto == "x1":
                vt_free = x1_ev[-1]
                l0_layer.s_free_dc = x1_ev
                if nx:
                    l0_layer.pre["proj0"] = True
                    l0_layer.pre["silu_ev"] = silu_ev
                    l0_layer.pre["rope_ev"] = rope_ev
                    l0_layer.pre["ktm_ev_d"] = ktm_ev_d
                ao_free = x1_ev[-1]
                last_x_ev = x1_ev[-1]
                continue

            q_ev = [None] * NC_F
            for fc in range(NC_F):
                q_ev[fc] = sc.s(lambda e, fc=fc, xs=xsl(fc): e.copy(
                    f8a[:, 8192 + fc * 512: 8192 + (fc + 1) * 512], xs), wait=[x1_ev[fc]])
            s_ev2 = [None] * NC_F
            for dc in range(NC_F):
                bk = take()
                for ck in range(NC_F):
                    sc.t(lambda e, bk=bk, ck=ck, dc=dc: e.matmul(
                        bk.ap[:, 0:512], f8a[:, ck * 1024 + dc * 128: ck * 1024 + dc * 128 + 128],
                        f8a[:, 8192 + ck * 512: 8192 + (ck + 1) * 512],
                        start=(ck == 0), stop=(ck == NC_F - 1)),
                        wait=([bk.free, wev["w8"], q_ev[-1]] if ck == 0 else []))
                bk.ready = (("sT", sc.cnt["sT"]))
                s_ev2[dc] = sc.v(lambda e, bk=bk, dc=dc, xs=xsl(dc): e.scalar_tensor_tensor(
                    bsl(OFF_VT + dc * 512, 512), bk.ap[:, 0:512], cols[:, 40 + dc:41 + dc],
                    xs, op0=ALU.add, op1=ALU.add),
                    wait=[bk.ready, x1_ev[dc], wev["f0b2"]])
                bk.free = s_ev2[dc]
                rel(bk)
            if nx:
                def fill2(aof=ao_free):
                    l0_ktm(0)
                    l0_layer.pre["ktm0"] = True
                    l0_layer.pre["gate_ev"] = emit_g2(l0_layer.pre["g1_ev"], aof)
            else:
                fill2 = None
            x2_ev = ln_norm(lambda fc: bsl(OFF_VT + fc * 512, 512), lambda fc: xsl(fc), s_ev2,
                            filler=fill2)
            vt_free = x2_ev[-1]
            l0_layer.s_free_dc = x2_ev
            if nx:
                l0_layer.pre["proj0"] = True
                l0_layer.pre["silu_ev"] = silu_ev
                l0_layer.pre["rope_ev"] = rope_ev
                l0_layer.pre["ktm_ev_d"] = ktm_ev_d
            ao_free = x2_ev[-1]
            last_x_ev = x2_ev[-1]
        return last_x_ev


    L1X, L1QKV, L1Q, L1K, L1V, L1AO, L1U, L1RT = 0, 8192, 10240, 12288, 14336, 30976, 47360, 47872
    L1S = 24576

    def l1_layer(x2_done):
        import os as _os
        if S_ps in held:
            pass
        if S_ps not in avail:
            S_ps.free = x2_done
            avail.append(S_ps)
        l1_layer.pending_norm = None
        l1_layer.x_free = x2_done
        l1_layer.v_free = x2_done
        l1_layer.u_free = x2_done
        l1_layer.ao_free = x2_done
        l1_layer.s_free = x2_done
        neg3_ev = sc.v(lambda e: e.memset(cols[:, 50:51], -3.0))
        fz_ev = sc.v(lambda e: e.memset(bfa[0:64, 48384:49408], 0.0))
        fz2_ev = sc.v(lambda e: e.memset(env["f32a"][0:64, 0:1024], 0.0), wait=[fz_ev])
        f32a = env["f32a"]
        vv = bfa[:, L1V:L1V + 16640].rearrange("p (c h d) -> p c h d", c=16, h=16)
        x3_ev_all = []
        vproj_last_reader = None
        x_ev_o = {}
        for s_ in range(SEQ_PER_CORE):
            sbase = s_ * S
            ones_v_ev = sc.v(lambda e: e.memset(vv[:, :, :, 64:65], 1.0),
                             wait=[l1_layer.v_free])
            wv1_ev = None
            for ck in range(8):
                wv1_ev = staged(
                    w_in["qkvT"][ck * 128:(ck + 1) * 128, 2048:3072], 1024,
                    lambda sl, ld, ck=ck: sc.v(
                        lambda e, sl=sl: e.tensor_copy(bsl(L1X + ck * 1024, 1024), sl),
                        wait=[ld, l1_layer.x_free]))
            v_ev = None
            for tc in range(16):
                for j in range(2):
                    bk = take()
                    last = None
                    for ck in range(8):
                        last = sc.t(lambda e, bk=bk, ck=ck, tc=tc, j=j, sbase=sbase: e.matmul(
                            bk.ap[:, 0:512],
                            x_fm[:, ck * T + sbase + tc * 128: ck * T + sbase + tc * 128 + 128],
                            bsl(L1X + ck * 1024 + j * 512, 512),
                            start=(ck == 0), stop=(ck == 7)),
                            wait=([bk.free, wv1_ev, x2_done, l1_layer.v_free] if ck == 0 else []))
                    bk.ready = last
                    def vevac(e, bk=bk, tc=tc, j=j):
                        dst = vv[:, tc:tc + 1, 8 * j:8 * j + 8, 0:64]
                        srcv = bk.ap[:, 0:512].rearrange("p (o h d) -> p o h d", o=1, h=8)
                        return e.copy(dst, srcv)
                    v_ev = sc.s(vevac, wait=[bk.ready, ones_v_ev])
                    bk.free = v_ev
                    rel(bk)
            f8bf = f8a[:, 0:12288].bitcast(BF16)

            def BUF(p, which, off, n, p0=0, p1=128):
                if p % 2 == 0:
                    base = {"q": L1Q, "k": L1K, "w": L1QKV}[which]
                    return bfa[p0:p1, base + off: base + off + n]
                base = {"q": 0, "k": 2048, "w": 4096}[which]
                return f8bf[p0:p1, base + off: base + off + n]

            ao_ev = {}
            att_free = wv1_ev
            wfree = [x2_done, x2_done]
            qkfree = [x2_done, x2_done]
            wq_done = {}
            rope_done = {}

            def emit_load(p):
                ev = None
                for ck in range(8):
                    ev = staged(
                        w_in["qkvT"][ck * 128:(ck + 1) * 128, 128 * p:128 * (p + 1)], 128,
                        lambda sl, ld, ck=ck, p=p: sc.v(
                            lambda e, sl=sl, w=BUF(p, "w", ck * 128, 128): e.tensor_scalar(
                                w, sl, 0.125, None, op0=ALU.mult),
                            wait=[ld, wfree[p % 2]]))
                for ck in range(8):
                    ev = staged(
                        w_in["qkvT"][ck * 128:(ck + 1) * 128, 1024 + 128 * p:1024 + 128 * (p + 1)], 128,
                        lambda sl, ld, ck=ck, p=p: sc.v(
                            lambda e, sl=sl, w=BUF(p, "w", 1024 + ck * 128, 128): e.tensor_copy(w, sl),
                            wait=[ld]))
                wq_done[p] = ev

            def emit_proj(p, which, i):
                woff = 0 if which == "q" else 1024
                bk = take()
                last = None
                for ck in range(8):
                    last = sc.t(lambda e, bk=bk, w=BUF(p, "w", woff + ck * 128, 128), ck=ck,
                                i=i, sbase=sbase: e.matmul(
                        bk.ap[:, 0:512], w,
                        x_fm[:, ck * T + sbase + i * 512: ck * T + sbase + (i + 1) * 512],
                        start=(ck == 0), stop=(ck == 7)),
                        wait=([bk.free, wq_done[p], x2_done, qkfree[p % 2]] if ck == 0 else []))
                bk.ready = last
                pe = sc.v(lambda e, bk=bk, d=BUF(p, which, i * 512, 512): e.tensor_copy(
                    d, bk.ap[:, 0:512]), wait=[bk.ready])
                bk.free = pe
                rel(bk)
                bk2 = take()
                rot = sc.t(lambda e, bk=bk2, srcb=BUF(p, which, i * 512, 512): e.matmul(
                    bk.ap[:, 0:512], p2m, srcb, start=True, stop=True),
                    wait=[bk2.free, pe, ld_consts])
                bk2.ready = rot
                sc.v(lambda e, srcb=BUF(p, which, i * 512, 512), i=i: e.tensor_tensor(
                    bsl(L1RT, 512), srcb, ctab[:, i * 512:(i + 1) * 512], op=ALU.mult), wait=[pe])
                sc.v(lambda e, bk=bk2, d=BUF(p, which, i * 512, 512), i=i: e.tensor_tensor(
                    d, bk.ap[:, 0:512], ctab[:, 2048 + i * 512: 2048 + (i + 1) * 512],
                    op=ALU.mult), wait=[bk2.ready])
                dn = sc.v(lambda e, d=BUF(p, which, i * 512, 512): e.tensor_tensor(
                    d, d, bsl(L1RT, 512), op=ALU.add))
                bk2.free = dn
                rel(bk2)
                rope_done[(p, which, i)] = dn

            emit_load(0)
            for i in range(4):
                emit_proj(0, "q", i)
                emit_proj(0, "k", i)
            for p in range(NPAIR):
                if p + 1 < NPAIR:
                    emit_load(p + 1)
                k_done = rope_done[(p, "k", 3)]
                for i in range(4):
                    q_done = rope_done[(p, "q", i)]
                    poA = take()
                    poB = take()
                    avA = avB = None
                    for half in range(2):
                        if half == 1 and l1_layer.pending_norm is not None:
                            l1_layer.pending_norm()
                            l1_layer.pending_norm = None
                        exp_half = []
                        for kc8 in range(8):
                            kc = 8 * half + kc8
                            bkA = take()
                            bkB = take()
                            mA = sc.t(lambda e, bk=bkA, kq=BUF(p, "k", kc * 128, 128, 0, 64),
                                      qq=BUF(p, "q", i * 512, 512, 0, 64): e.matmul(
                                bk.ap[:, 0:512], kq, qq, start=True, stop=True),
                                wait=[bkA.free, bkB.free, k_done, q_done])
                            mB = sc.t(lambda e, bk=bkB, kq=BUF(p, "k", kc * 128, 128, 64, 128),
                                      qq=BUF(p, "q", i * 512, 512, 64, 128): e.matmul(
                                bk.ap[:, 0:512], kq, qq, start=True, stop=True,
                                tile_position=(64, 0), skip_group_check=True))
                            eA = sc.s(lambda e, bk=bkA, kc8=kc8: e.activation(
                                bsl(L1X + kc8 * 512, 512), bk.ap[:, 0:512], AF.Exp,
                                bias=cols[:, 50:51]),
                                wait=[mA, neg3_ev] + ([att_free] if att_free else []))
                            eB = sc.s(lambda e, bk=bkB, kc8=kc8: e.activation(
                                bsl(L1X + 4096 + kc8 * 512, 512), bk.ap[:, 0:512], AF.Exp,
                                bias=cols[:, 50:51]), wait=[mB])
                            bkA.free = eA
                            bkB.free = eB
                            rel(bkA)
                            rel(bkB)
                            exp_half.append(eB)
                            if kc8 == 2 and p + 1 < NPAIR:
                                emit_proj(p + 1, "q" if half == 0 else "k", i)
                        for kc8 in range(8):
                            kc = 8 * half + kc8
                            avA = sc.t(lambda e, po=poA, kc=kc, kc8=kc8, p=p: e.matmul(
                                po.ap[0:65, 0:512],
                                bfa[:, L1V + kc * 1040 + 2 * p * 65: L1V + kc * 1040 + 2 * p * 65 + 65],
                                bsl(L1X + kc8 * 512, 512),
                                start=(kc == 0), stop=(kc == 15), skip_group_check=True),
                                wait=([poA.free, poB.free, v_ev, exp_half[kc8]] if kc == 0
                                      else [exp_half[kc8]]))
                            avB = sc.t(lambda e, po=poB, kc=kc, kc8=kc8, p=p: e.matmul(
                                po.ap[0:65, 0:512],
                                bfa[:, L1V + kc * 1040 + (2 * p + 1) * 65: L1V + kc * 1040 + (2 * p + 1) * 65 + 65],
                                bsl(L1X + 4096 + kc8 * 512, 512),
                                start=(kc == 0), stop=(kc == 15), skip_group_check=True))
                        att_free = avB
                    def norm_closure(p=p, i=i, poA=poA, poB=poB, avA=avA, avB=avB):
                        f32a_ = env["f32a"]
                        r0 = sc.v(lambda e, po=poA: e.reciprocal(
                            f32a_[0:1, 0:512], po.ap[64:65, 0:512]), wait=[avA, avB, fz2_ev])
                        rA = sc.v(lambda e: e.tensor_copy(bfa[0:1, 48384:48896], f32a_[0:1, 0:512]),
                                  wait=[fz_ev])
                        r1 = sc.v(lambda e, po=poB: e.reciprocal(
                            f32a_[0:1, 512:1024], po.ap[64:65, 0:512]))
                        rB = sc.v(lambda e: e.tensor_copy(bfa[0:1, 48896:49408], f32a_[0:1, 512:1024]))
                        uA = sc.s(lambda e, po=poA: e.copy(bsl(L1U, 512, 0, 64), po.ap[0:64, 0:512]),
                                  wait=[rA, l1_layer.u_free])
                        poA.free = uA
                        rel(poA)
                        bcA = take()
                        mmA = sc.t(lambda e, bk=bcA: e.matmul(
                            bk.ap[:, 0:512], bc2, bfa[0:2, 48384:48896], start=True, stop=True),
                            wait=[bcA.free, rA])
                        bcA.ready = mmA
                        evA = sc.v(lambda e, bk=bcA, p=p, i=i: e.tensor_tensor(
                            bfa[0:64, L1AO + p * 2048 + i * 512: L1AO + p * 2048 + (i + 1) * 512],
                            bsl(L1U, 512, 0, 64), bk.ap[0:64, 0:512], op=ALU.mult),
                            wait=[bcA.ready, uA] + ([l1_layer.ao_free] if l1_layer.ao_free else []))
                        uB = sc.s(lambda e, po=poB: e.copy(bsl(L1U, 512, 64, 128), po.ap[0:64, 0:512]),
                                  wait=[rB, evA])
                        poB.free = uB
                        rel(poB)
                        bcB = take()
                        mmB = sc.t(lambda e, bk=bcB: e.matmul(
                            bk.ap[:, 0:512], bc2, bfa[0:2, 48896:49408], start=True, stop=True),
                            wait=[bcB.free, rB])
                        bcB.ready = mmB
                        evB = sc.v(lambda e, bk=bcB, p=p, i=i: e.tensor_tensor(
                            bfa[64:128, L1AO + p * 2048 + i * 512: L1AO + p * 2048 + (i + 1) * 512],
                            bsl(L1U, 512, 64, 128), bk.ap[0:64, 0:512], op=ALU.mult),
                            wait=[bcB.ready, uB])
                        bcA.free = evA
                        bcB.free = evB
                        rel(bcA)
                        rel(bcB)
                        l1_layer.u_free = evB
                        ao_ev[(p, i)] = evB
                    l1_layer.pending_norm = norm_closure
                if p == NPAIR - 1 and l1_layer.pending_norm is not None:
                    l1_layer.pending_norm()
                    l1_layer.pending_norm = None
                qkfree[p % 2] = (("sT", sc.cnt["sT"]))
                wfree[p % 2] = qkfree[p % 2]
            l1_layer.v_free = ao_ev[(NPAIR - 1, 3)]
            ot_ev = None
            for ck in range(8):
                ot_ev = staged(
                    w_in["oT"][ck * 128:(ck + 1) * 128, :], 1024,
                    lambda sl, ld, ck=ck: sc.v(
                        lambda e, sl=sl: e.tensor_copy(bsl(L1X + ck * 1024, 1024), sl),
                        wait=[ld, att_free]))
            for tt in range(4):
                s_ev3 = [None] * NC_F
                for dc in range(NC_F):
                    bk = take()
                    last = None
                    for ck in range(NC_F):
                        last = sc.t(lambda e, bk=bk, ck=ck, dc=dc, tt=tt: e.matmul(
                            bk.ap[:, 0:512], bsl(L1X + ck * 1024 + dc * 128, 128),
                            bsl(L1AO + ck * 2048 + tt * 512, 512),
                            start=(ck == 0), stop=(ck == 7)),
                            wait=([bk.free, ot_ev] + [ao_ev[(pp, tt)] for pp in range(NPAIR)]
                                  if ck == 0 else []))
                    bk.ready = last
                    s_ev3[dc] = sc.v(lambda e, bk=bk, dc=dc, tt=tt, sbase=sbase: e.tensor_tensor(
                        bsl(L1S + dc * 512, 512),
                        x_fm[:, dc * T + sbase + tt * 512: dc * T + sbase + (tt + 1) * 512],
                        bk.ap[:, 0:512], op=ALU.add), wait=[bk.ready, l1_layer.s_free])
                    bk.free = s_ev3[dc]
                    rel(bk)
                x3_ev = ln_norm(
                    lambda fc: bsl(L1S + fc * 512, 512),
                    lambda fc, tt=tt, sbase=sbase: x_fm[:, fc * T + sbase + tt * 512: fc * T + sbase + (tt + 1) * 512],
                    s_ev3)
                x3_ev_all.append(x3_ev[-1])
                x_ev_o[(s_, tt)] = x3_ev[-1]
                l1_layer.s_free = x3_ev[-1]
            l1_layer.x_free = x3_ev_all[-1]
            l1_layer.ao_free = x3_ev_all[-1]
            l1_layer.v_free = x3_ev_all[-1]
        l1_layer.x3_done = x3_ev_all[-1]
        l1_layer.x_ev_o = x_ev_o
        return x3_ev_all[-1]
    l1_layer.x_free = None
    l1_layer.v_free = None
    l1_layer.u_free = None
    l1_layer.ao_free = None

    F1W, F1A1, F1A2 = 0, 8192, 16384
    F1H1P = (24576, 28672)
    F1H2P = (32768, 32768)
    F1SP = (36864, 40960)

    def f1_phase(x3_done, write_out):
        f32a = env["f32a"]
        wevs = {}
        for name, off in (("f1w", F1W), ("f1a1", F1A1), ("f1a2", F1A2)):
            for ck in range(8):
                wevs[name] = staged(
                    w_in[name][ck * 128:(ck + 1) * 128, :], 1024,
                    lambda sl, ld, ck=ck, off=off: sc.v(
                        lambda e, sl=sl: e.tensor_copy(bsl(off + ck * 1024, 1024), sl),
                        wait=[ld, x3_done]))
        slot_ev = [None, None]
        h_free = {0: [None, None, None], 1: [None, None, None]}
        h2_free = [None]
        cfree = [None, None]
        for t8 in range(NTILE):
            par = t8 % 2
            F1H1, F1H2, F1S = F1H1P[par], F1H2P[par], F1SP[par]
            hfp = h_free[par]
            sbase = (t8 // 4) * S
            tt = t8 % 4

            def x3sl(fc, t8=t8):
                return x_fm[:, fc * T + t8 * 512: fc * T + (t8 + 1) * 512]

            x3_ok = l1_layer.x_ev_o[(t8 // 4, tt)]
            prev_ev = [x3_ok] * NC_F
            for idx, (woff, dsto, wname) in enumerate(
                    ((F1W, F1H1, "f1w"), (F1A1, F1H2, "f1a1"))):
                nxt = [None] * NC_F
                for dc in range(NC_F):
                    bk = take()
                    last = None
                    for ck in range(NC_F):
                        src = (x3sl(ck) if idx == 0 else bsl(F1H2 + 0, 0) if False else
                               (x3sl(ck) if idx == 0 else bsl(F1H1 + ck * 512, 512)))
                        last = sc.t(lambda e, bk=bk, ck=ck, dc=dc, woff=woff, src=src: e.matmul(
                            bk.ap[:, 0:512], bsl(woff + ck * 1024 + dc * 128, 128), src,
                            start=(ck == 0), stop=(ck == 7)),
                            wait=([bk.free, wevs[wname],
                                   (hfp[idx] if idx == 0 else h2_free[0])] + prev_ev
                                  if ck == 0 else []))
                    bk.ready = last
                    nxt[dc] = sc.v(lambda e, bk=bk, dc=dc, dsto=dsto: e.tensor_copy(
                        bsl(dsto + dc * 512, 512), bk.ap[:, 0:512]), wait=[bk.ready])
                    bk.free = nxt[dc]
                    rel(bk)
                prev_ev = nxt
            s_ev4 = [None] * NC_F
            t_ev = [None] * NC_F
            for dc in range(NC_F):
                bk = take()
                last = None
                for ck in range(NC_F):
                    last = sc.t(lambda e, bk=bk, ck=ck, dc=dc, F1H2=F1H2: e.matmul(
                        bk.ap[:, 0:512], bsl(F1A2 + ck * 1024 + dc * 128, 128),
                        bsl(F1H2 + ck * 512, 512), start=(ck == 0), stop=(ck == 7)),
                        wait=([bk.free, wevs["f1a2"], hfp[2]] + prev_ev if ck == 0 else []))
                bk.ready = last
                t_ev[dc] = sc.s(lambda e, bk=bk, dc=dc, F1H1=F1H1: e.activation(
                    bsl(F1H1 + dc * 512, 512), bk.ap[:, 0:512], AF.Tanh,
                    bias=cols[:, 16 + dc:17 + dc]), wait=[bk.ready, wev["f1b"]])
                s_ev4[dc] = sc.v(lambda e, dc=dc, xs=x3sl(dc), F1S=F1S, F1H1=F1H1: e.tensor_tensor(
                    bsl(F1S + dc * 512, 512), xs, bsl(F1H1 + dc * 512, 512), op=ALU.add),
                    wait=[t_ev[dc]])
                bk.free = s_ev4[dc]
                rel(bk)
            bank_s = ln_sums(lambda fc: bsl(F1S + fc * 512, 512), lambda fc: s_ev4[fc])
            r_ev, mur_ev = ln_stats(bank_s)
            rel(bank_s)
            if not write_out:
                x4_ev = ln_norm(lambda fc, F1S=F1S: bsl(F1S + fc * 512, 512),
                                lambda fc, t8=t8: x_fm[:, fc * T + t8 * 512: fc * T + (t8 + 1) * 512],
                                s_ev4)
                h_free[par] = [x4_ev[-1]] * 3
                h2_free[0] = x4_ev[-1]
                continue
            for c4 in range(4):
                slot = (t8 * 4 + c4) % 2
                sl = f32a[:, slot * 1024:(slot + 1) * 1024]
                tp = None
                for fc in range(NC_F):
                    tp = sc.t(lambda e, fc=fc, c4=c4, F1S=F1S: e.transpose(
                        tbank.ap[:, fc * 128:(fc + 1) * 128],
                        bsl(F1S + fc * 512 + c4 * 128, 128), idt_bf),
                        wait=([tbank.free, sq_cell[0], mur_ev] if fc == 0 else []))
                tbank.ready = tp
                bkc = take()
                cmm = sc.t(lambda e, bk=bkc, c4=c4: e.matmul(
                    bk.ap[:, 0:1], rows[0:2, SEG_A + c4 * 128: SEG_A + (c4 + 1) * 128],
                    bc2f[:, 0:1], start=True, stop=False), wait=[bkc.free, r_ev])
                cmm = sc.t(lambda e, bk=bkc, c4=c4: e.matmul(
                    bk.ap[:, 1:2], rows[0:2, SEG_C + c4 * 128: SEG_C + (c4 + 1) * 128],
                    bc2f[:, 0:1], start=False, stop=True), wait=[mur_ev])
                bkc.ready = cmm
                rows_guard[0] = cmm
                cev = sc.v(lambda e, bk=bkc, c4=c4: e.tensor_copy(
                    cols[:, 48 + 2 * (c4 % 2):50 + 2 * (c4 % 2)], bk.ap[:, 0:2]),
                    wait=[bkc.ready, cfree[c4 % 2]])
                bkc.free = cev
                rel(bkc)
                oev = None
                for fc in range(NC_F):
                    oev = sc.v(lambda e, fc=fc, sl=sl, c4=c4: e.tensor_scalar(
                        sl[:, fc * 128:(fc + 1) * 128], tbank.ap[:, fc * 128:(fc + 1) * 128],
                        cols[:, 48 + 2 * (c4 % 2):49 + 2 * (c4 % 2)],
                        cols[:, 49 + 2 * (c4 % 2):50 + 2 * (c4 % 2)], op0=ALU.mult, op1=ALU.add),
                        wait=([tbank.ready, cev, slot_ev[slot]] if fc == 0 else []))
                cfree[c4 % 2] = oev
                tbank.free = oev
                st = sc.dma("gpsimd", lambda e, sl=sl, t8=t8, c4=c4: e.dma_start(
                    out=out[t8 * 512 + c4 * 128: t8 * 512 + (c4 + 1) * 128, :], in_=sl),
                    "sD2" if slot == 0 else "sD3", wait=[oev])
                slot_ev[slot] = st
            tend = (("sT", sc.cnt["sT"]))
            h_free[par] = [s_ev4[-1], tend, tend]
            h2_free[0] = tend
        return slot_ev

    ev_x_ready = (("sS", sc.cnt["sS"]))
    emit_generic_out = True
    if upto != "x0":
        load_weights_l0()
        x2_done = l0_layer()
        ev_x_ready = (("sV", sc.cnt["sV"]))
        if upto not in ("x1", "x2"):
            l1_layer(x2_done)
            ev_x_ready = (("sV", sc.cnt["sV"]))
            if upto == "x3":
                pass
            else:
                f1_phase(l1_layer.x3_done, write_out=True)
                emit_generic_out = False

    out_stage = env["f32a"][:, 0:2048]
    slot_ev = [tap.last, tap.last]
    if not emit_generic_out:
        n_tc_gen = 0
    else:
        n_tc_gen = n_tc
    for c in range(n_tc_gen):
        slot = c % 2
        sl = out_stage[:, slot * 1024:(slot + 1) * 1024]
        tb = tbank
        tp_ev = None
        for fc in range(NC_F):
            tp_ev = sc.t(lambda e, fc=fc, c=c, tb=tb:
                         e.transpose(tb.ap[:, fc * 128:(fc + 1) * 128],
                                     x_fm[:, fc * T + c * 128: fc * T + (c + 1) * 128], idt_bf),
                         wait=([ev_x_ready, ld_consts, tb.free, slot_ev[slot]] if fc == 0 else []))
        tb.ready = tp_ev
        ev = sc.v(lambda e, sl=sl, tb=tb: e.tensor_copy(sl, tb.ap[:, 0:1024]), wait=[tb.ready])
        tb.free = ev
        st = sc.dma("gpsimd", lambda e, sl=sl, c=c: e.dma_start(out=out[c * 128:(c + 1) * 128, :], in_=sl),
                    "sD2" if slot == 0 else "sD3", wait=[ev])
        slot_ev[slot] = st
    sc.emit("gpsimd", None, wait=[("sD2", sc.cnt["sD2"]), ("sD3", sc.cnt["sD3"])])


def host_constants():
    c = np.zeros((128, CONST_W), dtype=np.float32)
    d = np.float32(HD)
    inv = (1.0 / (np.float32(10000.0) ** (np.arange(0, HD, 2, dtype=np.float32) / d))).astype(np.float32)
    pos = np.arange(S, dtype=np.float32)
    ang = (pos[None, :] * np.concatenate([inv, inv])[:, None]).astype(np.float32)
    cos = np.cos(ang).astype(np.float32)
    sin = np.sin(ang).astype(np.float32)
    c[:, 0:S] = np.concatenate([cos, cos], axis=0)
    c[:, S:2 * S] = np.concatenate([sin, sin], axis=0)
    c[:, 4096:4224] = np.eye(128, dtype=np.float32)
    triu = np.triu(np.ones((128, 128), dtype=np.float32))
    c[:, 4224:4352] = triu
    c[:, 4352:4480] = triu
    P = np.zeros((64, 64), dtype=np.float32)
    for i in range(32):
        P[i + 32, i] = -1.0
        P[i, i + 32] = 1.0
    c[0:64, 4480:4544] = P
    c[64:128, 4544:4608] = P
    c[:, 4608] = 1.0
    c[0, 4616:4744] = 1.0
    c[32, 4616:4744] = 1.0
    return c


KERNEL_UPTO = "out"


def kernel(L0_attn_x, L0_attn_Wq, L0_attn_Wk, L0_attn_Wv, L0_attn_Wo, L0_attn_Wg1,
           L0_attn_Wg2, L0_attn_ln_w, L0_attn_ln_b, L0_ffn_weight, L0_ffn_bias,
           L1_attn_qkv_w, L1_attn_o_w, L1_ffn_weight, L1_ffn_aux_weight1,
           L1_ffn_aux_weight2, L1_ffn_bias):
    xf = np.ascontiguousarray(np.asarray(L0_attn_x, dtype=np.float32))
    consts = host_constants()

    def v2d(v):
        return np.ascontiguousarray(np.asarray(v, np.float32).reshape(NC_F, 128).T)


    shared = {
        "consts": consts,
        "wq": np.ascontiguousarray(L0_attn_Wq, np.float32),
        "wk": np.ascontiguousarray(L0_attn_Wk, np.float32),
        "wv": np.ascontiguousarray(L0_attn_Wv, np.float32),
        "wo": np.ascontiguousarray(L0_attn_Wo, np.float32),
        "wg1": np.ascontiguousarray(L0_attn_Wg1, np.float32),
        "wg2": np.ascontiguousarray(L0_attn_Wg2, np.float32),
        "lnw2d": v2d(L0_attn_ln_w), "lnb2d": v2d(L0_attn_ln_b),
        "f0wT": np.ascontiguousarray(np.asarray(L0_ffn_weight, np.float32).T),
        "f0b2d": v2d(L0_ffn_bias),
        "qkvT": np.ascontiguousarray(np.asarray(L1_attn_qkv_w, np.float32).T),
        "oT": np.ascontiguousarray(np.asarray(L1_attn_o_w, np.float32).T),
        "f1w": np.ascontiguousarray(L1_ffn_weight, np.float32),
        "f1a1": np.ascontiguousarray(L1_ffn_aux_weight1, np.float32),
        "f1a2": np.ascontiguousarray(L1_ffn_aux_weight2, np.float32),
        "f1b2d": v2d(L1_ffn_bias),
    }
    nc = build_nc(upto=KERNEL_UPTO)
    in_maps = []
    for c in range(NCORES):
        m = dict(shared)
        m["x_in"] = np.ascontiguousarray(
            xf[c * SEQ_PER_CORE:(c + 1) * SEQ_PER_CORE].reshape(T, D))
        in_maps.append(m)
    res = run_bass_kernel_spmd(nc, in_maps, core_ids=list(range(NCORES)))
    outs = [res.results[c]["out"].reshape(SEQ_PER_CORE, S, D) for c in range(NCORES)]
    return np.concatenate(outs, axis=0).astype(np.float32)


if __name__ == "__main__":
    rng = np.random.default_rng(0)
    args = {}
    import inspect
    sig = inspect.signature(kernel)
    for name in sig.parameters:
        shp = {"L0_attn_x": (B, S, D), "L0_attn_Wq": (D, D), "L0_attn_Wk": (D, D),
               "L0_attn_Wv": (D, D), "L0_attn_Wo": (D, D), "L0_attn_Wg1": (D, HD),
               "L0_attn_Wg2": (HD, D), "L0_attn_ln_w": (D,), "L0_attn_ln_b": (D,),
               "L0_ffn_weight": (D, D), "L0_ffn_bias": (D,), "L1_attn_qkv_w": (3 * D, D),
               "L1_attn_o_w": (D, D), "L1_ffn_weight": (D, D), "L1_ffn_aux_weight1": (D, D),
               "L1_ffn_aux_weight2": (D, D), "L1_ffn_bias": (D,)}[name]
        args[name] = rng.standard_normal(shp).astype(np.float32) * 0.1
    o = kernel(**args)
    print("out", o.shape, o.dtype, np.abs(o - args["L0_attn_x"]).max())


# version 15
# speedup vs baseline: 1.0437x; 1.0002x over previous
import math
import numpy as np
import concourse.bass as bass
import concourse.mybir as mybir
from concourse.bass_utils import run_bass_kernel_spmd

F32 = mybir.dt.float32
BF16 = mybir.dt.bfloat16
FP8 = mybir.dt.float8e4
AF = mybir.ActivationFunctionType
ALU = mybir.AluOpType

B, S, D = 16, 2048, 1024
NCORES = 8
SEQ_PER_CORE = B // NCORES
T = SEQ_PER_CORE * S
NC_F = D // 128
H, HD = 16, 64
NPAIR = H // 2
TILE = 512
NTILE = T // TILE
CH = 128

ENGS = ("tensor", "scalar", "vector", "gpsimd", "sync")
DEV_TAPS = {}


class Sched:

    def __init__(self):
        self.ops = {e: [] for e in ENGS}
        self.cnt = {}
        self.last_wait = {e: {} for e in ENGS}

    def sem(self, name):
        self.cnt.setdefault(name, 0)

    def emit(self, eng, fn=None, wait=(), inc=None):
        w = []
        for ev in wait:
            if ev is None:
                continue
            s, v = ev
            if v is None:
                v = self.cnt.get(s, 0)
            if v <= 0:
                continue
            assert v <= self.cnt.get(s, 0), f"wait on future count {s}={v} (cur {self.cnt.get(s,0)})"
            if self.last_wait[eng].get(s, -1) >= v:
                continue
            self.last_wait[eng][s] = v
            w.append((s, v))
        ev_out = None
        if inc is not None:
            s, a = inc
            self.cnt[s] = self.cnt.get(s, 0) + a
            ev_out = (s, self.cnt[s])
        self.ops[eng].append((fn, w, inc))
        return ev_out

    def t(self, fn, wait=(), inc=True):
        return self.emit("tensor", fn, wait, ("sT", 1) if inc else None)

    def s(self, fn, wait=(), inc=True):
        w = list(wait) + [("sS", self.cnt.get("sS", 0))]
        return self.emit("scalar", fn, w, ("sS", 1) if inc else None)

    def v(self, fn, wait=(), inc=True):
        w = list(wait) + [("sV", self.cnt.get("sV", 0))]
        return self.emit("vector", fn, w, ("sV", 1) if inc else None)

    def g(self, fn, wait=(), inc=True):
        w = list(wait) + [("sG", self.cnt.get("sG", 0))]
        return self.emit("gpsimd", fn, w, ("sG", 1) if inc else None)

    def dma(self, eng, fn, sem, wait=()):
        return self.emit(eng, fn, wait, (sem, 16))


class Bank:
    __slots__ = ("ap", "name", "ready", "free")

    def __init__(self, ap, name):
        self.ap = ap
        self.name = name
        self.ready = None
        self.free = None


class Alloc:

    def __init__(self, total_elems, name):
        self.total = total_elems
        self.name = name
        self.cursor = 0
        self.marks = {}

    def take(self, n, align=8):
        c = (self.cursor + align - 1) // align * align
        assert c + n <= self.total, f"{self.name} arena overflow: {c}+{n} > {self.total}"
        self.cursor = c + n
        return c

    def mark(self, key):
        self.marks[key] = self.cursor

    def reset(self, key):
        self.cursor = self.marks[key]


def build_nc(dev_taps=(), upto="out"):
    nc = bass.Bass("TRN2", target_bir_lowering=False)
    sc = Sched()
    for s_ in ("sT", "sS", "sV", "sG", "sD0", "sD1", "sD2", "sD3", "sD4", "sX0", "sX1"):
        sc.sem(s_)

    x_in = nc.dram_tensor("x_in", [T, D], F32, kind="ExternalInput").ap()
    consts = nc.dram_tensor("consts", [128, CONST_W], F32, kind="ExternalInput").ap()
    w_in = {}
    for name, shp in WEIGHT_SHAPES.items():
        w_in[name] = nc.dram_tensor(name, shp, F32, kind="ExternalInput").ap()
    out = nc.dram_tensor("out", [T, D], F32, kind="ExternalOutput").ap()
    taps = {}
    for tname, tshape in DEV_TAPS.items():
        taps[tname] = nc.dram_tensor("tap_" + tname, list(tshape), F32, kind="ExternalOutput").ap()

    import contextlib
    with contextlib.ExitStack() as st_:
        e = st_.enter_context
        x_fm = e(nc.sbuf_tensor("x_fm", [128, NC_F * T], BF16))
        ctab = e(nc.sbuf_tensor("ctab", [128, 4096], BF16))
        cbf = e(nc.sbuf_tensor("cbf", [128, 1024], BF16))
        rows = e(nc.sbuf_tensor("rows", [128, 2304], F32))
        cols = e(nc.sbuf_tensor("cols", [128, 64], F32))
        stg = e(nc.sbuf_tensor("stg", [128, 2 * 1024], F32))
        bfa = e(nc.sbuf_tensor("bfa", [128, BFA_N], BF16))
        f8a = e(nc.sbuf_tensor("f8a", [128, F8_N], FP8))
        f32a = e(nc.sbuf_tensor("f32a", [128, F32A_N], F32))
        banks = [Bank(e(nc.psum_tensor(f"pb{i}", [128, 512], F32)), f"pb{i}") for i in range(7)]
        tbank = Bank(e(nc.psum_tensor("tb", [128, 1024], BF16)), "tb")
        sems = {}
        for s_ in ("sT", "sS", "sV", "sG", "sD0", "sD1", "sD2", "sD3", "sD4", "sX0", "sX1"):
            sems[s_] = e(nc.semaphore(s_))
        block = e(nc.Block())

        build_program(nc, sc, locals(), upto=upto)
        build_nc.last_sc = sc

        for engname in ENGS:
            section = getattr(block, engname)

            def _body(eng, _ops=sc.ops[engname]):
                for fn, waits, inc in _ops:
                    for s_, v in waits:
                        eng.wait_ge(sems[s_], v)
                    instr = fn(eng) if fn is not None else None
                    if inc is not None:
                        if instr is not None:
                            instr.then_inc(sems[inc[0]], inc[1])
                        else:
                            eng.nop().then_inc(sems[inc[0]], inc[1])

            section(_body)
    return nc


CONST_W = 4800

WEIGHT_SHAPES = {
    "wq": [D, D], "wk": [D, D], "wv": [D, D], "wo": [D, D],
    "wg1": [D, HD], "wg2": [HD, D],
    "lnw2d": [128, NC_F], "lnb2d": [128, NC_F],
    "f0wT": [D, D], "f0b2d": [128, NC_F],
    "qkvT": [D, 3 * D], "oT": [D, D],
    "f1w": [D, D], "f1a1": [D, D], "f1a2": [D, D], "f1b2d": [128, NC_F],
}

BFA_N = 49408
F8_N = 12 * 1024
F32A_N = 2 * 1024


def build_program(nc, sc, env, upto="out"):
    x_in, out = env["x_in"], env["out"]
    x_fm, ctab, cbf = env["x_fm"], env["ctab"], env["cbf"]
    stg, banks, rows = env["stg"], env["banks"], env["rows"]
    f32a, consts = env["f32a"], env["consts"]
    bfa, f8a, cols, w_in = env["bfa"], env["f8a"], env["cols"], env["w_in"]
    tbank = env["tbank"]

    S_ps = banks[6]
    avail = list(banks[0:6])
    held = []

    def take():
        b = avail.pop(0)
        held.append(b)
        return b

    def rel(b):
        held.remove(b)
        avail.append(b)

    OFF_WQ, OFF_WK, OFF_WV, OFF_WOW = 0, 8192, 16384, 24576
    OFF_U2, OFF_F0B2, OFF_WG1, OFF_WG2 = 32768, 33792, 34816, 35328
    OFF_QT, OFF_KT, OFF_KTM, OFF_G1 = 36352, 36864, 37376, 37888
    OFF_ATTD, OFF_SSB, OFF_VT, OFF_AO = 38400, 38912, 39424, 43520
    OFF_RTMP, OFF_SQ, OFF_ST2 = 47616, 48128, 48640
    SEG_A, SEG_B, SEG_C, SEG_D = 0, 512, 1024, 1536

    def bsl(off, n, p0=0, p1=128):
        return bfa[p0:p1, off:off + n]

    taps = env["taps"]
    tap_stage = env["f32a"]

    def tap(tag, ap, ev):
        if tag not in taps:
            return
        dst = taps[tag]
        n = dst.shape[1]
        p = dst.shape[0]
        cv = sc.v(lambda e, ap=ap, p=p, n=n: e.tensor_copy(tap_stage[0:p, 0:n], ap),
                  wait=[ev, tap.last])
        tap.last = sc.dma("gpsimd", lambda e, dst=dst, p=p, n=n: e.dma_start(
            out=dst, in_=tap_stage[0:p, 0:n]), "sD2", wait=[cv])
    tap.last = None

    ev_tab = None
    for piece in range(4):
        sl = stg[:, (piece % 2) * 1024:(piece % 2) * 1024 + 1024]
        ld = sc.dma("sync", lambda e, p=piece, sl=sl: e.dma_start(out=sl, in_=consts[:, p * 1024:(p + 1) * 1024]),
                    "sD0" if piece % 2 == 0 else "sD1", wait=[ev_tab] if piece >= 2 else [])
        ev_tab = sc.v(lambda e, p=piece, sl=sl: e.tensor_copy(ctab[:, p * 1024:(p + 1) * 1024], sl), wait=[ld])
    sl = stg[:, 0:704]
    ld_m = sc.dma("sync", lambda e, sl=sl: e.dma_start(out=sl, in_=consts[:, 4096:4800]), "sD0", wait=[ev_tab])
    ev_cbf = sc.v(lambda e, sl=sl: e.tensor_copy(cbf[:, 0:704], sl), wait=[ld_m])
    ev_idf = sc.v(lambda e, sl=sl: e.tensor_copy(rows[:, 2048:2176], sl[:, 0:128]), wait=[ld_m])
    ld_consts = ev_idf
    idt_bf = cbf[:, 0:128]
    ident_f32 = rows[:, 2048:2176]
    triu2 = cbf[:, 128:384]
    p2m = cbf[:, 384:512]
    onescol = cbf[:, 512:513]
    bc2 = cbf[0:2, 520:648]
    bc2_32 = cbf[32:34, 520:648]
    mz1 = sc.v(lambda e: e.memset(rows[0:64, 0:2048], 0.0))
    mz2 = sc.v(lambda e: e.memset(rows[0:64, 2176:2304], 0.0), wait=[mz1])
    mz3 = sc.v(lambda e: e.memset(rows[0:1, 2176:2304], 1.0), wait=[mz2])
    mz4 = sc.v(lambda e: e.memset(bfa[0:64, OFF_ST2:OFF_ST2 + 512], 0.0), wait=[mz3])
    ev_ones = sc.v(lambda e: e.memset(bfa[0:64, OFF_U2:OFF_U2 + 1024], 0.0), wait=[mz4])
    bc2f = rows[0:2, 2176:2304]
    st2bf = bfa[0:2, OFF_ST2:OFF_ST2 + 512]

    n_tc = T // 128
    stg_ev = [ld_consts, ld_consts]
    x_ready = []
    for c in range(n_tc):
        slot = c % 2
        stg_ap = stg[:, slot * 1024:(slot + 1) * 1024]
        ld = sc.dma("gpsimd", lambda e, c=c, a=stg_ap: e.dma_start(out=a, in_=x_in[c * 128:(c + 1) * 128, :]),
                    "sX0" if slot == 0 else "sX1", wait=[stg_ev[slot]])
        ev = None
        for half in range(2):
            bk = take()
            tp_ev = None
            for j in range(4):
                fc = half * 4 + j
                tp_ev = sc.t(lambda e, a=stg_ap, fc=fc, bk=bk, j=j:
                             e.transpose(bk.ap[:, j * 128:(j + 1) * 128],
                                         a[:, fc * 128:(fc + 1) * 128], ident_f32),
                             wait=[ld, ld_consts, bk.free] if j == 0 else [])
            bk.ready = tp_ev

            def ev_fn(e, bk=bk, c=c, half=half):
                dst = x_fm[:, :].rearrange("p (f t) -> p f t", f=NC_F)
                src = bk.ap[:, 0:512].rearrange("p (f t) -> p f t", f=4)
                return e.copy(dst[:, half * 4:(half + 1) * 4, c * 128:(c + 1) * 128], src)
            ev = sc.s(ev_fn, wait=[bk.ready])
            bk.free = ev
            rel(bk)
        stg_ev[slot] = ev
        x_ready.append(ev)

    wev = {}

    def staged(dram_sl, width, cast_fn, prows=128):
        slot = staged.rot[0] % 2
        staged.rot[0] += 1
        sl = stg[0:prows, slot * 1024: slot * 1024 + width]
        ld = sc.dma("sync", lambda e, sl=sl, d=dram_sl: e.dma_start(out=sl, in_=d),
                    "sD0" if slot == 0 else "sD1", wait=[stg_ev[slot]])
        ev = cast_fn(sl, ld)
        stg_ev[slot] = ev
        return ev
    staged.rot = [0]

    def load_plain_bf(name, off, width=1024):
        for ck in range(8):
            wev[name] = staged(
                w_in[name][ck * 128:(ck + 1) * 128, 0:width], width,
                lambda sl, ld, ck=ck: sc.v(
                    lambda e, sl=sl: e.tensor_copy(bsl(off + ck * width, width), sl), wait=[ld]))

    def load_weights_l0():
        sc.dma("sync", lambda e: e.dma_start(out=cols[:, 0:8], in_=w_in["lnw2d"]), "sD4")
        sc.dma("sync", lambda e: e.dma_start(out=cols[:, 8:16], in_=w_in["lnb2d"]), "sD4")
        sc.dma("sync", lambda e: e.dma_start(out=cols[:, 16:24], in_=w_in["f1b2d"]), "sD4")
        load_plain_bf("wq", OFF_WQ)
        load_plain_bf("wk", OFF_WK)
        load_plain_bf("wv", OFF_WV)
        for ck in range(8):
            wev["wg1"] = staged(
                w_in["wg1"][ck * 128:(ck + 1) * 128, :], 64,
                lambda sl, ld, ck=ck: sc.v(
                    lambda e, sl=sl: e.tensor_copy(bsl(OFF_WG1 + ck * 64, 64), sl), wait=[ld]))
        wev["wg2"] = staged(
            w_in["wg2"][:, :], 1024,
            lambda sl, ld: sc.v(lambda e, sl=sl: e.tensor_copy(bfa[0:64, OFF_WG2:OFF_WG2 + 1024], sl),
                                wait=[ld]), prows=64)
        for ck in range(8):
            wev["w8"] = staged(
                w_in["f0wT"][ck * 128:(ck + 1) * 128, :], 1024,
                lambda sl, ld, ck=ck: sc.v(
                    lambda e, sl=sl: e.tensor_copy(f8a[:, ck * 1024:(ck + 1) * 1024], sl), wait=[ld]))
        sc.dma("sync", lambda e: e.dma_start(out=cols[:, 40:48], in_=w_in["f0b2d"]), "sD4")
        colev = ("sD4", sc.cnt["sD4"])
        wev["lnw"] = colev
        wev["lnb"] = colev
        wev["f1b"] = colev
        wev["f0b2"] = colev
        bank_u = take()
        bank_c = take()
        for ck in range(8):
            def cast_wo(sl, ld, ck=ck):
                for half in range(2):
                    sc.t(lambda e, sl=sl, h=half, ck=ck: e.matmul(
                        bank_u.ap[(0 if h == 0 else 32):(1 if h == 0 else 33), 0:512],
                        cols[:, ck:ck + 1], sl[:, h * 512:(h + 1) * 512],
                        start=(ck == 0), stop=(ck == 7)),
                        wait=(([wev["lnw"], wev["lnb"], bank_u.free, bank_c.free] if ck == 0 else [])
                              + ([ld] if half == 0 else [])))
                for dc in range(8):
                    sc.t(lambda e, sl=sl, dc=dc, ck=ck: e.matmul(
                        bank_c.ap[:, dc:dc + 1], sl[:, dc * 128:(dc + 1) * 128],
                        cols[:, 8 + ck:9 + ck], start=(ck == 0 and dc == 0),
                        stop=(ck == 7 and dc == 7)))
                return sc.v(lambda e, sl=sl, ck=ck: e.tensor_scalar(
                    bsl(OFF_WOW + ck * 1024, 1024), sl, cols[:, ck:ck + 1], None, op0=ALU.mult),
                    wait=[wev["lnw"], (("sT", sc.cnt["sT"]))])
            wev["wo"] = staged(w_in["wo"][ck * 128:(ck + 1) * 128, :], 1024, cast_wo)
        bank_u.ready = (("sT", sc.cnt["sT"]))
        sc.v(lambda e: e.tensor_copy(bsl(OFF_U2, 512, 0, 1), bank_u.ap[0:1, 0:512]),
             wait=[bank_u.ready, ev_ones])
        sc.v(lambda e: e.tensor_copy(bsl(OFF_U2 + 512, 512, 0, 1), bank_u.ap[32:33, 0:512]))
        e4 = sc.v(lambda e: e.tensor_copy(cols[:, 32:40], bank_c.ap[:, 0:8]))
        bank_u.free = e4
        bank_c.free = e4
        rel(bank_u)
        rel(bank_c)
        wev["u2"] = e4


    sq_cell = [None]
    mur_bf_cell = [None]
    rows_guard = [None]

    def ln_stats(bank_s):
        sc.v(lambda e, bk=bank_s: e.tensor_scalar(
            rows[0:1, SEG_B:SEG_B + 512], bk.ap[0:1, 0:512], 1.0 / 1024.0, None,
            op0=ALU.mult), wait=[bank_s.ready, ev_ones, rows_guard[0]])
        sc.v(lambda e, bk=bank_s: e.tensor_scalar(
            rows[0:1, SEG_C:SEG_C + 512], bk.ap[32:33, 0:512], 1.0 / 1024.0, None,
            op0=ALU.mult))
        sc.v(lambda e: e.tensor_tensor(rows[0:1, SEG_D:SEG_D + 512],
                                       rows[0:1, SEG_B:SEG_B + 512],
                                       rows[0:1, SEG_B:SEG_B + 512], op=ALU.mult))
        vv = sc.v(lambda e: e.scalar_tensor_tensor(rows[0:1, SEG_A:SEG_A + 512],
                                                   rows[0:1, SEG_C:SEG_C + 512], 1e-5,
                                                   rows[0:1, SEG_D:SEG_D + 512],
                                                   op0=ALU.add, op1=ALU.subtract))
        ev5 = sc.s(lambda e: e.activation(rows[0:1, SEG_D:SEG_D + 512],
                                          rows[0:1, SEG_A:SEG_A + 512], AF.Sqrt),
                   wait=[vv])
        r_ev = sc.v(lambda e: e.reciprocal(rows[0:1, SEG_A:SEG_A + 512],
                                           rows[0:1, SEG_D:SEG_D + 512]), wait=[ev5])
        mur_ev = sc.v(lambda e: e.scalar_tensor_tensor(
            rows[0:1, SEG_C:SEG_C + 512], rows[0:1, SEG_B:SEG_B + 512], -1.0,
            rows[0:1, SEG_A:SEG_A + 512], op0=ALU.mult, op1=ALU.mult))
        mur_bf_cell[0] = sc.v(lambda e: e.tensor_copy(
            bfa[0:1, OFF_ST2:OFF_ST2 + 512], rows[0:1, SEG_C:SEG_C + 512]), wait=[ev_ones])
        r_bf = sc.v(lambda e: e.tensor_copy(
            bfa[32:33, OFF_ST2:OFF_ST2 + 512], rows[0:1, SEG_A:SEG_A + 512]))
        bank_s.free = r_bf
        return r_bf, mur_bf_cell[0]

    def ln_sums(src_fn, rd_ev):
        bank_s = take()
        for fc in range(NC_F):
            sc.t(lambda e, bk=bank_s, fc=fc, ap=src_fn(fc): e.matmul(
                bk.ap[0:1, 0:512], onescol, ap,
                start=(fc == 0), stop=(fc == 7)),
                wait=([bank_s.free, rd_ev(fc)] if fc == 0 else [rd_ev(fc)]))
        smm = None
        for fc in range(NC_F):
            sqw = [rd_ev(fc)]
            if smm is not None:
                sqw.append(smm)
            elif sq_cell[0] is not None:
                sqw.append(sq_cell[0])
            sqe = sc.v(lambda e, fc=fc, ap=src_fn(fc): e.tensor_tensor(
                bsl(OFF_SQ, 512), ap, ap, op=ALU.mult), wait=sqw)
            smm = sc.t(lambda e, bk=bank_s, fc=fc: e.matmul(
                bk.ap[32:33, 0:512], onescol, bsl(OFF_SQ, 512),
                start=(fc == 0), stop=(fc == 7)), wait=[sqe])
        bank_s.ready = smm
        sq_cell[0] = smm
        return bank_s

    def ln_norm(src_fn, dst_fn, s_evs, filler=None):
        bank_s = ln_sums(src_fn, lambda fc: s_evs[fc])
        r_ev, mur_ev = ln_stats(bank_s)
        rel(bank_s)
        if filler is not None:
            filler()
        bb = take()
        rb = sc.t(lambda e, bk=bb: e.matmul(
            bk.ap[:, 0:512], bc2_32, bfa[32:34, OFF_ST2:OFF_ST2 + 512], start=True, stop=True),
            wait=[bb.free, r_ev, ld_consts])
        bb.ready = rb
        bcx = take()
        cbx = sc.t(lambda e, bk=bcx: e.matmul(
            bk.ap[:, 0:512], bc2, bfa[0:2, OFF_ST2:OFF_ST2 + 512], start=True, stop=True),
            wait=[bcx.free, mur_ev])
        bcx.ready = cbx
        outev = [None] * NC_F
        for fc in range(NC_F):
            sc.v(lambda e, fc=fc, bk=bb, ap=src_fn(fc): e.tensor_tensor(
                ap, ap, bk.ap[:, 0:512], op=ALU.mult), wait=[bb.ready, sq_cell[0]])
            outev[fc] = sc.v(lambda e, fc=fc, bk=bcx, ap=src_fn(fc), d=dst_fn(fc): e.tensor_tensor(
                d, ap, bk.ap[:, 0:512], op=ALU.add), wait=[bcx.ready])
        bb.free = outev[-1]
        bcx.free = outev[-1]
        rel(bb)
        rel(bcx)
        return outev

    def l0_layer():
        import os as _os
        NTILE_RUN = int(_os.environ.get("DEV_NTILE", NTILE))
        LVL = int(_os.environ.get("DEV_LVL", "99"))
        l0_layer.last_sevac = None
        l0_layer.s_free_dc = None
        l0_layer.pre = {}
        l0_layer.qk_free = [None, None]
        l0_layer.ktm_free = None
        s_evac_ev = [None] * NPAIR
        xq_free = None
        vt_free = None
        ao_free = None
        attd_free = [None, None]
        last_x_ev = None
        for t in range(NTILE_RUN):
            ts = t % 4
            first_of_seq = (ts == 0)

            def xsl(fc, t=t):
                return x_fm[:, fc * T + t * TILE: fc * T + (t + 1) * TILE]

            ev_xt = x_ready[4 * t + 3]

            v_ev = None
            for c4 in range(4):
                for j in range(2):
                    bk = take()
                    vtw = [l0_layer.s_free_dc[2 * c4], l0_layer.s_free_dc[2 * c4 + 1]] \
                        if l0_layer.s_free_dc is not None else []
                    last = None
                    for ck in range(8):
                        last = sc.t(lambda e, bk=bk, ck=ck, c4=c4, j=j, t=t: e.matmul(
                            bk.ap[:, 0:512],
                            x_fm[:, ck * T + (4 * t + c4) * 128: ck * T + (4 * t + c4) * 128 + 128],
                            bsl(OFF_WV + ck * 1024 + j * 512, 512),
                            start=(ck == 0), stop=(ck == 7)),
                            wait=([bk.free, x_ready[4 * t + c4], wev["wv"]] + vtw
                                  if ck == 0 else []))
                    bk.ready = last
                    v_ev = sc.s(lambda e, bk=bk, c4=c4, j=j: e.copy(
                        bsl(OFF_VT + c4 * 1024 + j * 512, 512), bk.ap[:, 0:512]), wait=[bk.ready])
                    bk.free = v_ev
                    rel(bk)

            if LVL < 2:
                continue
            def emit_g1(xsl_, ev_xt_):
                bk = take()
                last = None
                for ck in range(8):
                    last = sc.t(lambda e, bk=bk, ck=ck, xs=xsl_(ck): e.matmul(
                        bk.ap[0:64, 0:512], bsl(OFF_WG1 + ck * 64, 64), xs,
                        start=(ck == 0), stop=(ck == 7)),
                        wait=([bk.free, ev_xt_, wev["wg1"]] if ck == 0 else []))
                bk.ready = last
                ev = sc.s(lambda e, bk=bk: e.copy(bsl(OFF_G1, 512, 0, 64), bk.ap[0:64, 0:512]),
                          wait=[bk.ready])
                bk.free = ev
                rel(bk)
                return ev

            g1_ev = l0_layer.pre.pop("g1_ev", None)
            if g1_ev is None:
                g1_ev = emit_g1(xsl, ev_xt)

            def emit_g2(g1e, aof):
                gev = [None] * NC_F
                for fc in range(NC_F):
                    bk = take()
                    mv = sc.t(lambda e, bk=bk, fc=fc: e.matmul(
                        bk.ap[:, 0:512], bfa[0:64, OFF_WG2 + fc * 128: OFF_WG2 + (fc + 1) * 128],
                        bsl(OFF_G1, 512, 0, 64), start=True, stop=True),
                        wait=[bk.free, g1e, wev["wg2"], aof])
                    bk.ready = mv
                    gev[fc] = sc.s(lambda e, bk=bk, fc=fc: e.activation(
                        bsl(OFF_AO + fc * 512, 512), bk.ap[:, 0:512], AF.Sigmoid),
                        wait=[bk.ready])
                    bk.free = gev[fc]
                    rel(bk)
                return gev

            gate_ev = l0_layer.pre.pop("gate_ev", None)
            if gate_ev is None:
                gate_ev = emit_g2(g1_ev, ao_free)
                if t == 0:
                    tap("gate0", bsl(OFF_AO, 512), gate_ev[0])

            if LVL < 3:
                continue
            def QB(p, off, n, p0=0, p1=128):
                base = OFF_QT if p % 2 == 0 else OFF_F0B2
                return bfa[p0:p1, base + off: base + off + n]

            def KB(p, off, n, p0=0, p1=128):
                base = OFF_KT if p % 2 == 0 else OFF_F0B2 + 512
                return bfa[p0:p1, base + off: base + off + n]

            po_ev = [None] * NPAIR
            silu_ev = l0_layer.pre.pop("silu_ev", {})
            rope_ev = l0_layer.pre.pop("rope_ev", {})
            ktm_ev_d = l0_layer.pre.pop("ktm_ev_d", {})

            def proj_mm(p, which, xsl_=None, ev_xt_=None):
                if xsl_ is None:
                    xsl_, ev_xt_ = xsl, ev_xt
                woff = OFF_WQ if which == "q" else OFF_WK
                wname = "wq" if which == "q" else "wk"
                BUFF = QB if which == "q" else KB
                bk = take()
                last = None
                for ck in range(8):
                    last = sc.t(lambda e, bk=bk, ck=ck, woff=woff, p=p, xs=xsl_(ck): e.matmul(
                        bk.ap[:, 0:512], bsl(woff + ck * 1024 + p * 128, 128), xs,
                        start=(ck == 0), stop=(ck == 7)),
                        wait=([bk.free, ev_xt_, wev[wname], l0_layer.qk_free[p % 2]]
                              if ck == 0 else []))
                bk.ready = last
                ev = sc.s(lambda e, bk=bk, d=BUFF(p, 0, 512): e.activation(
                    d, bk.ap[:, 0:512], AF.Silu), wait=[bk.ready])
                bk.free = ev
                rel(bk)
                silu_ev[(p, which)] = ev

            def proj_rope(p, which, ts_=None):
                if ts_ is None:
                    ts_ = ts
                BUFF = QB if which == "q" else KB
                ev = silu_ev[(p, which)]
                bk2 = take()
                rot = sc.t(lambda e, bk=bk2, srcb=BUFF(p, 0, 512): e.matmul(
                    bk.ap[:, 0:512], p2m, srcb, start=True, stop=True),
                    wait=[bk2.free, ev, ld_consts])
                bk2.ready = rot
                sc.v(lambda e, srcb=BUFF(p, 0, 512), ts=ts_: e.tensor_tensor(
                    bsl(OFF_RTMP, 512), srcb, ctab[:, ts * 512:(ts + 1) * 512],
                    op=ALU.mult), wait=[ev])
                sc.v(lambda e, bk=bk2, d=BUFF(p, 0, 512), ts=ts_: e.tensor_tensor(
                    d, bk.ap[:, 0:512], ctab[:, 2048 + ts * 512: 2048 + (ts + 1) * 512],
                    op=ALU.mult), wait=[bk2.ready])
                dn = sc.v(lambda e, d=BUFF(p, 0, 512): e.tensor_tensor(
                    d, d, bsl(OFF_RTMP, 512), op=ALU.add))
                bk2.free = dn
                rel(bk2)
                rope_ev[(p, which)] = dn
                if t == 0 and p == 0 and which == "q":
                    tap("q_silu", QB(0, 0, 512), ev)
                    tap("q_rope", QB(0, 0, 512), dn)

            def l0_proj(p, which, xsl_=None, ev_xt_=None, ts_=None):
                proj_mm(p, which, xsl_, ev_xt_)
                proj_rope(p, which, ts_)

            def l0_ktm(p):
                tp = None
                for c4 in range(4):
                    tp = sc.t(lambda e, c4=c4, srcb=KB(p, c4 * 128, 128): e.transpose(
                        tbank.ap[:, c4 * 128:(c4 + 1) * 128], srcb, idt_bf),
                        wait=([rope_ev[(p, "k")], tbank.free, l0_layer.ktm_free]
                              if c4 == 0 else []))
                tbank.ready = tp
                ev = sc.s(lambda e: e.copy(bsl(OFF_KTM, 512), tbank.ap[:, 0:512]),
                          wait=[tbank.ready])
                tbank.free = ev
                ktm_ev_d[p] = ev

            def l0_attn_chunk(p, c4, po, po2, filler=None):
                cs = ts * 4 + c4
                bdA = take()
                bdB = take()
                mA = sc.t(lambda e, bd=bdA, kq=KB(p, c4 * 128, 128, 0, 64),
                          qq=QB(p, c4 * 128, 128, 0, 64): e.matmul(
                    bd.ap[:, 0:128], kq, qq, start=True, stop=True),
                    wait=[bdA.free, bdB.free, rope_ev[(p, "q")], rope_ev[(p, "k")]])
                mB = sc.t(lambda e, bd=bdB, kq=KB(p, c4 * 128, 128, 64, 128),
                          qq=QB(p, c4 * 128, 128, 64, 128): e.matmul(
                    bd.ap[:, 0:128], kq, qq, start=True, stop=True,
                    tile_position=(64, 0), skip_group_check=True))
                bdA.ready = mA
                bdB.ready = mB
                par = c4 % 2
                sc.v(lambda e, bd=bdA, par=par: e.tensor_tensor(
                    bsl(OFF_ATTD + par * 256, 128), bd.ap[:, 0:128], triu2[:, 0:128],
                    op=ALU.mult), wait=[bdA.ready, bdB.ready, attd_free[par]])
                m_ev = sc.v(lambda e, bd=bdB, par=par: e.tensor_tensor(
                    bsl(OFF_ATTD + par * 256 + 128, 128), bd.ap[:, 0:128], triu2[:, 0:128],
                    op=ALU.mult))
                bdA.free = m_ev
                bdB.free = m_ev
                rel(bdA)
                rel(bdB)
                if t == 0 and p == 0 and c4 == 0:
                    tap("attd", bsl(OFF_ATTD, 256), m_ev)
                if filler is not None:
                    filler()
                last_mm = (c4 == 3)
                sc.t(lambda e, po=po, c4=c4, p=p, st=(last_mm and cs == 0): e.matmul(
                    po.ap[0:64, c4 * 128:(c4 + 1) * 128],
                    bfa[:, OFF_VT + c4 * 1024 + 2 * p * 64: OFF_VT + c4 * 1024 + 2 * p * 64 + 64],
                    bsl(OFF_ATTD + (c4 % 2) * 256, 128), start=(c4 == 0), stop=st),
                    wait=([po.free, po2.free, m_ev, v_ev] if c4 == 0 else [m_ev]))
                last_po = sc.t(lambda e, po=po2, c4=c4, p=p, st=(last_mm and cs == 0): e.matmul(
                    po.ap[64:128, c4 * 128:(c4 + 1) * 128],
                    bfa[:, OFF_VT + c4 * 1024 + (2 * p + 1) * 64: OFF_VT + c4 * 1024 + (2 * p + 1) * 64 + 64],
                    bsl(OFF_ATTD + (c4 % 2) * 256 + 128, 128), start=(c4 == 0), stop=st,
                    tile_position=(0, 64), skip_group_check=True))
                if cs > 0:
                    sc.t(lambda e, po=po, c4=c4, p=p, st=last_mm,
                         qq=QB(p, c4 * 128, 128, 0, 64): e.matmul(
                        po.ap[0:64, c4 * 128:(c4 + 1) * 128], bsl(OFF_SSB + p * 64, 64, 0, 64),
                        qq, start=False, stop=st),
                        wait=[s_evac_ev[p]] if s_evac_ev[p] else [])
                    last_po = sc.t(lambda e, po=po2, c4=c4, p=p, st=last_mm,
                                   qq=QB(p, c4 * 128, 128, 64, 128): e.matmul(
                        po.ap[64:128, c4 * 128:(c4 + 1) * 128], bsl(OFF_SSB + p * 64, 64, 64, 128),
                        qq, start=False, stop=st,
                        tile_position=(64, 64), skip_group_check=True))
                su_w = [ktm_ev_d[p], v_ev]
                if l0_layer.last_sevac is not None:
                    su_w.append(l0_layer.last_sevac)
                s_first = (first_of_seq and c4 == 0 and p == 0)
                sc.t(lambda e, c4=c4, p=p, st=s_first: e.matmul(
                    S_ps.ap[0:64, p * 64:(p + 1) * 64], bsl(OFF_KTM + c4 * 128, 64),
                    bfa[:, OFF_VT + c4 * 1024 + 2 * p * 64: OFF_VT + c4 * 1024 + 2 * p * 64 + 64],
                    start=st, stop=True, skip_group_check=True), wait=su_w)
                sB = sc.t(lambda e, c4=c4, p=p, st=s_first: e.matmul(
                    S_ps.ap[64:128, p * 64:(p + 1) * 64], bsl(OFF_KTM + c4 * 128 + 64, 64),
                    bfa[:, OFF_VT + c4 * 1024 + (2 * p + 1) * 64: OFF_VT + c4 * 1024 + (2 * p + 1) * 64 + 64],
                    start=st, stop=True, tile_position=(0, 64), skip_group_check=True))
                s_evac_ev[p] = sc.s(lambda e, p=p: e.copy(
                    bsl(OFF_SSB + p * 64, 64), S_ps.ap[:, p * 64:(p + 1) * 64]), wait=[sB])
                l0_layer.last_sevac = s_evac_ev[p]
                attd_free[par] = last_po
                if t == 0 and p == 0 and c4 == 0:
                    tap("s_sb", bsl(OFF_SSB, 64), s_evac_ev[p])

            if not l0_layer.pre.pop("proj0", False):
                l0_proj(0, "q")
                l0_proj(0, "k")
            if not l0_layer.pre.pop("ktm0", False):
                l0_ktm(0)
            for p in range(NPAIR):
                po = take()
                po2 = take()
                for c4 in range(4):
                    filler = None
                    if p + 1 < NPAIR:
                        if c4 == 0:
                            filler = (lambda pp=p + 1: proj_mm(pp, "q"))
                        elif c4 == 1:
                            filler = (lambda pp=p + 1: proj_rope(pp, "q"))
                        elif c4 == 2:
                            filler = (lambda pp=p + 1: proj_mm(pp, "k"))
                        elif c4 == 3:
                            filler = (lambda pp=p + 1: proj_rope(pp, "k"))
                    l0_attn_chunk(p, c4, po, po2, filler)
                po.ready = (("sT", sc.cnt["sT"]))
                sc.v(lambda e, po=po, p=p: e.tensor_tensor(
                    bsl(OFF_AO + p * 512, 512, 0, 64), po.ap[0:64, 0:512],
                    bsl(OFF_AO + p * 512, 512, 0, 64),
                    op=ALU.mult), wait=[po.ready, gate_ev[p]])
                po_ev[p] = sc.v(lambda e, po=po2, p=p: e.tensor_tensor(
                    bsl(OFF_AO + p * 512, 512, 64, 128), po.ap[64:128, 0:512],
                    bsl(OFF_AO + p * 512, 512, 64, 128),
                    op=ALU.mult))
                po.free = po_ev[p]
                po2.free = po_ev[p]
                rel(po)
                rel(po2)
                l0_layer.qk_free[p % 2] = (("sT", sc.cnt["sT"]))
                l0_layer.ktm_free = (("sT", sc.cnt["sT"]))
                if p + 1 < NPAIR:
                    l0_ktm(p + 1)
                if t == 0 and p == 0:
                    tap("ao_p0", bsl(OFF_AO, 512), po_ev[0])

            if LVL < 8:
                continue
            nx = (t + 1 < NTILE_RUN)
            if nx:
                t1 = t + 1

                def xsl_next(fc, t1=t1):
                    return x_fm[:, fc * T + t1 * TILE: fc * T + (t1 + 1) * TILE]

                ev_xt_next = x_ready[4 * t1 + 3]
                ts_next = t1 % 4
            bank_s = ln_sums(lambda fc: bsl(OFF_AO + fc * 512, 512), lambda fc: po_ev[fc])
            r_ev, mur_ev = ln_stats(bank_s)
            rel(bank_s)
            if nx:
                l0_layer.pre["g1_ev"] = emit_g1(xsl_next, ev_xt_next)
            bank_b = take()
            rb = sc.t(lambda e, bk=bank_b: e.matmul(
                bk.ap[:, 0:512], bc2_32, bfa[32:34, OFF_ST2:OFF_ST2 + 512], start=True, stop=True),
                wait=[bank_b.free, r_ev, ev_ones, ld_consts])
            bank_b.ready = rb
            if nx:
                l0_proj(0, "q", xsl_next, ev_xt_next, ts_next)
            m_ev = [None] * NC_F
            for fc in range(NC_F):
                m_ev[fc] = sc.v(lambda e, fc=fc, bk=bank_b: e.tensor_tensor(
                    bsl(OFF_AO + fc * 512, 512), bsl(OFF_AO + fc * 512, 512), bk.ap[:, 0:512],
                    op=ALU.mult), wait=[bank_b.ready, sq_cell[0]])
            bank_b.free = m_ev[-1]
            rel(bank_b)

            if LVL < 9:
                continue
            s_ev = [None] * NC_F
            for dc in range(NC_F):
                bk = take()
                for ck in range(NC_F):
                    sc.t(lambda e, bk=bk, ck=ck, dc=dc: e.matmul(
                        bk.ap[:, 0:512], bsl(OFF_WOW + ck * 1024 + dc * 128, 128),
                        bsl(OFF_AO + ck * 512, 512), start=(ck == 0), stop=False),
                        wait=([bk.free, wev["wo"], wev["u2"], m_ev[-1]] if ck == 0 else []))
                last = sc.t(lambda e, bk=bk, dc=dc: e.matmul(
                    bk.ap[:, 0:512], bfa[0:2, OFF_U2 + dc * 128: OFF_U2 + (dc + 1) * 128],
                    st2bf, start=False, stop=True),
                    wait=[mur_bf_cell[0], ev_ones])
                bk.ready = last
                s_ev[dc] = sc.v(lambda e, bk=bk, dc=dc, xs=xsl(dc): e.scalar_tensor_tensor(
                    bsl(OFF_VT + dc * 512, 512), bk.ap[:, 0:512], cols[:, 32 + dc:33 + dc],
                    xs, op0=ALU.add, op1=ALU.add),
                    wait=[bk.ready])
                bk.free = s_ev[dc]
                rel(bk)
            ao_free = (("sT", sc.cnt["sT"]))

            if LVL < 10:
                continue
            if t == 0:
                tap("s0", bsl(OFF_VT, 512), s_ev[0])
                tap("stats0", rows[0:1, 0:2048], s_ev[0])
            fill1 = (lambda: l0_proj(0, "k", xsl_next, ev_xt_next, ts_next)) if nx else None
            x1_ev = ln_norm(lambda fc: bsl(OFF_VT + fc * 512, 512), lambda fc: xsl(fc), s_ev,
                            filler=fill1)
            if upto == "x1":
                vt_free = x1_ev[-1]
                l0_layer.s_free_dc = x1_ev
                if nx:
                    l0_layer.pre["proj0"] = True
                    l0_layer.pre["silu_ev"] = silu_ev
                    l0_layer.pre["rope_ev"] = rope_ev
                    l0_layer.pre["ktm_ev_d"] = ktm_ev_d
                ao_free = x1_ev[-1]
                last_x_ev = x1_ev[-1]
                continue

            q_ev = [None] * NC_F
            for fc in range(NC_F):
                q_ev[fc] = sc.s(lambda e, fc=fc, xs=xsl(fc): e.copy(
                    f8a[:, 8192 + fc * 512: 8192 + (fc + 1) * 512], xs), wait=[x1_ev[fc]])
            s_ev2 = [None] * NC_F
            for dc in range(NC_F):
                bk = take()
                for ck in range(NC_F):
                    sc.t(lambda e, bk=bk, ck=ck, dc=dc: e.matmul(
                        bk.ap[:, 0:512], f8a[:, ck * 1024 + dc * 128: ck * 1024 + dc * 128 + 128],
                        f8a[:, 8192 + ck * 512: 8192 + (ck + 1) * 512],
                        start=(ck == 0), stop=(ck == NC_F - 1)),
                        wait=([bk.free, wev["w8"], q_ev[-1]] if ck == 0 else []))
                bk.ready = (("sT", sc.cnt["sT"]))
                s_ev2[dc] = sc.v(lambda e, bk=bk, dc=dc, xs=xsl(dc): e.scalar_tensor_tensor(
                    bsl(OFF_VT + dc * 512, 512), bk.ap[:, 0:512], cols[:, 40 + dc:41 + dc],
                    xs, op0=ALU.add, op1=ALU.add),
                    wait=[bk.ready, x1_ev[dc], wev["f0b2"]])
                bk.free = s_ev2[dc]
                rel(bk)
            if nx:
                def fill2(aof=ao_free):
                    l0_ktm(0)
                    l0_layer.pre["ktm0"] = True
                    l0_layer.pre["gate_ev"] = emit_g2(l0_layer.pre["g1_ev"], aof)
            else:
                fill2 = None
            x2_ev = ln_norm(lambda fc: bsl(OFF_VT + fc * 512, 512), lambda fc: xsl(fc), s_ev2,
                            filler=fill2)
            vt_free = x2_ev[-1]
            l0_layer.s_free_dc = x2_ev
            if nx:
                l0_layer.pre["proj0"] = True
                l0_layer.pre["silu_ev"] = silu_ev
                l0_layer.pre["rope_ev"] = rope_ev
                l0_layer.pre["ktm_ev_d"] = ktm_ev_d
            ao_free = x2_ev[-1]
            last_x_ev = x2_ev[-1]
        return last_x_ev


    L1X, L1QKV, L1Q, L1K, L1V, L1AO, L1U, L1RT = 0, 8192, 10240, 12288, 14336, 30976, 47360, 47872
    L1S = 24576

    def l1_layer(x2_done):
        import os as _os
        if S_ps in held:
            pass
        if S_ps not in avail:
            S_ps.free = x2_done
            avail.append(S_ps)
        l1_layer.pending_norm = None
        l1_layer.x_free = x2_done
        l1_layer.v_free = x2_done
        l1_layer.u_free = x2_done
        l1_layer.ao_free = x2_done
        l1_layer.s_free = x2_done
        neg3_ev = sc.v(lambda e: e.memset(cols[:, 50:51], -3.0))
        fz_ev = sc.v(lambda e: e.memset(bfa[0:64, 48384:49408], 0.0))
        fz2_ev = sc.v(lambda e: e.memset(env["f32a"][0:64, 0:1024], 0.0), wait=[fz_ev])
        f32a = env["f32a"]
        vv = bfa[:, L1V:L1V + 16640].rearrange("p (c h d) -> p c h d", c=16, h=16)
        x3_ev_all = []
        vproj_last_reader = None
        x_ev_o = {}
        for s_ in range(SEQ_PER_CORE):
            sbase = s_ * S
            ones_v_ev = sc.v(lambda e: e.memset(vv[:, :, :, 64:65], 1.0),
                             wait=[l1_layer.v_free])
            wv1_ev = None
            for ck in range(8):
                wv1_ev = staged(
                    w_in["qkvT"][ck * 128:(ck + 1) * 128, 2048:3072], 1024,
                    lambda sl, ld, ck=ck: sc.v(
                        lambda e, sl=sl: e.tensor_copy(bsl(L1X + ck * 1024, 1024), sl),
                        wait=[ld, l1_layer.x_free]))
            v_ev = None
            for tc in range(16):
                for j in range(2):
                    bk = take()
                    last = None
                    for ck in range(8):
                        last = sc.t(lambda e, bk=bk, ck=ck, tc=tc, j=j, sbase=sbase: e.matmul(
                            bk.ap[:, 0:512],
                            x_fm[:, ck * T + sbase + tc * 128: ck * T + sbase + tc * 128 + 128],
                            bsl(L1X + ck * 1024 + j * 512, 512),
                            start=(ck == 0), stop=(ck == 7)),
                            wait=([bk.free, wv1_ev, x2_done, l1_layer.v_free] if ck == 0 else []))
                    bk.ready = last
                    def vevac(e, bk=bk, tc=tc, j=j):
                        dst = vv[:, tc:tc + 1, 8 * j:8 * j + 8, 0:64]
                        srcv = bk.ap[:, 0:512].rearrange("p (o h d) -> p o h d", o=1, h=8)
                        return e.copy(dst, srcv)
                    v_ev = sc.s(vevac, wait=[bk.ready, ones_v_ev])
                    bk.free = v_ev
                    rel(bk)
            f8bf = f8a[:, 0:12288].bitcast(BF16)

            def BUF(p, which, off, n, p0=0, p1=128):
                if p % 2 == 0:
                    base = {"q": L1Q, "k": L1K, "w": L1QKV}[which]
                    return bfa[p0:p1, base + off: base + off + n]
                base = {"q": 0, "k": 2048, "w": 4096}[which]
                return f8bf[p0:p1, base + off: base + off + n]

            ao_ev = {}
            att_free = wv1_ev
            wfree = [x2_done, x2_done]
            qkfree = [x2_done, x2_done]
            wq_done = {}
            rope_done = {}

            def emit_load(p):
                ev = None
                for ck in range(8):
                    ev = staged(
                        w_in["qkvT"][ck * 128:(ck + 1) * 128, 128 * p:128 * (p + 1)], 128,
                        lambda sl, ld, ck=ck, p=p: sc.v(
                            lambda e, sl=sl, w=BUF(p, "w", ck * 128, 128): e.tensor_scalar(
                                w, sl, 0.125, None, op0=ALU.mult),
                            wait=[ld, wfree[p % 2]]))
                for ck in range(8):
                    ev = staged(
                        w_in["qkvT"][ck * 128:(ck + 1) * 128, 1024 + 128 * p:1024 + 128 * (p + 1)], 128,
                        lambda sl, ld, ck=ck, p=p: sc.v(
                            lambda e, sl=sl, w=BUF(p, "w", 1024 + ck * 128, 128): e.tensor_copy(w, sl),
                            wait=[ld]))
                wq_done[p] = ev

            pe_done = {}

            def proj1(p, which, i):
                woff = 0 if which == "q" else 1024
                bk = take()
                last = None
                for ck in range(8):
                    last = sc.t(lambda e, bk=bk, w=BUF(p, "w", woff + ck * 128, 128), ck=ck,
                                i=i, sbase=sbase: e.matmul(
                        bk.ap[:, 0:512], w,
                        x_fm[:, ck * T + sbase + i * 512: ck * T + sbase + (i + 1) * 512],
                        start=(ck == 0), stop=(ck == 7)),
                        wait=([bk.free, wq_done[p], x2_done, qkfree[p % 2]] if ck == 0 else []))
                bk.ready = last
                pe = sc.v(lambda e, bk=bk, d=BUF(p, which, i * 512, 512): e.tensor_copy(
                    d, bk.ap[:, 0:512]), wait=[bk.ready])
                bk.free = pe
                rel(bk)
                pe_done[(p, which, i)] = pe

            def proj2(p, which, i):
                pe = pe_done[(p, which, i)]
                bk2 = take()
                rot = sc.t(lambda e, bk=bk2, srcb=BUF(p, which, i * 512, 512): e.matmul(
                    bk.ap[:, 0:512], p2m, srcb, start=True, stop=True),
                    wait=[bk2.free, pe, ld_consts])
                bk2.ready = rot
                sc.v(lambda e, srcb=BUF(p, which, i * 512, 512), i=i: e.tensor_tensor(
                    bsl(L1RT, 512), srcb, ctab[:, i * 512:(i + 1) * 512], op=ALU.mult), wait=[pe])
                sc.v(lambda e, bk=bk2, d=BUF(p, which, i * 512, 512), i=i: e.tensor_tensor(
                    d, bk.ap[:, 0:512], ctab[:, 2048 + i * 512: 2048 + (i + 1) * 512],
                    op=ALU.mult), wait=[bk2.ready])
                dn = sc.v(lambda e, d=BUF(p, which, i * 512, 512): e.tensor_tensor(
                    d, d, bsl(L1RT, 512), op=ALU.add))
                bk2.free = dn
                rel(bk2)
                rope_done[(p, which, i)] = dn

            def emit_proj(p, which, i):
                proj1(p, which, i)
                proj2(p, which, i)

            emit_load(0)
            for i in range(4):
                emit_proj(0, "q", i)
                emit_proj(0, "k", i)
            for p in range(NPAIR):
                if p + 1 < NPAIR:
                    emit_load(p + 1)
                k_done = rope_done[(p, "k", 3)]
                for i in range(4):
                    q_done = rope_done[(p, "q", i)]
                    poA = take()
                    poB = take()
                    avA = avB = None
                    for half in range(2):
                        if half == 1 and l1_layer.pending_norm is not None:
                            l1_layer.pending_norm()
                            l1_layer.pending_norm = None
                        exp_half = []
                        for kc8 in range(8):
                            kc = 8 * half + kc8
                            bkA = take()
                            bkB = take()
                            mA = sc.t(lambda e, bk=bkA, kq=BUF(p, "k", kc * 128, 128, 0, 64),
                                      qq=BUF(p, "q", i * 512, 512, 0, 64): e.matmul(
                                bk.ap[:, 0:512], kq, qq, start=True, stop=True),
                                wait=[bkA.free, bkB.free, k_done, q_done])
                            mB = sc.t(lambda e, bk=bkB, kq=BUF(p, "k", kc * 128, 128, 64, 128),
                                      qq=BUF(p, "q", i * 512, 512, 64, 128): e.matmul(
                                bk.ap[:, 0:512], kq, qq, start=True, stop=True,
                                tile_position=(64, 0), skip_group_check=True))
                            eA = sc.s(lambda e, bk=bkA, kc8=kc8: e.activation(
                                bsl(L1X + kc8 * 512, 512), bk.ap[:, 0:512], AF.Exp,
                                bias=cols[:, 50:51]),
                                wait=[mA, neg3_ev] + ([att_free] if att_free else []))
                            eB = sc.s(lambda e, bk=bkB, kc8=kc8: e.activation(
                                bsl(L1X + 4096 + kc8 * 512, 512), bk.ap[:, 0:512], AF.Exp,
                                bias=cols[:, 50:51]), wait=[mB])
                            bkA.free = eA
                            bkB.free = eB
                            rel(bkA)
                            rel(bkB)
                            exp_half.append(eB)
                            if kc8 == 2 and p + 1 < NPAIR:
                                proj1(p + 1, "q" if half == 0 else "k", i)
                            elif kc8 == 5 and p + 1 < NPAIR:
                                proj2(p + 1, "q" if half == 0 else "k", i)
                        for kc8 in range(8):
                            kc = 8 * half + kc8
                            avA = sc.t(lambda e, po=poA, kc=kc, kc8=kc8, p=p: e.matmul(
                                po.ap[0:65, 0:512],
                                bfa[:, L1V + kc * 1040 + 2 * p * 65: L1V + kc * 1040 + 2 * p * 65 + 65],
                                bsl(L1X + kc8 * 512, 512),
                                start=(kc == 0), stop=(kc == 15), skip_group_check=True),
                                wait=([poA.free, poB.free, v_ev, exp_half[kc8]] if kc == 0
                                      else [exp_half[kc8]]))
                            avB = sc.t(lambda e, po=poB, kc=kc, kc8=kc8, p=p: e.matmul(
                                po.ap[0:65, 0:512],
                                bfa[:, L1V + kc * 1040 + (2 * p + 1) * 65: L1V + kc * 1040 + (2 * p + 1) * 65 + 65],
                                bsl(L1X + 4096 + kc8 * 512, 512),
                                start=(kc == 0), stop=(kc == 15), skip_group_check=True))
                        att_free = avB
                    def norm_closure(p=p, i=i, poA=poA, poB=poB, avA=avA, avB=avB):
                        f32a_ = env["f32a"]
                        r0 = sc.v(lambda e, po=poA: e.reciprocal(
                            f32a_[0:1, 0:512], po.ap[64:65, 0:512]), wait=[avA, avB, fz2_ev])
                        rA = sc.v(lambda e: e.tensor_copy(bfa[0:1, 48384:48896], f32a_[0:1, 0:512]),
                                  wait=[fz_ev])
                        r1 = sc.v(lambda e, po=poB: e.reciprocal(
                            f32a_[0:1, 512:1024], po.ap[64:65, 0:512]))
                        rB = sc.v(lambda e: e.tensor_copy(bfa[0:1, 48896:49408], f32a_[0:1, 512:1024]))
                        uA = sc.s(lambda e, po=poA: e.copy(bsl(L1U, 512, 0, 64), po.ap[0:64, 0:512]),
                                  wait=[rA, l1_layer.u_free])
                        poA.free = uA
                        rel(poA)
                        bcA = take()
                        mmA = sc.t(lambda e, bk=bcA: e.matmul(
                            bk.ap[:, 0:512], bc2, bfa[0:2, 48384:48896], start=True, stop=True),
                            wait=[bcA.free, rA])
                        bcA.ready = mmA
                        evA = sc.v(lambda e, bk=bcA, p=p, i=i: e.tensor_tensor(
                            bfa[0:64, L1AO + p * 2048 + i * 512: L1AO + p * 2048 + (i + 1) * 512],
                            bsl(L1U, 512, 0, 64), bk.ap[0:64, 0:512], op=ALU.mult),
                            wait=[bcA.ready, uA] + ([l1_layer.ao_free] if l1_layer.ao_free else []))
                        uB = sc.s(lambda e, po=poB: e.copy(bsl(L1U, 512, 64, 128), po.ap[0:64, 0:512]),
                                  wait=[rB, evA])
                        poB.free = uB
                        rel(poB)
                        bcB = take()
                        mmB = sc.t(lambda e, bk=bcB: e.matmul(
                            bk.ap[:, 0:512], bc2, bfa[0:2, 48896:49408], start=True, stop=True),
                            wait=[bcB.free, rB])
                        bcB.ready = mmB
                        evB = sc.v(lambda e, bk=bcB, p=p, i=i: e.tensor_tensor(
                            bfa[64:128, L1AO + p * 2048 + i * 512: L1AO + p * 2048 + (i + 1) * 512],
                            bsl(L1U, 512, 64, 128), bk.ap[0:64, 0:512], op=ALU.mult),
                            wait=[bcB.ready, uB])
                        bcA.free = evA
                        bcB.free = evB
                        rel(bcA)
                        rel(bcB)
                        l1_layer.u_free = evB
                        ao_ev[(p, i)] = evB
                    l1_layer.pending_norm = norm_closure
                if p == NPAIR - 1 and l1_layer.pending_norm is not None:
                    l1_layer.pending_norm()
                    l1_layer.pending_norm = None
                qkfree[p % 2] = (("sT", sc.cnt["sT"]))
                wfree[p % 2] = qkfree[p % 2]
            l1_layer.v_free = ao_ev[(NPAIR - 1, 3)]
            ot_ev = None
            for ck in range(8):
                ot_ev = staged(
                    w_in["oT"][ck * 128:(ck + 1) * 128, :], 1024,
                    lambda sl, ld, ck=ck: sc.v(
                        lambda e, sl=sl: e.tensor_copy(bsl(L1X + ck * 1024, 1024), sl),
                        wait=[ld, att_free]))
            for tt in range(4):
                s_ev3 = [None] * NC_F
                for dc in range(NC_F):
                    bk = take()
                    last = None
                    for ck in range(NC_F):
                        last = sc.t(lambda e, bk=bk, ck=ck, dc=dc, tt=tt: e.matmul(
                            bk.ap[:, 0:512], bsl(L1X + ck * 1024 + dc * 128, 128),
                            bsl(L1AO + ck * 2048 + tt * 512, 512),
                            start=(ck == 0), stop=(ck == 7)),
                            wait=([bk.free, ot_ev] + [ao_ev[(pp, tt)] for pp in range(NPAIR)]
                                  if ck == 0 else []))
                    bk.ready = last
                    s_ev3[dc] = sc.v(lambda e, bk=bk, dc=dc, tt=tt, sbase=sbase: e.tensor_tensor(
                        bsl(L1S + dc * 512, 512),
                        x_fm[:, dc * T + sbase + tt * 512: dc * T + sbase + (tt + 1) * 512],
                        bk.ap[:, 0:512], op=ALU.add), wait=[bk.ready, l1_layer.s_free])
                    bk.free = s_ev3[dc]
                    rel(bk)
                x3_ev = ln_norm(
                    lambda fc: bsl(L1S + fc * 512, 512),
                    lambda fc, tt=tt, sbase=sbase: x_fm[:, fc * T + sbase + tt * 512: fc * T + sbase + (tt + 1) * 512],
                    s_ev3)
                x3_ev_all.append(x3_ev[-1])
                x_ev_o[(s_, tt)] = x3_ev[-1]
                l1_layer.s_free = x3_ev[-1]
            l1_layer.x_free = x3_ev_all[-1]
            l1_layer.ao_free = x3_ev_all[-1]
            l1_layer.v_free = x3_ev_all[-1]
        l1_layer.x3_done = x3_ev_all[-1]
        l1_layer.x_ev_o = x_ev_o
        return x3_ev_all[-1]
    l1_layer.x_free = None
    l1_layer.v_free = None
    l1_layer.u_free = None
    l1_layer.ao_free = None

    F1W, F1A1, F1A2 = 0, 8192, 16384
    F1H1P = (24576, 28672)
    F1H2P = (32768, 32768)
    F1SP = (36864, 40960)

    def f1_phase(x3_done, write_out):
        f32a = env["f32a"]
        wevs = {}
        for name, off in (("f1w", F1W), ("f1a1", F1A1), ("f1a2", F1A2)):
            for ck in range(8):
                wevs[name] = staged(
                    w_in[name][ck * 128:(ck + 1) * 128, :], 1024,
                    lambda sl, ld, ck=ck, off=off: sc.v(
                        lambda e, sl=sl: e.tensor_copy(bsl(off + ck * 1024, 1024), sl),
                        wait=[ld, x3_done]))
        slot_ev = [None, None]
        h_free = {0: [None, None, None], 1: [None, None, None]}
        h2_free = [None]
        cfree = [None, None]
        for t8 in range(NTILE):
            par = t8 % 2
            F1H1, F1H2, F1S = F1H1P[par], F1H2P[par], F1SP[par]
            hfp = h_free[par]
            sbase = (t8 // 4) * S
            tt = t8 % 4

            def x3sl(fc, t8=t8):
                return x_fm[:, fc * T + t8 * 512: fc * T + (t8 + 1) * 512]

            x3_ok = l1_layer.x_ev_o[(t8 // 4, tt)]
            prev_ev = [x3_ok] * NC_F
            for idx, (woff, dsto, wname) in enumerate(
                    ((F1W, F1H1, "f1w"), (F1A1, F1H2, "f1a1"))):
                nxt = [None] * NC_F
                for dc in range(NC_F):
                    bk = take()
                    last = None
                    for ck in range(NC_F):
                        src = (x3sl(ck) if idx == 0 else bsl(F1H2 + 0, 0) if False else
                               (x3sl(ck) if idx == 0 else bsl(F1H1 + ck * 512, 512)))
                        last = sc.t(lambda e, bk=bk, ck=ck, dc=dc, woff=woff, src=src: e.matmul(
                            bk.ap[:, 0:512], bsl(woff + ck * 1024 + dc * 128, 128), src,
                            start=(ck == 0), stop=(ck == 7)),
                            wait=([bk.free, wevs[wname],
                                   (hfp[idx] if idx == 0 else h2_free[0])] + prev_ev
                                  if ck == 0 else []))
                    bk.ready = last
                    nxt[dc] = sc.v(lambda e, bk=bk, dc=dc, dsto=dsto: e.tensor_copy(
                        bsl(dsto + dc * 512, 512), bk.ap[:, 0:512]), wait=[bk.ready])
                    bk.free = nxt[dc]
                    rel(bk)
                prev_ev = nxt
            s_ev4 = [None] * NC_F
            t_ev = [None] * NC_F
            for dc in range(NC_F):
                bk = take()
                last = None
                for ck in range(NC_F):
                    last = sc.t(lambda e, bk=bk, ck=ck, dc=dc, F1H2=F1H2: e.matmul(
                        bk.ap[:, 0:512], bsl(F1A2 + ck * 1024 + dc * 128, 128),
                        bsl(F1H2 + ck * 512, 512), start=(ck == 0), stop=(ck == 7)),
                        wait=([bk.free, wevs["f1a2"], hfp[2]] + prev_ev if ck == 0 else []))
                bk.ready = last
                t_ev[dc] = sc.s(lambda e, bk=bk, dc=dc, F1H1=F1H1: e.activation(
                    bsl(F1H1 + dc * 512, 512), bk.ap[:, 0:512], AF.Tanh,
                    bias=cols[:, 16 + dc:17 + dc]), wait=[bk.ready, wev["f1b"]])
                s_ev4[dc] = sc.v(lambda e, dc=dc, xs=x3sl(dc), F1S=F1S, F1H1=F1H1: e.tensor_tensor(
                    bsl(F1S + dc * 512, 512), xs, bsl(F1H1 + dc * 512, 512), op=ALU.add),
                    wait=[t_ev[dc]])
                bk.free = s_ev4[dc]
                rel(bk)
            bank_s = ln_sums(lambda fc: bsl(F1S + fc * 512, 512), lambda fc: s_ev4[fc])
            r_ev, mur_ev = ln_stats(bank_s)
            rel(bank_s)
            if not write_out:
                x4_ev = ln_norm(lambda fc, F1S=F1S: bsl(F1S + fc * 512, 512),
                                lambda fc, t8=t8: x_fm[:, fc * T + t8 * 512: fc * T + (t8 + 1) * 512],
                                s_ev4)
                h_free[par] = [x4_ev[-1]] * 3
                h2_free[0] = x4_ev[-1]
                continue
            for c4 in range(4):
                slot = (t8 * 4 + c4) % 2
                sl = f32a[:, slot * 1024:(slot + 1) * 1024]
                tp = None
                for fc in range(NC_F):
                    tp = sc.t(lambda e, fc=fc, c4=c4, F1S=F1S: e.transpose(
                        tbank.ap[:, fc * 128:(fc + 1) * 128],
                        bsl(F1S + fc * 512 + c4 * 128, 128), idt_bf),
                        wait=([tbank.free, sq_cell[0], mur_ev] if fc == 0 else []))
                tbank.ready = tp
                bkc = take()
                cmm = sc.t(lambda e, bk=bkc, c4=c4: e.matmul(
                    bk.ap[:, 0:1], rows[0:2, SEG_A + c4 * 128: SEG_A + (c4 + 1) * 128],
                    bc2f[:, 0:1], start=True, stop=False), wait=[bkc.free, r_ev])
                cmm = sc.t(lambda e, bk=bkc, c4=c4: e.matmul(
                    bk.ap[:, 1:2], rows[0:2, SEG_C + c4 * 128: SEG_C + (c4 + 1) * 128],
                    bc2f[:, 0:1], start=False, stop=True), wait=[mur_ev])
                bkc.ready = cmm
                rows_guard[0] = cmm
                cev = sc.v(lambda e, bk=bkc, c4=c4: e.tensor_copy(
                    cols[:, 48 + 2 * (c4 % 2):50 + 2 * (c4 % 2)], bk.ap[:, 0:2]),
                    wait=[bkc.ready, cfree[c4 % 2]])
                bkc.free = cev
                rel(bkc)
                oev = None
                for fc in range(NC_F):
                    oev = sc.v(lambda e, fc=fc, sl=sl, c4=c4: e.tensor_scalar(
                        sl[:, fc * 128:(fc + 1) * 128], tbank.ap[:, fc * 128:(fc + 1) * 128],
                        cols[:, 48 + 2 * (c4 % 2):49 + 2 * (c4 % 2)],
                        cols[:, 49 + 2 * (c4 % 2):50 + 2 * (c4 % 2)], op0=ALU.mult, op1=ALU.add),
                        wait=([tbank.ready, cev, slot_ev[slot]] if fc == 0 else []))
                cfree[c4 % 2] = oev
                tbank.free = oev
                st = sc.dma("gpsimd", lambda e, sl=sl, t8=t8, c4=c4: e.dma_start(
                    out=out[t8 * 512 + c4 * 128: t8 * 512 + (c4 + 1) * 128, :], in_=sl),
                    "sD2" if slot == 0 else "sD3", wait=[oev])
                slot_ev[slot] = st
            tend = (("sT", sc.cnt["sT"]))
            h_free[par] = [s_ev4[-1], tend, tend]
            h2_free[0] = tend
        return slot_ev

    ev_x_ready = (("sS", sc.cnt["sS"]))
    emit_generic_out = True
    if upto != "x0":
        load_weights_l0()
        x2_done = l0_layer()
        ev_x_ready = (("sV", sc.cnt["sV"]))
        if upto not in ("x1", "x2"):
            l1_layer(x2_done)
            ev_x_ready = (("sV", sc.cnt["sV"]))
            if upto == "x3":
                pass
            else:
                f1_phase(l1_layer.x3_done, write_out=True)
                emit_generic_out = False

    out_stage = env["f32a"][:, 0:2048]
    slot_ev = [tap.last, tap.last]
    if not emit_generic_out:
        n_tc_gen = 0
    else:
        n_tc_gen = n_tc
    for c in range(n_tc_gen):
        slot = c % 2
        sl = out_stage[:, slot * 1024:(slot + 1) * 1024]
        tb = tbank
        tp_ev = None
        for fc in range(NC_F):
            tp_ev = sc.t(lambda e, fc=fc, c=c, tb=tb:
                         e.transpose(tb.ap[:, fc * 128:(fc + 1) * 128],
                                     x_fm[:, fc * T + c * 128: fc * T + (c + 1) * 128], idt_bf),
                         wait=([ev_x_ready, ld_consts, tb.free, slot_ev[slot]] if fc == 0 else []))
        tb.ready = tp_ev
        ev = sc.v(lambda e, sl=sl, tb=tb: e.tensor_copy(sl, tb.ap[:, 0:1024]), wait=[tb.ready])
        tb.free = ev
        st = sc.dma("gpsimd", lambda e, sl=sl, c=c: e.dma_start(out=out[c * 128:(c + 1) * 128, :], in_=sl),
                    "sD2" if slot == 0 else "sD3", wait=[ev])
        slot_ev[slot] = st
    sc.emit("gpsimd", None, wait=[("sD2", sc.cnt["sD2"]), ("sD3", sc.cnt["sD3"])])


def host_constants():
    c = np.zeros((128, CONST_W), dtype=np.float32)
    d = np.float32(HD)
    inv = (1.0 / (np.float32(10000.0) ** (np.arange(0, HD, 2, dtype=np.float32) / d))).astype(np.float32)
    pos = np.arange(S, dtype=np.float32)
    ang = (pos[None, :] * np.concatenate([inv, inv])[:, None]).astype(np.float32)
    cos = np.cos(ang).astype(np.float32)
    sin = np.sin(ang).astype(np.float32)
    c[:, 0:S] = np.concatenate([cos, cos], axis=0)
    c[:, S:2 * S] = np.concatenate([sin, sin], axis=0)
    c[:, 4096:4224] = np.eye(128, dtype=np.float32)
    triu = np.triu(np.ones((128, 128), dtype=np.float32))
    c[:, 4224:4352] = triu
    c[:, 4352:4480] = triu
    P = np.zeros((64, 64), dtype=np.float32)
    for i in range(32):
        P[i + 32, i] = -1.0
        P[i, i + 32] = 1.0
    c[0:64, 4480:4544] = P
    c[64:128, 4544:4608] = P
    c[:, 4608] = 1.0
    c[0, 4616:4744] = 1.0
    c[32, 4616:4744] = 1.0
    return c


KERNEL_UPTO = "out"


def kernel(L0_attn_x, L0_attn_Wq, L0_attn_Wk, L0_attn_Wv, L0_attn_Wo, L0_attn_Wg1,
           L0_attn_Wg2, L0_attn_ln_w, L0_attn_ln_b, L0_ffn_weight, L0_ffn_bias,
           L1_attn_qkv_w, L1_attn_o_w, L1_ffn_weight, L1_ffn_aux_weight1,
           L1_ffn_aux_weight2, L1_ffn_bias):
    xf = np.ascontiguousarray(np.asarray(L0_attn_x, dtype=np.float32))
    consts = host_constants()

    def v2d(v):
        return np.ascontiguousarray(np.asarray(v, np.float32).reshape(NC_F, 128).T)


    shared = {
        "consts": consts,
        "wq": np.ascontiguousarray(L0_attn_Wq, np.float32),
        "wk": np.ascontiguousarray(L0_attn_Wk, np.float32),
        "wv": np.ascontiguousarray(L0_attn_Wv, np.float32),
        "wo": np.ascontiguousarray(L0_attn_Wo, np.float32),
        "wg1": np.ascontiguousarray(L0_attn_Wg1, np.float32),
        "wg2": np.ascontiguousarray(L0_attn_Wg2, np.float32),
        "lnw2d": v2d(L0_attn_ln_w), "lnb2d": v2d(L0_attn_ln_b),
        "f0wT": np.ascontiguousarray(np.asarray(L0_ffn_weight, np.float32).T),
        "f0b2d": v2d(L0_ffn_bias),
        "qkvT": np.ascontiguousarray(np.asarray(L1_attn_qkv_w, np.float32).T),
        "oT": np.ascontiguousarray(np.asarray(L1_attn_o_w, np.float32).T),
        "f1w": np.ascontiguousarray(L1_ffn_weight, np.float32),
        "f1a1": np.ascontiguousarray(L1_ffn_aux_weight1, np.float32),
        "f1a2": np.ascontiguousarray(L1_ffn_aux_weight2, np.float32),
        "f1b2d": v2d(L1_ffn_bias),
    }
    nc = build_nc(upto=KERNEL_UPTO)
    in_maps = []
    for c in range(NCORES):
        m = dict(shared)
        m["x_in"] = np.ascontiguousarray(
            xf[c * SEQ_PER_CORE:(c + 1) * SEQ_PER_CORE].reshape(T, D))
        in_maps.append(m)
    res = run_bass_kernel_spmd(nc, in_maps, core_ids=list(range(NCORES)))
    outs = [res.results[c]["out"].reshape(SEQ_PER_CORE, S, D) for c in range(NCORES)]
    return np.concatenate(outs, axis=0).astype(np.float32)


if __name__ == "__main__":
    rng = np.random.default_rng(0)
    args = {}
    import inspect
    sig = inspect.signature(kernel)
    for name in sig.parameters:
        shp = {"L0_attn_x": (B, S, D), "L0_attn_Wq": (D, D), "L0_attn_Wk": (D, D),
               "L0_attn_Wv": (D, D), "L0_attn_Wo": (D, D), "L0_attn_Wg1": (D, HD),
               "L0_attn_Wg2": (HD, D), "L0_attn_ln_w": (D,), "L0_attn_ln_b": (D,),
               "L0_ffn_weight": (D, D), "L0_ffn_bias": (D,), "L1_attn_qkv_w": (3 * D, D),
               "L1_attn_o_w": (D, D), "L1_ffn_weight": (D, D), "L1_ffn_aux_weight1": (D, D),
               "L1_ffn_aux_weight2": (D, D), "L1_ffn_bias": (D,)}[name]
        args[name] = rng.standard_normal(shp).astype(np.float32) * 0.1
    o = kernel(**args)
    print("out", o.shape, o.dtype, np.abs(o - args["L0_attn_x"]).max())
